# version 02
# baseline (speedup 1.0000x reference)
import numpy as np
from contextlib import ExitStack
import concourse.bass as bass
import concourse.mybir as mybir
from concourse.bass_utils import run_bass_kernel_spmd

F32 = mybir.dt.float32
BF16 = mybir.dt.bfloat16
AF = mybir.ActivationFunctionType
ALU = mybir.AluOpType
AX = mybir.AxisListType

T, DM, H, DH, NS = 4096, 1024, 16, 64, 8
C, NCH = 64, 64
GRP, GT, NG = 2, 128, 32
TILE, NT = 512, 8
FLOOR = 5.0
DHID = 4096
EPS = 1e-5

CONSTS = [EPS, 1.0 / DM, -0.5, 0.5, 2.0, -2.0, 127.0, 128.5, -128.0,
          1e12, -1e12, 1.0 / 127.0]

SKIP_SAME_D = True
ENGINES = ("sync", "gpsimd", "vector", "scalar", "tensor")


def _ap(t, off, dims):
    return bass.AP(t, off, dims)


class Sched:

    def __init__(self, nc, sems, dsems):
        self.nc = nc
        self.sems = sems
        self.dsems = dsems
        self.count = {e: 0 for e in sems}
        self.dcount = {e: 0 for e in sems}
        self.steps = {e: [] for e in sems}

    def step(self, eng, fn, deps=(), dma=0):
        deps = tuple(t for t in deps if t is not None)
        if dma:
            self.dcount[eng] += 16 * dma
            tok = (eng, "d", self.dcount[eng])
        else:
            self.count[eng] += 1
            tok = (eng, "c", self.count[eng])
        self.steps[eng].append((fn, deps, dma))
        return tok

    def emit(self, block):
        for eng_name in ENGINES:
            steps = self.steps[eng_name]
            if not steps:
                continue

            def body(e, steps=steps, eng_name=eng_name):
                for fn, deps, dma in steps:
                    for (deng, kind, dval) in deps:
                        if deng == eng_name and kind == "c":
                            continue
                        if deng == eng_name and kind == "d" and SKIP_SAME_D:
                            continue
                        sem = (self.dsems if kind == "d" else self.sems)[deng]
                        e.wait_ge(sem, dval)
                    last = fn(e, (self.dsems if dma else self.sems)[eng_name])
                    if not dma:
                        assert last is not None, "step fn must return instr"
                        last.then_inc(self.sems[eng_name], 1)
            getattr(block, eng_name)(body)


def lnwb_col_ap(mk, j):
    return mk.lnwb_col[:, j:j + 1]


class MK:
    def __init__(self, debug=(), p1_tiles=NT, phases=(1, 2, 3, 4)):
        self.p1_tiles = p1_tiles
        self.phases = set(phases)
        self.debug = set(debug)
        nc = self.nc = bass.Bass("TRN2", target_bir_lowering=False)
        d = self.d = {}

        def din(name, shape, dt=F32):
            d[name] = nc.dram_tensor(name, shape, dt, kind="ExternalInput")

        def dsc(name, shape, dt):
            kind = "ExternalOutput" if name in self.debug else "Internal"
            d[name] = nc.dram_tensor(name, shape, dt, kind=kind)

        din("x", [T, DM])
        for w in ("wqT", "wkT", "wvT", "waT", "waRT", "woT"):
            din(w, [DM, DM])
        din("wbgT", [DM, 48])
        din("w1T", [DM, DHID])
        din("w2T", [DHID, DM])
        din("ba", [DM]); din("baR", [DM]); din("bbg", [48])
        din("lnw", [DM]); din("lnb", [DM])
        din("convw", [DHID, 3]); din("convb", [DHID])
        din("ident", [128, 128], BF16)
        din("identf", [128, 128], F32)
        din("mask_nsr", [128, 128], BF16)
        din("mask_pq", [128, 320], BF16)
        din("hmap_sq", [128, NS * 16], BF16)
        din("hmap_bc", [16, NS * 128], BF16)
        din("hmap_bc2", [48, NS * 128], BF16)

        dsc("knS", [128, NS * T], BF16)
        dsc("knbS", [128, NS * T], BF16)
        dsc("kngS", [128, NS * T], BF16)
        dsc("qS", [128, NS * T], BF16)
        dsc("vS", [128, NS * T], BF16)
        dsc("DS", [128, NS * T], F32)
        dsc("DRS", [128, NS * T], F32)
        dsc("OS", [128, NS * T], BF16)
        dsc("x1S", [128, NS * T], F32)
        dsc("xqS", [128, NS * T], BF16)
        dsc("bgS", [128, 2 * T // 4], F32)
        d["out"] = nc.dram_tensor("out", [T, DM], F32, kind="ExternalOutput")

    def _sched(self, ctx, tag):
        E = ctx.enter_context
        sems = {e: E(self.nc.semaphore(f"{tag}_c_{e}")) for e in ENGINES}
        dsems = {e: E(self.nc.semaphore(f"{tag}_d_{e}")) for e in ENGINES}
        return Sched(self.nc, sems, dsems)

    def build(self):
        nc, d = self.nc, self.d
        with ExitStack() as octx:
            Eo = octx.enter_context
            ident_sb = Eo(nc.sbuf_tensor("ident_sb", [128, 128], BF16))
            lnstats = Eo(nc.sbuf_tensor("lnstats", [128, NT * 4 * 8], F32))
            cpool = Eo(nc.sbuf_tensor("cpool", [128, len(CONSTS)], F32))
            self.ident, self.lnstats = ident_sb, lnstats
            for ci, cv in enumerate(CONSTS):
                nc.const_aps.register(F32, cv, cpool[:, ci:ci + 1])
            with ExitStack() as mctx:
                Em = mctx.enter_context
                identf_sb = Em(nc.sbuf_tensor("identf_sb", [128, 128], F32))
                bgrow = Em(nc.sbuf_tensor("bgrow", [128, 32 * 48], F32))
                nbgrow = Em(nc.sbuf_tensor("nbgrow", [128, 32 * 48], F32))
                ba_col = Em(nc.sbuf_tensor("ba_col", [128, 16], F32))
                lnwb_col = Em(nc.sbuf_tensor("lnwb_col", [128, 16], F32))
                bbg_col = Em(nc.sbuf_tensor("bbg_col", [48, 1], F32))
                hmsq_sb = Em(nc.sbuf_tensor("hmsq_sb", [128, NS * 16], BF16))
                hmbc_sb = Em(nc.sbuf_tensor("hmbc_sb", [16, NS * 128], BF16))
                hmbc2_sb = Em(nc.sbuf_tensor("hmbc2_sb", [48, NS * 128],
                                             BF16))
                mnsr_sb = Em(nc.sbuf_tensor("mnsr_sb", [128, 128], BF16))
                mpq_sb = Em(nc.sbuf_tensor("mpq_sb", [128, 320], BF16))
                bgrow_c = Em(nc.sbuf_tensor("bgrow_c", [128, NCH * 64], F32))
                c0 = Em(nc.semaphore("c0"))
                self.identf = identf_sb
                self.bgrow, self.nbgrow = bgrow, nbgrow
                self.ba_col, self.lnwb_col, self.bbg_col = \
                    ba_col, lnwb_col, bbg_col
                self.hmsq, self.hmbc, self.mnsr, self.mpq = \
                    hmsq_sb, hmbc_sb, mnsr_sb, mpq_sb
                self.hmbc2 = hmbc2_sb
                self.bgrow_c = bgrow_c

                with nc.Block() as blk:
                    @blk.gpsimd
                    def _(e):
                        for ci, cv in enumerate(CONSTS):
                            e.memset(cpool[:, ci:ci + 1], float(cv))
                        e.sem_inc(c0, 1)

                    @blk.sync
                    def _(e):
                        loads = [
                            (ident_sb[:], d["ident"].ap()),
                            (identf_sb[:], d["identf"].ap()),
                            (hmsq_sb[:], d["hmap_sq"].ap()),
                            (hmbc_sb[:], d["hmap_bc"].ap()),
                            (hmbc2_sb[:], d["hmap_bc2"].ap()),
                            (mnsr_sb[:], d["mask_nsr"].ap()),
                            (mpq_sb[:], d["mask_pq"].ap()),
                            (ba_col[:, 0:8],
                             _ap(d["ba"], 0, [[1, 128], [128, 8]])),
                            (ba_col[:, 8:16],
                             _ap(d["baR"], 0, [[1, 128], [128, 8]])),
                            (lnwb_col[:, 0:8],
                             _ap(d["lnw"], 0, [[1, 128], [128, 8]])),
                            (lnwb_col[:, 8:16],
                             _ap(d["lnb"], 0, [[1, 128], [128, 8]])),
                            (bbg_col[:], _ap(d["bbg"], 0, [[1, 48], [1, 1]])),
                        ]
                        for o, i in loads:
                            e.dma_start(out=o, in_=i,
                                        allow_slow_non_contiguous=True
                                        ).then_inc(c0, 16)
                        e.wait_ge(c0, 16 * len(loads) + 1)
                nc.all_engine_barrier()
                if 1 in self.phases:
                    with nc.named_scope("phase1"):
                        self.phase1()
                nc.all_engine_barrier()
                if 2 in self.phases:
                    with nc.named_scope("phase2"):
                        self.phase2()
                nc.all_engine_barrier()
            with ExitStack() as pctx:
                Ep = pctx.enter_context
                self.hsrow = Ep(nc.sbuf_tensor("hsrow", [1, T], BF16))
                self.wq_s = Ep(nc.sbuf_tensor("wq_s", [128, 16], F32))
                if 3 in self.phases:
                    with nc.named_scope("phase3"):
                        self.phase3()
                nc.all_engine_barrier()
                if 4 in self.phases:
                    with nc.named_scope("phase4"):
                        self.phase4()
        return nc

    def phase1(self):
        nc, d = self.nc, self.d
        with ExitStack() as ctx:
            E = ctx.enter_context
            xrow = E(nc.sbuf_tensor("p1_xrow", [128, 4 * DM], F32))
            scr = E(nc.sbuf_tensor("p1_scr", [128, DM], F32))
            xnp = E(nc.sbuf_tensor("p1_xnp", [128, 2 * DM], BF16))
            xnT = E(nc.sbuf_tensor("p1_xnT", [128, NS * TILE], BF16))
            wq_sb = E(nc.sbuf_tensor("p1_wq", [128, NS * DM], BF16))
            wk_sb = E(nc.sbuf_tensor("p1_wk", [128, NS * DM], BF16))
            wv_sb = E(nc.sbuf_tensor("p1_wv", [128, NS * DM], BF16))
            wa_sb = E(nc.sbuf_tensor("p1_wa", [128, NS * DM], BF16))
            waR_sb = E(nc.sbuf_tensor("p1_waR", [128, NS * DM], BF16))
            wbg_sb = E(nc.sbuf_tensor("p1_wbg", [128, NS * 48], BF16))
            ktile = E(nc.sbuf_tensor("p1_kt", [128, NS * TILE], BF16))
            qtile = E(nc.sbuf_tensor("p1_qt", [128, NS * TILE], BF16))
            vtile = E(nc.sbuf_tensor("p1_vt", [128, NS * TILE], BF16))
            ksqt = E(nc.sbuf_tensor("p1_ksq", [128, TILE], BF16))
            za_sb = E(nc.sbuf_tensor("p1_za", [128, TILE], F32))

            dtile = E(nc.sbuf_tensor("p1_dt", [128, NS * TILE], F32))

            e1 = E(nc.sbuf_tensor("p1_e1", [128, TILE], F32))
            e2 = E(nc.sbuf_tensor("p1_e2", [128, TILE], F32))
            rin = E(nc.sbuf_tensor("p1_rin", [16, TILE], BF16))
            bgt = E(nc.sbuf_tensor("p1_bg", [48, TILE], F32))
            bgb = E(nc.sbuf_tensor("p1_bgb", [48, TILE], BF16))
            ps_t = E(nc.psum_tensor("p1_pt", [128, 1024], BF16))
            ps_a = E(nc.psum_tensor("p1_pa", [128, 512], F32))
            ps_b = E(nc.psum_tensor("p1_pb", [128, 512], F32))
            ps_c = E(nc.psum_tensor("p1_pc", [128, 512], F32))
            ps_d = E(nc.psum_tensor("p1_pd", [128, 512], F32))
            ps_e = E(nc.psum_tensor("p1_pe", [128, 512], F32))
            sch = self._sched(ctx, "p1")
            st = sch.step

            wload = {}
            for wname, wsb, ncol in [("wqT", wq_sb, DM), ("wkT", wk_sb, DM),
                                     ("wvT", wv_sb, DM), ("waT", wa_sb, DM),
                                     ("waRT", waR_sb, DM), ("wbgT", wbg_sb, 48)]:
                def fn(e, sem, wname=wname, wsb=wsb, ncol=ncol):
                    for s in range(NS):
                        e.dma_start(
                            out=wsb[:, s * ncol:(s + 1) * ncol],
                            in_=_ap(d[wname], s * 128 * ncol,
                                    [[ncol, 128], [1, ncol]])).then_inc(sem, 16)
                wload[wname] = st("gpsimd", fn, dma=NS)

            xload = {}
            xnT_ready_prev = None
            spill = {}
            tile_done = {}
            prev = {"tbg_evac": None, "arin": None, "d_last": None,
                    "bg_evac1": None}
            for it in range(self.p1_tiles):
                xsl = lambda stt: xrow[:, stt * DM:(stt + 1) * DM]

                def cs(stt, j, it=it):
                    b = it * 32 + stt * 8 + j
                    return self.lnstats[:, b:b + 1]

                deps = [xnT_ready_prev] if it >= 1 else []

                def fn(e, sem, it=it):
                    for stt in range(4):
                        e.dma_start(
                            out=xrow[:, stt * DM:(stt + 1) * DM],
                            in_=_ap(d["x"], (it * TILE + stt * 128) * DM,
                                    [[DM, 128], [1, DM]]),
                        ).then_inc(sem, 16)
                xload[it] = st("sync", fn, deps, dma=4)

                def fn(e, sem, xsl=xsl, cs=cs):
                    for stt in range(4):
                        ins = e.reduce_sum(cs(stt, 0), xsl(stt), axis=AX.X)
                    return ins
                v1a = st("vector", fn, [xload[it]])

                def fn(e, sem, cs=cs):
                    for stt in range(4):
                        ins = e.activation(cs(stt, 1), cs(stt, 0), AF.Identity,
                                           scale=1.0 / DM)
                    return ins
                a1a = st("scalar", fn, [v1a])

                def fn(e, sem, xsl=xsl, cs=cs):
                    for stt in range(4):
                        xa = xsl(stt)
                        e.tensor_single_scalar(xa, xa, cs(stt, 1), ALU.subtract)
                        ins = e.scalar_tensor_tensor(scr[:], xa, 1.0, xa,
                                                     ALU.mult, ALU.mult,
                                                     accum_out=cs(stt, 2))
                    return ins
                v1 = st("vector", fn, [a1a])

                def fn(e, sem, cs=cs):
                    for stt in range(4):
                        e.activation(cs(stt, 3), cs(stt, 2), AF.Ln, bias=EPS,
                                     scale=1.0 / DM)
                        ins = e.activation(cs(stt, 3), cs(stt, 3), AF.Exp,
                                           scale=-0.5)
                    return ins
                a1 = st("scalar", fn, [v1])

                def fn(e, sem, xsl=xsl, cs=cs):
                    for stt in range(4):
                        xa = xsl(stt)
                        e.tensor_single_scalar(xa, xa, cs(stt, 3), ALU.mult)
                        ins = e.reduce_sum(cs(stt, 4), xa, axis=AX.X)
                    return ins
                v2a = st("vector", fn, [a1])

                def fn(e, sem, cs=cs):
                    for stt in range(4):
                        ins = e.activation(cs(stt, 7), cs(stt, 4), AF.Identity,
                                           scale=1.0 / DM)
                    return ins
                a2a = st("scalar", fn, [v2a])

                def fn(e, sem, xsl=xsl, cs=cs):
                    for stt in range(4):
                        xa = xsl(stt)
                        e.tensor_single_scalar(xa, xa, cs(stt, 7), ALU.subtract)
                        ins = e.scalar_tensor_tensor(scr[:], xa, 1.0, xa,
                                                     ALU.mult, ALU.mult,
                                                     accum_out=cs(stt, 5))
                    return ins
                v2 = st("vector", fn, [a2a])

                def fn(e, sem, cs=cs):
                    for stt in range(4):
                        e.activation(cs(stt, 6), cs(stt, 5), AF.Ln, bias=EPS,
                                     scale=1.0 / DM)
                        ins = e.activation(cs(stt, 6), cs(stt, 6), AF.Exp,
                                           scale=-0.5)
                    return ins
                a2 = st("scalar", fn, [v2])

                tr_done = []
                for stt in range(4):
                    vx_deps = [a2]
                    if stt >= 2:
                        vx_deps.append(tr_done[2 * (stt - 2) + 1])

                    def fn(e, sem, stt=stt, xsl=xsl, cs=cs):
                        return e.tensor_single_scalar(
                            xnp[:, (stt % 2) * DM:(stt % 2 + 1) * DM],
                            xsl(stt), cs(stt, 6), ALU.mult)
                    vx = st("vector", fn, vx_deps)

                    t_deps = [vx]
                    if stt == 0 and prev["tbg_evac"] is not None:
                        t_deps.append(prev["tbg_evac"])
                    if stt >= 1:
                        t_deps.append(tr_done[2 * (stt - 1) + 1])

                    def fn(e, sem, stt=stt):
                        for s in range(4):
                            ins = e.transpose(ps_t[:, s * 128:(s + 1) * 128],
                                              xnp[:, (stt % 2) * DM + s * 128:
                                                  (stt % 2) * DM + (s + 1) * 128],
                                              self.ident[:])
                        return ins
                    t1 = st("tensor", fn, t_deps)

                    def fn(e, sem, stt=stt):
                        for s in range(4):
                            ins = e.activation(
                                xnT[:, s * TILE + stt * 128:
                                    s * TILE + stt * 128 + 128],
                                ps_t[:, s * 128:(s + 1) * 128],
                                AF.Identity, bias=lnwb_col_ap(self, 8 + s),
                                scale=lnwb_col_ap(self, s))
                        return ins
                    ev1 = st("scalar", fn, [t1])
                    tr_done.append(ev1)

                    def fn(e, sem, stt=stt):
                        for s in range(4, NS):
                            ins = e.transpose(ps_t[:, (s - 4) * 128:(s - 3) * 128],
                                              xnp[:, (stt % 2) * DM + s * 128:
                                                  (stt % 2) * DM + (s + 1) * 128],
                                              self.ident[:])
                        return ins
                    t2 = st("tensor", fn, [ev1])

                    def fn(e, sem, stt=stt):
                        for s in range(4, NS):
                            ins = e.activation(
                                xnT[:, s * TILE + stt * 128:
                                    s * TILE + stt * 128 + 128],
                                ps_t[:, (s - 4) * 128:(s - 3) * 128],
                                AF.Identity, bias=lnwb_col_ap(self, 8 + s),
                                scale=lnwb_col_ap(self, s))
                        return ins
                    ev2 = st("scalar", fn, [t2])
                    tr_done.append(ev2)
                xnT_ready = tr_done[-1]
                xnT_ready_prev = xnT_ready

                bg_deps = [xnT_ready, wload["wbgT"]]
                if prev["arin"] is not None:
                    bg_deps.append(prev["arin"])

                def fn(e, sem):
                    for sk in range(NS):
                        ins = e.matmul(ps_d[0:48, 0:TILE],
                                       wbg_sb[:, sk * 48:(sk + 1) * 48],
                                       xnT[:, sk * TILE:(sk + 1) * TILE],
                                       start=(sk == 0), stop=(sk == NS - 1))
                    return ins
                tbg = st("tensor", fn, bg_deps)

                def fn(e, sem):
                    return e.tensor_single_scalar(bgt[:], ps_d[0:48, 0:TILE],
                                                  self.bbg_col[:], ALU.add)
                vbg1 = st("vector", fn, [tbg])

                def fn(e, sem):
                    return e.activation(bgt[:], bgt[:], AF.Exp, scale=-1.0)
                abg = st("scalar", fn, [vbg1])

                def fn(e, sem, it=it):
                    e.tensor_scalar(bgt[:], bgt[:], 1.0, None, ALU.add)
                    e.reciprocal(bgt[:], bgt[:])
                    return e.tensor_copy(bgb[:], bgt[:])
                vbg2 = st("vector", fn, [abg])

                def fn(e, sem):
                    for gi in range(4):
                        ins = e.transpose(
                            ps_t[:, 384 + gi * 48:384 + (gi + 1) * 48],
                            bgb[:, gi * 128:(gi + 1) * 128],
                            self.ident[0:48, 0:48])
                    return ins
                tbt = st("tensor", fn, [vbg2, xnT_ready])

                def fn(e, sem, it=it):
                    for gi in range(4):
                        grp = it * 4 + gi
                        e.tensor_copy(self.bgrow[:, grp * 48:(grp + 1) * 48],
                                      ps_t[:, 384 + gi * 48:384 + (gi + 1) * 48])
                    return e.tensor_scalar(
                        self.nbgrow[:, it * 192:(it + 1) * 192],
                        self.bgrow[:, it * 192:(it + 1) * 192],
                        -1.0, None, ALU.mult)
                bg_evac = st("vector", fn, [tbt])
                prev["tbg_evac"] = bg_evac

                def fn(e, sem, it=it):
                    for half, neg in ((0, 0), (1, 0), (0, 1), (1, 1)):
                        buf = self.bgrow if neg == 0 else self.nbgrow
                        for bg2 in range(2):
                            e.dma_start(
                                out=_ap(self.bgrow_c,
                                        (8 * it + half) * 64 + neg * 32
                                        + bg2 * 16,
                                        [[NCH * 64, 64], [128, 4], [1, 16]]),
                                in_=_ap(buf,
                                        half * 64 * (32 * 48) + it * 192
                                        + bg2 * 32,
                                        [[32 * 48, 64], [48, 4], [1, 16]]),
                                allow_slow_non_contiguous=True
                                ).then_inc(sem, 16)
                bgc_tok = st("gpsimd", fn, [bg_evac], dma=8)
                bgc_last = bgc_tok
                tile_done[it] = bgc_tok

                psl = [ps_a, ps_b, ps_c]
                pv_done = {}
                projs = [("q", wq_sb, qtile, "qS"), ("v", wv_sb, vtile, "vS"),
                         ("k", wk_sb, ktile, None),
                         ("za", wa_sb, None, None),
                         ("zaR", waR_sb, None, None)]
                wmap = {"q": "wqT", "v": "wvT", "k": "wkT", "za": "waT",
                        "zaR": "waRT"}
                d_prev = prev["d_last"]
                for ip, (pn, wsb, dst, spname) in enumerate(projs):
                    for so in range(NS):
                        idx = ip * NS + so
                        bank = psl[idx % 3]
                        deps = [xnT_ready, wload[wmap[pn]]]
                        if idx >= 3:
                            deps.append(pv_done[idx - 3])
                        if idx < 3 and it >= 1:
                            deps.append(tile_done[it - 1])
                        if so == 0 and it >= 1 and pn in ("q", "v", "k"):
                            deps.append(spill[(it - 1, {"q": "knbS",
                                                        "v": "kngS",
                                                        "k": "kngS"}[pn])])

                        def fn(e, sem, wsb=wsb, so=so, bank=bank):
                            for sk in range(NS):
                                ins = e.matmul(bank[:],
                                               wsb[:, sk * DM + so * 128:
                                                   sk * DM + (so + 1) * 128],
                                               xnT[:, sk * TILE:(sk + 1) * TILE],
                                               start=(sk == 0),
                                               stop=(sk == NS - 1))
                            return ins
                        tp = st("tensor", fn, deps)

                        if pn in ("q", "v", "k"):
                            def fn(e, sem, so=so, bank=bank, dst=dst):
                                return e.tensor_copy(
                                    _ap(dst, so * 128,
                                        [[NS * TILE, 128], [1024, 4],
                                         [1, 128]]), bank[:])
                            pv_done[idx] = st("vector", fn, [tp])
                        else:
                            dep2 = [tp]
                            if d_prev is not None:
                                dep2.append(d_prev)
                            if pn == "zaR" and so == 0 and \
                               (it, "DS") in spill:
                                dep2.append(spill[(it, "DS")])

                            def fn(e, sem, pn=pn, so=so, bank=bank):
                                col = self.ba_col[:, (0 if pn == "za" else 8) + so:
                                                  (1 if pn == "za" else 9) + so]
                                e.tensor_single_scalar(za_sb[:], bank[:], col,
                                                       ALU.add)
                                return e.tensor_scalar(e2[:], za_sb[:], -1.0,
                                                       0.0, ALU.mult, ALU.max)
                            vd1 = st("vector", fn, dep2)
                            pv_done[idx] = vd1

                            def fn(e, sem):
                                e.activation(e1[:], za_sb[:], AF.Abs)
                                e.activation(e1[:], e1[:], AF.Exp, scale=-1.0)
                                return e.activation(e1[:], e1[:], AF.Ln,
                                                    bias=1.0)
                            ad = st("scalar", fn, [vd1])

                            def fn(e, sem, so=so):
                                return e.tensor_tensor(
                                    _ap(dtile, so * 128,
                                        [[NS * TILE, 128], [1024, 4],
                                         [1, 128]]), e1[:], e2[:], ALU.add)
                            d_prev = st("vector", fn, [ad])

                    if spname:
                        def fn(e, sem, it=it, spname=spname, dst=dst):
                            e.dma_start(
                                out=_ap(d[spname], it * NS * TILE,
                                        [[NS * T, 128], [1, NS * TILE]]),
                                in_=dst[:]).then_inc(sem, 16)
                        spill[(it, spname)] = st("gpsimd", fn,
                                                 [pv_done[ip * NS + NS - 1]],
                                                 dma=1)
                    if pn in ("za", "zaR"):
                        spname2 = "DS" if pn == "za" else "DRS"

                        def fn(e, sem, it=it, spname2=spname2):
                            e.dma_start(
                                out=_ap(d[spname2], it * NS * TILE,
                                        [[NS * T, 128], [1, NS * TILE]]),
                                in_=dtile[:]).then_inc(sem, 16)
                        spill[(it, spname2)] = st("gpsimd", fn, [d_prev],
                                                  dma=1)
                prev["d_last"] = d_prev

                k_last = pv_done[2 * NS + NS - 1]
                prev_sq = None
                for sk in range(NS):
                    def fn(e, sem, sk=sk):
                        sl_ap = _ap(ktile, sk * 128,
                                    [[NS * TILE, 128], [1024, 4], [1, 128]])
                        return e.tensor_tensor(ksqt[:], sl_ap, sl_ap, ALU.mult)
                    vs = st("vector", fn, [k_last] + ([prev_sq] if prev_sq else []))

                    nd = [vs]
                    if sk == 0:
                        nd.append(vbg1)

                    def fn(e, sem, sk=sk):
                        return e.matmul(ps_d[0:16, 0:TILE],
                                        self.hmsq[:, sk * 16:(sk + 1) * 16],
                                        ksqt[:],
                                        start=(sk == 0), stop=(sk == NS - 1))
                    prev_sq = st("tensor", fn, nd)
                tnsq = prev_sq

                def fn(e, sem):
                    e.activation(rin[:], ps_d[0:16, 0:TILE], AF.Ln)
                    return e.activation(rin[:], rin[:], AF.Exp, scale=-0.5)
                arin = st("scalar", fn, [tnsq])
                prev["arin"] = arin

                def fn(e, sem):
                    return e.tensor_scalar(rin[:], rin[:], 1e12, None, ALU.min)
                vrin = st("vector", fn, [arin])

                prev_kn = None
                for so in range(NS):
                    def fn(e, sem, so=so):
                        return e.matmul(ps_e[:],
                                        self.hmbc[:, so * 128:(so + 1) * 128],
                                        rin[:], start=True, stop=True)
                    tb = st("tensor", fn,
                            [vrin] + ([prev_kn] if prev_kn else []))

                    def fn(e, sem, so=so):
                        sl_ap = _ap(ktile, so * 128,
                                    [[NS * TILE, 128], [1024, 4], [1, 128]])
                        return e.tensor_tensor(sl_ap, sl_ap,
                                               ps_e[:, 0:TILE], ALU.mult)
                    prev_kn = st("vector", fn, [tb])

                def fn(e, sem, it=it):
                    e.dma_start(
                        out=_ap(d["knS"], it * NS * TILE,
                                [[NS * T, 128], [1, NS * TILE]]),
                        in_=ktile[:]).then_inc(sem, 16)
                spill[(it, "knS")] = st("gpsimd", fn, [prev_kn], dma=1)

                prev_kb = None
                for dec, (dstb, scn) in enumerate(
                        [(qtile, "knbS"), (vtile, "kngS")]):
                    for so in range(NS):
                        deps = [prev_kn, vbg2]
                        if prev_kb is not None:
                            deps.append(prev_kb)
                        if so == 0:
                            deps.append(spill[(it, "qS" if dec == 0 else "vS")])

                        def fn(e, sem, so=so, dec=dec):
                            src = bgb[0:16, :] if dec == 0 else bgb[32:48, :]
                            lh = (self.hmbc[:, so * 128:(so + 1) * 128]
                                  if dec == 0 else
                                  self.hmbc2[32:48, so * 128:(so + 1) * 128])
                            return e.matmul(ps_e[:], lh, src,
                                            start=True, stop=True)
                        tb2 = st("tensor", fn, deps)

                        def fn(e, sem, so=so, dstb=dstb):
                            kap = _ap(ktile, so * 128,
                                      [[NS * TILE, 128], [1024, 4], [1, 128]])
                            oap = _ap(dstb, so * 128,
                                      [[NS * TILE, 128], [1024, 4], [1, 128]])
                            return e.tensor_tensor(oap, kap, ps_e[:, 0:TILE],
                                                   ALU.mult)
                        prev_kb = st("vector", fn, [tb2])

                    def fn(e, sem, it=it, scn=scn, dstb=dstb):
                        e.dma_start(
                            out=_ap(d[scn], it * NS * TILE,
                                    [[NS * T, 128], [1, NS * TILE]]),
                            in_=dstb[:]).then_inc(sem, 16)
                    spill[(it, scn)] = st("gpsimd", fn, [prev_kb], dma=1)


            def fn(e, sem):
                e.dma_start(out=self.bgrow_c[64:128, :],
                            in_=self.bgrow_c[0:64, :]).then_inc(sem, 16)
            st("gpsimd", fn, [prev["tbg_evac"], bgc_last], dma=1)

            with nc.Block() as blk:
                sch.emit(blk)

    def phase2(self):
        nc, d = self.nc, self.d
        NCHD = getattr(self, "nch_dev", NCH)
        NGD = (NCHD + GRP - 1) // GRP
        with ExitStack() as ctx:
            E = ctx.enter_context
            gb = {}
            for nm in ("kn", "knb", "kng", "q", "v"):
                gb[nm] = [E(nc.sbuf_tensor(f"p2_{nm}{i}", [128, NS * GT], BF16))
                          for i in range(2)]
            for nm in ("Dx", "DRx"):
                gb[nm] = [E(nc.sbuf_tensor(f"p2_{nm}{i}", [128, NS * GT], F32))
                          for i in range(2)]
            VAR_S = ["mvar", "qvar", "pvar", "gvar", "mcvar", "qcvar", "pcvar",
                     "scvar", "kbar", "qbar", "ksh", "kend"]
            VAR_R = ["mvarR", "qvarR", "gvarR", "mcvarR", "qcvarR", "scvarR",
                     "kbarR", "qbarR", "kendR"]
            for nm in VAR_S + VAR_R:
                gb[nm] = [E(nc.sbuf_tensor(f"p2_{nm}{i}", [128, NS * GT], BF16))
                          for i in range(2)]
            gb["O"] = [E(nc.sbuf_tensor(f"p2_O{i}", [128, NS * GT], BF16))
                       for i in range(2)]
            cs = E(nc.sbuf_tensor("p2_cs", [128, NS * GT], F32))
            csf = E(nc.sbuf_tensor("p2_csf", [128, NS * GT], F32))
            dfb = E(nc.sbuf_tensor("p2_dfb", [128, NS * GT], F32))
            arg1 = E(nc.sbuf_tensor("p2_arg1", [128, NS * GT], F32))
            arg2 = E(nc.sbuf_tensor("p2_arg2", [128, NS * GT], F32))
            eb1 = E(nc.sbuf_tensor("p2_eb1", [128, NS * GT], BF16))
            eb2 = E(nc.sbuf_tensor("p2_eb2", [128, NS * GT], BF16))
            refs = E(nc.sbuf_tensor("p2_refs", [128, 128], F32))
            lamf = E(nc.sbuf_tensor("p2_lam", [128, 64], F32))
            S_sb = [E(nc.sbuf_tensor(f"p2_S{i}", [128, 8 * 64], F32))
                    for i in range(2)]
            R_sb = [E(nc.sbuf_tensor(f"p2_R{i}", [128, 8 * 64], F32))
                    for i in range(2)]
            S_bf = [E(nc.sbuf_tensor(f"p2_Sb{i}", [128, 8 * 64], BF16))
                    for i in range(2)]
            R_bf = [E(nc.sbuf_tensor(f"p2_Rb{i}", [128, 8 * 64], BF16))
                    for i in range(2)]
            nall = E(nc.sbuf_tensor("p2_nall", [128, 320], BF16))
            nrm = E(nc.sbuf_tensor("p2_nrm", [128, 128], BF16))
            npw = E(nc.sbuf_tensor("p2_npw", [128, 5 * 128], BF16))
            npwR = E(nc.sbuf_tensor("p2_npwR", [128, 5 * 128], BF16))
            xb1 = E(nc.sbuf_tensor("p2_xb1", [128, 7 * 64], BF16))
            xb2 = E(nc.sbuf_tensor("p2_xb2", [128, 7 * 64], BF16))
            vbb = E(nc.sbuf_tensor("p2_vb", [128, 64], BF16))
            rgb = E(nc.sbuf_tensor("p2_rg", [128, 64], BF16))
            scrf = E(nc.sbuf_tensor("p2_scf", [128, 64], F32))
            scrb = E(nc.sbuf_tensor("p2_scb", [128, 64], BF16))
            ktb = E(nc.sbuf_tensor("p2_ktb", [128, 128], BF16))
            bgc = self.bgrow_c
            ps_g = [E(nc.psum_tensor(f"p2_psg{i}", [128, 512], F32))
                    for i in range(2)]
            ps_s = [E(nc.psum_tensor(f"p2_pss{i}", [128, 512], F32))
                    for i in range(2)]
            ps_m = E(nc.psum_tensor("p2_psm", [128, 512], F32))
            ps_o = [E(nc.psum_tensor(f"p2_pso{i}", [128, 512], F32))
                    for i in range(2)]
            ps_t = E(nc.psum_tensor("p2_pst", [128, 1024], BF16))
            sch = self._sched(ctx, "p2")
            self.p2_sched = sch
            import os as _os
            _lim = int(_os.environ.get("P2LIM", "-1"))
            dumb = refs
            _nstep = [0]

            def st(eng, fn, deps=(), dma=0):
                i = _nstep[0]
                _nstep[0] += 1
                if _lim >= 0 and i >= _lim:
                    if dma:
                        def fn2(e, sem, n=dma):
                            for _ in range(n):
                                e.dma_start(out=dumb[0:1, 0:2],
                                            in_=_ap(d["x"], 0, [[2, 1], [1, 2]])
                                            ).then_inc(sem, 16)
                        return sch.step(eng, fn2, deps, dma=dma)

                    def fn3(e, sem):
                        return e.memset(dumb[0:1, 0:1], 0.0)
                    return sch.step("vector", fn3, deps)
                return sch.step(eng, fn, deps, dma=dma)

            def fn(e, sem):
                for b in (S_sb[0], S_sb[1], R_sb[0], R_sb[1]):
                    e.memset(b[:], 0.0)
                for b in (S_bf[0], S_bf[1], R_bf[0], R_bf[1]):
                    e.memset(b[:], 0.0)
                e.memset(lamf[:], 0.0)
                e.memset(gb["O"][0][:], 0.0)
                e.memset(gb["O"][1][:], 0.0)
                e.memset(scrf[:], 0.0)
                e.memset(scrb[:], 0.0)
                ins = e.memset(refs[:], 0.0)
                return ins
            init_tok = st("vector", fn)

            prep_tok = {}
            load_tok = {}
            ospill_tok = {}
            slot_state = {h: {"state": init_tok} for h in range(H)}
            last_slot_of_group = {}
            gram_free = [None, None]
            sol_free = [None, None]
            o_free = [None, None]
            misc_free = None
            pst_free = None
            slotbuf_free = [None, None]
            last = init_tok

            for g in range(NGD):
                gi = g % 2
                ldeps = [last_slot_of_group[g - 2]] if g >= 2 else []

                def fn(e, sem, g=g, gi=gi):
                    for nm, sc in [("kn", "knS"), ("knb", "knbS"),
                                   ("kng", "kngS"), ("q", "qS"), ("v", "vS"),
                                   ("Dx", "DS"), ("DRx", "DRS")]:
                        e.dma_start(out=gb[nm][gi][:],
                                    in_=_ap(d[sc], g * NS * GT,
                                            [[NS * T, 128], [1, NS * GT]])
                                    ).then_inc(sem, 16)
                load_tok[g] = st("sync", fn, ldeps, dma=7)

                ptoks = []
                for dec in range(2):
                    dnm = "Dx" if dec == 0 else "DRx"
                    roff = dec * 64
                    pfx = "" if dec == 0 else "R"
                    kvb = "knb" if dec == 0 else "kng"

                    def fn(e, sem, gi=gi, dnm=dnm):
                        return e.tensor_scalar(dfb[:], gb[dnm][gi][:], FLOOR,
                                               None, ALU.min)
                    dep0 = [load_tok[g]] + ([ptoks[-1]] if ptoks else [])
                    if g >= 1 and dec == 0:
                        dep0.append(last_slot_of_group[g - 1])
                    t_min = st("vector", fn, dep0)

                    def fn(e, sem, gi=gi, dnm=dnm):
                        for s in range(NS):
                            for j in range(GRP):
                                sl = slice(s * GT + j * C, s * GT + (j + 1) * C)
                                e.tensor_tensor_scan(cs[:, sl],
                                                     gb[dnm][gi][:, sl],
                                                     gb[dnm][gi][:, sl], 0.0,
                                                     ALU.add, ALU.bypass)
                        for s in range(NS):
                            for j in range(GRP):
                                sl = slice(s * GT + j * C, s * GT + (j + 1) * C)
                                ins = e.tensor_tensor_scan(
                                    csf[:, sl], dfb[:, sl], dfb[:, sl], 0.0,
                                    ALU.add, ALU.bypass)
                        return ins
                    t_scan = st("vector", fn, [t_min])

                    def fn(e, sem, roff=roff):
                        for k2, (buf, col) in enumerate([(cs, C - 1), (cs, 31),
                                                         (csf, 15), (csf, 47)]):
                            ins = e.activation(
                                refs[:, roff + k2 * 16:roff + (k2 + 1) * 16],
                                _ap(buf, col, [[NS * GT, 128], [C, 16]]),
                                AF.Identity)
                        ins = e.activation(lamf[:, dec * 32:dec * 32 + 16],
                                           refs[:, roff:roff + 16], AF.Exp,
                                           scale=-1.0)
                        return ins
                    t_ref = st("scalar", fn, [t_scan])

                    def fn(e, sem):
                        e.tensor_copy(_ap(arg1, 1, [[NS * GT, 128],
                                                    [1, NS * GT - 1]]),
                                      _ap(cs, 0, [[NS * GT, 128],
                                                  [1, NS * GT - 1]]))
                        return e.memset(_ap(arg1, 0, [[NS * GT, 128], [C, 16],
                                                      [1, 1]]), 0.0)
                    t_arg1 = st("vector", fn, [t_ref])

                    def fn(e, sem):
                        e.activation(eb1[:], cs[:], AF.Exp, scale=-1.0)
                        return e.activation(eb2[:], arg1[:], AF.Exp,
                                            scale=-1.0)
                    t_e1 = st("scalar", fn, [t_arg1])

                    def fn(e, sem, gi=gi, pfx=pfx, dec=dec):
                        e.tensor_tensor(gb["kbar" + pfx][gi][:],
                                        gb["kn"][gi][:], eb1[:], ALU.mult)
                        q_ins = e.tensor_tensor(gb["qbar" + pfx][gi][:],
                                                gb["q"][gi][:], eb1[:],
                                                ALU.mult)
                        if dec == 0:
                            q_ins = e.tensor_tensor(gb["ksh"][gi][:],
                                                    gb["kn"][gi][:], eb2[:],
                                                    ALU.mult)
                        return q_ins
                    t_v1 = st("vector", fn, [t_e1])

                    def fn(e, sem, roff=roff):
                        for s in range(NS):
                            for j in range(GRP):
                                sl = slice(s * GT + j * C, s * GT + (j + 1) * C)
                                rc = refs[:, roff + s * 2 + j:
                                          roff + s * 2 + j + 1]
                                ins = e.tensor_single_scalar(
                                    arg2[:, sl], cs[:, sl], rc, ALU.subtract)
                        return ins
                    t_arg2 = st("vector", fn, [t_v1])

                    def fn(e, sem):
                        return e.activation(eb1[:], arg2[:], AF.Exp)
                    t_e2 = st("scalar", fn, [t_arg2])

                    def fn(e, sem, gi=gi, pfx=pfx):
                        return e.tensor_tensor(gb["kend" + pfx][gi][:],
                                               gb["kn"][gi][:], eb1[:],
                                               ALU.mult)
                    t_v2 = st("vector", fn, [t_e2])

                    def fn(e, sem, roff=roff):
                        for s in range(NS):
                            for j in range(GRP):
                                for hh in range(2):
                                    sl = slice(s * GT + j * C + hh * 32,
                                               s * GT + j * C + (hh + 1) * 32)
                                    rc = refs[:, roff + (2 + hh) * 16 + s * 2 + j:
                                              roff + (2 + hh) * 16 + s * 2 + j + 1]
                                    ins = e.tensor_single_scalar(
                                        arg1[:, sl], csf[:, sl], rc,
                                        ALU.subtract)
                        return ins
                    t_arg3 = st("vector", fn, [t_v2])

                    def fn(e, sem):
                        e.activation(eb1[:], arg1[:], AF.Exp, scale=-1.0)
                        return e.activation(eb2[:], arg1[:], AF.Exp)
                    t_e3 = st("scalar", fn, [t_arg3])

                    def fn(e, sem, gi=gi, pfx=pfx, kvb=kvb):
                        e.tensor_tensor(gb["mvar" + pfx][gi][:],
                                        gb[kvb][gi][:], eb1[:], ALU.mult)
                        e.tensor_tensor(gb["qvar" + pfx][gi][:],
                                        gb["q"][gi][:], eb1[:], ALU.mult)
                        return e.tensor_tensor(gb["gvar" + pfx][gi][:],
                                               gb["kn"][gi][:], eb2[:],
                                               ALU.mult)
                    t_v3 = st("vector", fn, [t_e3])

                    if dec == 0:
                        def fn(e, sem, roff=roff):
                            for s in range(NS):
                                for j in range(GRP):
                                    for hh in range(2):
                                        c0 = s * GT + j * C + hh * 32
                                        rc = refs[:, roff + (2 + hh) * 16 + s * 2 + j:
                                                  roff + (2 + hh) * 16 + s * 2 + j + 1]
                                        if c0 == 0:
                                            ins = e.tensor_single_scalar(
                                                arg2[:, 1:32],
                                                _ap(csf, 0, [[NS * GT, 128],
                                                             [1, 31]]),
                                                rc, ALU.subtract)
                                        else:
                                            ins = e.tensor_single_scalar(
                                                arg2[:, c0:c0 + 32],
                                                _ap(csf, c0 - 1,
                                                    [[NS * GT, 128], [1, 32]]),
                                                rc, ALU.subtract)
                            return ins
                        t_arg4 = st("vector", fn, [t_v3])

                        def fn(e, sem):
                            return e.activation(eb1[:], arg2[:], AF.Exp,
                                                scale=-1.0)
                        t_e4 = st("scalar", fn, [t_arg4])

                        def fn(e, sem, gi=gi):
                            return e.tensor_tensor(gb["pvar"][gi][:],
                                                   gb["kn"][gi][:], eb1[:],
                                                   ALU.mult)
                        t_v3b = st("vector", fn, [t_e4])
                    else:
                        t_v3b = t_v3

                    def fn(e, sem, roff=roff):
                        for s in range(NS):
                            for j in range(GRP):
                                sl = slice(s * GT + j * C, s * GT + (j + 1) * C)
                                rc = refs[:, roff + 16 + s * 2 + j:
                                          roff + 16 + s * 2 + j + 1]
                                ins = e.tensor_single_scalar(
                                    arg1[:, sl], cs[:, sl], rc, ALU.subtract)
                        return ins
                    t_arg5 = st("vector", fn, [t_v3b])

                    def fn(e, sem):
                        e.activation(arg2[:], arg1[:], AF.Abs)
                        return e.activation(eb1[:], arg2[:], AF.Exp,
                                            scale=-1.0)
                    t_e5 = st("scalar", fn, [t_arg5])

                    def fn(e, sem, gi=gi, pfx=pfx, kvb=kvb):
                        e.tensor_tensor(gb["mcvar" + pfx][gi][:],
                                        gb[kvb][gi][:], eb1[:], ALU.mult)
                        e.tensor_tensor(gb["qcvar" + pfx][gi][:],
                                        gb["q"][gi][:], eb1[:], ALU.mult)
                        return e.tensor_tensor(gb["scvar" + pfx][gi][:],
                                               gb["kn"][gi][:], eb1[:],
                                               ALU.mult)
                    t_v5 = st("vector", fn, [t_e5])

                    if dec == 0:
                        def fn(e, sem):
                            return e.activation(
                                _ap(eb2, 1, [[NS * GT, 128], [1, NS * GT - 1]]),
                                _ap(arg1, 0, [[NS * GT, 128], [1, NS * GT - 1]]),
                                AF.Exp, scale=-1.0)
                        t_e6 = st("scalar", fn, [t_v5])

                        def fn(e, sem, gi=gi):
                            return e.tensor_tensor(gb["pcvar"][gi][:],
                                                   gb["kn"][gi][:], eb2[:],
                                                   ALU.mult)
                        ptoks.append(st("vector", fn, [t_e6]))
                    else:
                        ptoks.append(t_v5)
                prep_tok[g] = ptoks[-1]

                for j in range(GRP):
                    c = g * GRP + j
                    if c >= NCHD:
                        break
                    for h in range(H):
                        k = c * H + h
                        ki = k % 2
                        rows = (h % 2) * 64
                        cb = (h // 2) * GT + j * C
                        hb = (h // 2) * 64
                        sidx = (h // 2) * 2 + j
                        ss = slot_state[h]
                        first_slot_of_group = (j == 0 and h == 0)

                        deps = [prep_tok[g]]
                        if gram_free[ki] is not None:
                            deps.append(gram_free[ki])
                        if first_slot_of_group and g >= 2 and \
                                (g - 2) in ospill_tok:
                            deps.append(ospill_tok[g - 2])

                        def fn(e, sem, gi=gi, rows=rows, cb=cb, ki=ki):
                            ps = ps_g[ki]

                            def gs(nm, c0, n):
                                return gb[nm][gi][rows:rows + 64,
                                                  cb + c0:cb + c0 + n]
                            first = True
                            ins = None
                            for vb, tnm in enumerate(["mvar", "qvar", "pvar"]):
                                ins = e.matmul(ps[rows:rows + 32, vb * 64:vb * 64 + 32],
                                               gs("gvar", 0, 32),
                                               gs(tnm, 0, 32),
                                               start=True, stop=True,
                                               tile_position=(rows, rows))
                                first = False
                                ins = e.matmul(
                                    ps[rows + 32:rows + 64, vb * 64:vb * 64 + 64],
                                    gs("gvar", 32, 32), gs(tnm, 0, 64),
                                    start=True, stop=True,
                                    tile_position=(rows, rows + 32))
                                tc = {"mvar": "mcvar", "qvar": "qcvar",
                                      "pvar": "pcvar"}[tnm]
                                ins = e.matmul(
                                    ps[rows:rows + 32, vb * 64 + 32:vb * 64 + 64],
                                    gs("scvar", 0, 32), gs(tc, 32, 32),
                                    start=True, stop=True,
                                    tile_position=(rows, rows))
                            for vb, tnm in enumerate(["mvarR", "qvarR"]):
                                vb2 = vb + 3
                                e.matmul(ps[rows:rows + 32, vb2 * 64:vb2 * 64 + 32],
                                         gs("gvarR", 0, 32), gs(tnm, 0, 32),
                                         start=True, stop=True,
                                         tile_position=(rows, rows))
                                e.matmul(
                                    ps[rows + 32:rows + 64, vb2 * 64:vb2 * 64 + 64],
                                    gs("gvarR", 32, 32), gs(tnm, 0, 64),
                                    start=True, stop=True,
                                    tile_position=(rows, rows + 32))
                                tc = {"mvarR": "mcvarR", "qvarR": "qcvarR"}[tnm]
                                ins = e.matmul(
                                    ps[rows:rows + 32, vb2 * 64 + 32:vb2 * 64 + 64],
                                    gs("scvarR", 0, 32), gs(tc, 32, 32),
                                    start=True, stop=True,
                                    tile_position=(rows, rows))
                            return ins
                        t_gram = st("tensor", fn, deps)

                        deps = [t_gram]
                        if slotbuf_free[ki] is not None:
                            deps.append(slotbuf_free[ki])

                        def fn(e, sem, rows=rows, ki=ki):
                            return e.tensor_tensor(nall[rows:rows + 64, :],
                                                   ps_g[ki][rows:rows + 64, 0:320],
                                                   self.mpq[rows:rows + 64, :], ALU.mult)
                        t_nall = st("vector", fn, deps)
                        gram_free[ki] = t_nall

                        deps = [t_nall]
                        if pst_free is not None:
                            deps.append(pst_free)

                        def fn(e, sem, rows=rows, ki=ki):
                            e.transpose(ps_t[rows:rows + 64, 0:64], nall[rows:rows + 64, 0:64],
                                        self.ident[rows:rows + 64, rows:rows + 64])
                            return e.transpose(ps_t[rows:rows + 64, 64:128],
                                               nall[rows:rows + 64, 192:256],
                                               self.ident[rows:rows + 64, rows:rows + 64])
                        t_seed = st("tensor", fn, deps)

                        def fn(e, sem, rows=rows, ki=ki):
                            return e.tensor_copy(nrm[rows:rows + 64, :], ps_t[rows:rows + 64, 0:128])
                        t_nrm = st("vector", fn, [t_seed])
                        pst_free = t_nrm

                        sq_toks = []
                        for path in range(2):
                            npb = npw if path == 0 else npwR
                            nts0 = 0 if path == 0 else 192
                            nrm0 = 0 if path == 0 else 64
                            prevt = t_nrm
                            for lvl in range(5):
                                si = (k + lvl + path) % 2
                                deps = [prevt]
                                if sol_free[si] is not None:
                                    deps.append(sol_free[si])

                                def fn(e, sem, rows=rows, ki=ki, lvl=lvl, si=si, npb=npb,
                                       nts0=nts0, nrm0=nrm0):
                                    if lvl == 0:
                                        ts_ap = nall[rows:rows + 64, nts0:nts0 + 64]
                                        rm_ap = nrm[rows:rows + 64, nrm0:nrm0 + 64]
                                    else:
                                        ts_ap = npb[rows:rows + 64, (lvl - 1) * 128:
                                                        (lvl - 1) * 128 + 64]
                                        rm_ap = npb[rows:rows + 64,
                                                    (lvl - 1) * 128 + 64:
                                                    lvl * 128]
                                    e.matmul(ps_s[si][rows:rows + 64, 0:64],
                                             rm_ap, ts_ap, start=True,
                                             stop=False,
                                             tile_position=(rows, rows))
                                    return e.matmul(
                                        ps_s[si][rows:rows + 64, 64:128],
                                        ts_ap, rm_ap, start=False, stop=True,
                                        tile_position=(rows, rows))
                                t_sq = st("tensor", fn, deps)

                                def fn(e, sem, rows=rows, ki=ki, lvl=lvl, si=si, npb=npb):
                                    return e.tensor_copy(
                                        npb[rows:rows + 64,
                                            lvl * 128:(lvl + 1) * 128],
                                        ps_s[si][rows:rows + 64, 0:128])
                                prevt = st("vector", fn, [t_sq])
                                sol_free[si] = prevt
                            sq_toks.append(prevt)

                        deps = [prep_tok[g], pst_free]

                        def fn(e, sem, gi=gi, rows=rows, cb=cb):
                            return e.transpose(
                                ps_t[rows:rows + 64, 128:192],
                                gb["v"][gi][rows:rows + 64, cb:cb + 64],
                                self.ident[rows:rows + 64, rows:rows + 64])
                        t_vt = st("tensor", fn, deps)

                        def fn(e, sem, rows=rows, ki=ki, c=c, h=h):
                            bcol = bgc[rows:rows + 64, c * 64 + h:c * 64 + h + 1]
                            return e.activation(vbb[rows:rows + 64, :], ps_t[rows:rows + 64, 128:192],
                                                AF.Identity, scale=bcol)
                        t_vb = st("scalar", fn, [t_vt])
                        pst_free = t_vb

                        deps = [ss["state"], t_nall]
                        if misc_free is not None:
                            deps.append(misc_free)

                        def fn(e, sem, gi=gi, rows=rows, cb=cb, c=c, hb=hb):
                            return e.matmul(
                                ps_m[rows:rows + 64, 0:64],
                                gb["kbar"][gi][rows:rows + 64, cb:cb + 64],
                                S_bf[c % 2][rows:rows + 64, hb:hb + 64],
                                start=True, stop=True,
                                tile_position=(rows, rows))
                        t_u = st("tensor", fn, deps)

                        def fn(e, sem, rows=rows, ki=ki, c=c, h=h):
                            nbcol = bgc[rows:rows + 64, c * 64 + 32 + h:c * 64 + 33 + h]
                            return e.scalar_tensor_tensor(
                                xb1[rows:rows + 64, 0:64], ps_m[rows:rows + 64, 0:64], nbcol,
                                vbb[rows:rows + 64, :], ALU.mult, ALU.add)
                        t_b = st("vector", fn, [t_vb, t_u])
                        misc_free = t_b

                        prevt = t_b
                        for i in range(6):
                            si = (k + i) % 2
                            deps = [prevt, sq_toks[0]]
                            if sol_free[si] is not None:
                                deps.append(sol_free[si])

                            def fn(e, sem, rows=rows, ki=ki, i=i, si=si):
                                if i == 0:
                                    ts_ap = nall[rows:rows + 64, 0:64]
                                else:
                                    ts_ap = npw[rows:rows + 64, (i - 1) * 128:
                                                    (i - 1) * 128 + 64]
                                return e.matmul(
                                    ps_s[si][rows:rows + 64, 128:192], ts_ap,
                                    xb1[rows:rows + 64, i * 64:(i + 1) * 64],
                                    start=True, stop=True,
                                    tile_position=(rows, rows))
                            t_ap2 = st("tensor", fn, deps)

                            def fn(e, sem, rows=rows, ki=ki, i=i, si=si):
                                return e.tensor_tensor(
                                    xb1[rows:rows + 64, (i + 1) * 64:(i + 2) * 64],
                                    ps_s[si][rows:rows + 64, 128:192],
                                    xb1[rows:rows + 64, i * 64:(i + 1) * 64],
                                    ALU.add)
                            prevt = st("vector", fn, [t_ap2])
                            sol_free[si] = prevt
                        t_w = prevt

                        deps = [t_w, ss["state"], misc_free]

                        def fn(e, sem, gi=gi, rows=rows, cb=cb, ki=ki, c=c,
                               hb=hb):
                            e.matmul(ps_m[rows:rows + 64, 64:128],
                                     S_bf[c % 2][rows:rows + 64, hb:hb + 64],
                                     gb["ksh"][gi][rows:rows + 64, cb:cb + 64],
                                     start=True, stop=False,
                                     tile_position=(rows, rows))
                            return e.matmul(ps_m[rows:rows + 64, 64:128],
                                            xb1[rows:rows + 64, 384:448],
                                            nall[rows:rows + 64, 128:192],
                                            start=False, stop=True,
                                            tile_position=(rows, rows))
                        t_pred = st("tensor", fn, deps)

                        def fn(e, sem, gi=gi, rows=rows, cb=cb, ki=ki):
                            e.scalar_tensor_tensor(
                                scrf[rows:rows + 64, :],
                                ps_m[rows:rows + 64, 64:128], -1.0,
                                gb["v"][gi][rows:rows + 64, cb:cb + 64],
                                ALU.mult, ALU.add)
                            return e.tensor_scalar(scrb[rows:rows + 64, :],
                                                   scrf[rows:rows + 64, :],
                                                   1.0, -1.0, ALU.min, ALU.max)
                        t_r = st("vector", fn, [t_pred])
                        misc_free = t_r

                        deps = [t_r, pst_free]

                        def fn(e, sem, ki=ki, rows=rows):
                            return e.transpose(
                                ps_t[rows:rows + 64, 192:256], scrb[rows:rows + 64, :],
                                self.ident[rows:rows + 64, rows:rows + 64])
                        t_rt = st("tensor", fn, deps)

                        def fn(e, sem, rows=rows, ki=ki, c=c, h=h):
                            gcol = bgc[rows:rows + 64, c * 64 + 16 + h:c * 64 + 17 + h]
                            return e.activation(rgb[rows:rows + 64, :], ps_t[rows:rows + 64, 192:256],
                                                AF.Identity, scale=gcol)
                        t_rg = st("scalar", fn, [t_rt])
                        pst_free = t_rg

                        deps = [ss["state"], t_r]

                        def fn(e, sem, gi=gi, rows=rows, cb=cb, c=c, hb=hb):
                            return e.matmul(
                                ps_m[rows:rows + 64, 128:192],
                                gb["kbarR"][gi][rows:rows + 64, cb:cb + 64],
                                R_bf[c % 2][rows:rows + 64, hb:hb + 64],
                                start=True, stop=True,
                                tile_position=(rows, rows))
                        t_ur = st("tensor", fn, deps)

                        def fn(e, sem, rows=rows, ki=ki, c=c, h=h):
                            ngcol = bgc[rows:rows + 64, c * 64 + 48 + h:c * 64 + 49 + h]
                            return e.scalar_tensor_tensor(
                                xb2[rows:rows + 64, 0:64], ps_m[rows:rows + 64, 128:192],
                                ngcol, rgb[rows:rows + 64, :], ALU.mult, ALU.add)
                        t_br = st("vector", fn, [t_rg, t_ur])

                        prevt = t_br
                        for i in range(6):
                            si = (k + i + 1) % 2
                            deps = [prevt, sq_toks[1]]
                            if sol_free[si] is not None:
                                deps.append(sol_free[si])

                            def fn(e, sem, rows=rows, ki=ki, i=i, si=si):
                                if i == 0:
                                    ts_ap = nall[rows:rows + 64, 192:256]
                                else:
                                    ts_ap = npwR[rows:rows + 64, (i - 1) * 128:
                                                     (i - 1) * 128 + 64]
                                return e.matmul(
                                    ps_s[si][rows:rows + 64, 192:256], ts_ap,
                                    xb2[rows:rows + 64, i * 64:(i + 1) * 64],
                                    start=True, stop=True,
                                    tile_position=(rows, rows))
                            t_ap2 = st("tensor", fn, deps)

                            def fn(e, sem, rows=rows, ki=ki, i=i, si=si):
                                return e.tensor_tensor(
                                    xb2[rows:rows + 64, (i + 1) * 64:(i + 2) * 64],
                                    ps_s[si][rows:rows + 64, 192:256],
                                    xb2[rows:rows + 64, i * 64:(i + 1) * 64],
                                    ALU.add)
                            prevt = st("vector", fn, [t_ap2])
                            sol_free[si] = prevt
                        t_wr = prevt

                        deps = [t_w, t_wr, ss["state"]]
                        if o_free[ki] is not None:
                            deps.append(o_free[ki])

                        def fn(e, sem, gi=gi, rows=rows, cb=cb, ki=ki, c=c,
                               hb=hb):
                            ps = ps_o[ki]
                            e.matmul(ps[rows:rows + 64, 0:64],
                                     S_bf[c % 2][rows:rows + 64, hb:hb + 64],
                                     gb["qbar"][gi][rows:rows + 64,
                                                    cb:cb + 64],
                                     start=True, stop=False,
                                     tile_position=(rows, rows))
                            e.matmul(ps[rows:rows + 64, 0:64],
                                     xb1[rows:rows + 64, 384:448],
                                     nall[rows:rows + 64, 64:128],
                                     start=False, stop=False,
                                     tile_position=(rows, rows))
                            e.matmul(ps[rows:rows + 64, 0:64],
                                     R_bf[c % 2][rows:rows + 64, hb:hb + 64],
                                     gb["qbarR"][gi][rows:rows + 64,
                                                     cb:cb + 64],
                                     start=False, stop=False,
                                     tile_position=(rows, rows))
                            return e.matmul(ps[rows:rows + 64, 0:64],
                                            xb2[rows:rows + 64, 384:448],
                                            nall[rows:rows + 64, 256:320],
                                            start=False, stop=True,
                                            tile_position=(rows, rows))
                        t_omm = st("tensor", fn, deps)

                        odeps = [t_omm]
                        if first_slot_of_group and g >= 2 and \
                                (g - 2) in ospill_tok:
                            odeps.append(ospill_tok[g - 2])

                        def fn(e, sem, gi=gi, rows=rows, cb=cb, ki=ki):
                            return e.tensor_copy(
                                gb["O"][gi][rows:rows + 64, cb:cb + 64],
                                ps_o[ki][rows:rows + 64, 0:64])
                        t_oev = st("vector", fn, odeps)
                        o_free[ki] = t_oev

                        deps = [prep_tok[g], pst_free]

                        def fn(e, sem, gi=gi, rows=rows, cb=cb):
                            e.transpose(ps_t[rows:rows + 64, 256:320],
                                        gb["kend"][gi][rows:rows + 64,
                                                       cb:cb + 64],
                                        self.ident[rows:rows + 64,
                                                   rows:rows + 64])
                            return e.transpose(
                                ps_t[rows:rows + 64, 320:384],
                                gb["kendR"][gi][rows:rows + 64, cb:cb + 64],
                                self.ident[rows:rows + 64, rows:rows + 64])
                        t_kt = st("tensor", fn, deps)

                        def fn(e, sem, rows=rows, ki=ki):
                            return e.tensor_copy(ktb[rows:rows + 64, :], ps_t[rows:rows + 64, 256:384])
                        t_ktev = st("vector", fn, [t_kt])
                        pst_free = t_ktev

                        deps = [t_ktev, t_w, t_wr, t_omm, t_br]

                        def fn(e, sem, ki=ki, rows=rows):
                            e.matmul(ps_m[rows:rows + 64, 192:256],
                                     ktb[rows:rows + 64, 0:64],
                                     xb1[rows:rows + 64, 384:448],
                                     start=True, stop=True,
                                     tile_position=(rows, rows))
                            return e.matmul(ps_m[rows:rows + 64, 256:320],
                                            ktb[rows:rows + 64, 64:128],
                                            xb2[rows:rows + 64, 384:448],
                                            start=True, stop=True,
                                            tile_position=(rows, rows))
                        t_smm = st("tensor", fn, deps)

                        def fn(e, sem, rows=rows, c=c, hb=hb, sidx=sidx):
                            lc_a = lamf[rows:rows + 64, sidx:sidx + 1]
                            lc_r = lamf[rows:rows + 64,
                                        32 + sidx:32 + sidx + 1]
                            e.scalar_tensor_tensor(
                                S_sb[(c + 1) % 2][rows:rows + 64, hb:hb + 64],
                                S_sb[c % 2][rows:rows + 64, hb:hb + 64], lc_a,
                                ps_m[rows:rows + 64, 192:256], ALU.mult,
                                ALU.add)
                            e.tensor_copy(
                                S_bf[(c + 1) % 2][rows:rows + 64, hb:hb + 64],
                                S_sb[(c + 1) % 2][rows:rows + 64, hb:hb + 64])
                            e.scalar_tensor_tensor(
                                R_sb[(c + 1) % 2][rows:rows + 64, hb:hb + 64],
                                R_sb[c % 2][rows:rows + 64, hb:hb + 64], lc_r,
                                ps_m[rows:rows + 64, 256:320], ALU.mult,
                                ALU.add)
                            return e.tensor_copy(
                                R_bf[(c + 1) % 2][rows:rows + 64, hb:hb + 64],
                                R_sb[(c + 1) % 2][rows:rows + 64, hb:hb + 64])
                        t_sev = st("vector", fn, [t_smm])
                        if getattr(self, "dbg_slot", None) == (c, h):
                            def fn(e, sem, rows=rows, ki=ki):
                                e.dma_start(out=_ap(d["x1S"], 0,
                                                    [[NS * T, 64], [1, 320]]),
                                            in_=nall[rows:rows + 64, :]).then_inc(sem, 16)
                                e.dma_start(out=_ap(d["x1S"], 512,
                                                    [[NS * T, 64], [1, 448]]),
                                            in_=xb1[rows:rows + 64, :]).then_inc(sem, 16)
                                e.dma_start(out=_ap(d["x1S"], 1024,
                                                    [[NS * T, 64], [1, 448]]),
                                            in_=xb2[rows:rows + 64, :]).then_inc(sem, 16)
                                e.dma_start(out=_ap(d["x1S"], 1536,
                                                    [[NS * T, 128], [1, 64]]),
                                            in_=scrb[:]).then_inc(sem, 16)
                            st("gpsimd", fn, [t_sev], dma=4)
                        misc_free = t_sev
                        ss["state"] = t_sev
                        slotbuf_free[ki] = t_sev
                        last = t_sev
                    last_slot_of_group[g] = last

                def fn(e, sem, g=g, gi=gi):
                    e.dma_start(out=_ap(d["OS"], g * NS * GT,
                                        [[NS * T, 128], [1, NS * GT]]),
                                in_=gb["O"][gi][:]).then_inc(sem, 16)
                ospill_tok[g] = st("gpsimd", fn, [last_slot_of_group[g]],
                                   dma=1)

            with nc.Block() as blk:
                sch.emit(blk)

    def phase3(self):
        nc, d = self.nc, self.d
        P3T = 512
        n3 = getattr(self, "n3", T // P3T)
        NCHW = 16
        with ExitStack() as ctx:
            E = ctx.enter_context
            wo_sb = E(nc.sbuf_tensor("p3_wo", [128, NS * DM], BF16))
            otile = E(nc.sbuf_tensor("p3_ot", [128, 2 * 4 * DM], BF16))
            xrow = E(nc.sbuf_tensor("p3_xr", [128, 4 * DM], F32))
            atn = E(nc.sbuf_tensor("p3_atn", [128, NS * P3T], BF16))
            atT = E(nc.sbuf_tensor("p3_atT", [128, DM], F32))
            x1r = E(nc.sbuf_tensor("p3_x1r", [128, 4 * DM], F32))
            scr = E(nc.sbuf_tensor("p3_scr", [128, DM], F32))
            scr2 = E(nc.sbuf_tensor("p3_scr2", [128, DM], F32))
            xqr = E(nc.sbuf_tensor("p3_xqr", [128, DM], BF16))
            bitV = E(nc.sbuf_tensor("p3_btv", [128, 512], BF16))
            bitG = E(nc.sbuf_tensor("p3_btg", [128, 512], BF16))
            rG = E(nc.sbuf_tensor("p3_rg", [128, 512], F32))
            cols = E(nc.sbuf_tensor("p3_cols", [128, 16], F32))
            m2c = E(nc.sbuf_tensor("p3_m2c", [128, 1], BF16))
            wch = [E(nc.sbuf_tensor(f"p3_wch{i}", [128, 4096], F32))
                   for i in range(2)]
            pab = E(nc.sbuf_tensor("p3_pab", [128, 4], BF16))
            pabr = E(nc.sbuf_tensor("p3_pabr", [1, 8], BF16))
            ones1 = E(nc.sbuf_tensor("p3_ones1", [1, 128], BF16))
            onesc = E(nc.sbuf_tensor("p3_onesc", [128, 1], BF16))
            ps_a = E(nc.psum_tensor("p3_pa", [128, 512], F32))
            ps_b = E(nc.psum_tensor("p3_pb", [128, 512], F32))
            ps_c = E(nc.psum_tensor("p3_pc", [128, 512], F32))
            ps_e = E(nc.psum_tensor("p3_pe", [128, 512], F32))
            ps_t = E(nc.psum_tensor("p3_pt", [128, 1024], BF16))
            sch = self._sched(ctx, "p3")
            st = sch.step

            def fn(e, sem):
                e.memset(self.hsrow[:], 0.0)
                e.memset(cols[:], 0.0)
                e.memset(self.wq_s[:], 0.0)
                e.memset(pab[:], 0.0)
                e.memset(pabr[:], 0.0)
                e.memset(ones1[:], 1.0)
                return e.memset(onesc[:], 1.0)
            t_init = st("vector", fn)

            def fn(e, sem):
                for s in range(NS):
                    e.dma_start(
                        out=wo_sb[:, s * DM:(s + 1) * DM],
                        in_=_ap(d["woT"], s * 128 * DM,
                                [[DM, 128], [1, DM]])).then_inc(sem, 16)
            t_wo = st("gpsimd", fn, dma=NS)

            pa_dma, pa_red, pa_acc = {}, {}, {}

            def passA_load(ci):
                wnm = "w1T" if ci < 8 else "w2T"
                ch = ci % 8

                def fn(e, sem, wnm=wnm, ch=ch, ci=ci):
                    e.dma_start(out=wch[ci % 2][:],
                                in_=_ap(d[wnm], ch * 128 * 4096,
                                        [[4096, 128], [1, 4096]])
                                ).then_inc(sem, 16)
                pa_dma[ci] = st("gpsimd", fn, [pa_red.get(ci - 2)], dma=1)

            def passA_red(ci):
                ac = 8 * (ci // 8)

                def fn(e, sem, ci=ci, ac=ac):
                    return e.activation(wch[ci % 2][:], wch[ci % 2][:],
                                        AF.Abs,
                                        accum_out=self.wq_s[:, ac + 1:ac + 2])
                pa_red[ci] = st("scalar", fn, [pa_dma[ci], pa_acc.get(ci - 1)])

                def fn(e, sem, ci=ci, ac=ac):
                    if ci % 8 == 0:
                        return e.tensor_copy(self.wq_s[:, ac:ac + 1],
                                             self.wq_s[:, ac + 1:ac + 2])
                    return e.tensor_tensor(self.wq_s[:, ac:ac + 1],
                                           self.wq_s[:, ac:ac + 1],
                                           self.wq_s[:, ac + 1:ac + 2],
                                           ALU.add)
                pa_acc[ci] = st("vector", fn, [pa_red[ci], t_init])

            pst_free = None
            atev_last = {}
            x1_last = {}
            sp_prev = [None] * 4
            sp_last = None
            hdma_prev = None
            lad_prev = []
            nextload = nextred = 0
            for it in range(n3):
                while nextload < min(NCHW, 2 * (it + 1)):
                    passA_load(nextload)
                    nextload += 1
                while nextred < min(NCHW, 2 * it):
                    passA_red(nextred)
                    nextred += 1

                def fn(e, sem, it=it):
                    e.dma_start(
                        out=otile[:, (it % 2) * 4 * DM:(it % 2 + 1) * 4 * DM],
                        in_=_ap(d["OS"], it * 4 * DM,
                                [[NS * T, 128], [1, 4 * DM]])
                        ).then_inc(sem, 16)
                t_old = st("sync", fn, [atev_last.get(it - 2)], dma=1)

                def fn(e, sem, it=it):
                    for stt in range(4):
                        e.dma_start(
                            out=xrow[:, stt * DM:(stt + 1) * DM],
                            in_=_ap(d["x"], (it * P3T + stt * 128) * DM,
                                    [[DM, 128], [1, DM]]),
                        ).then_inc(sem, 16)
                t_xld = st("sync", fn, [x1_last.get(it - 1)], dma=4)

                at_ev = []
                for so in range(NS):
                    bank = [ps_a, ps_b][so % 2]
                    deps = [t_old, t_wo]
                    if len(at_ev) >= 2:
                        deps.append(at_ev[-2])

                    def fn(e, sem, it=it, so=so, bank=bank):
                        ob = (it % 2) * 4 * DM
                        for sk in range(NS):
                            ins = e.matmul(
                                bank[:],
                                wo_sb[:, sk * DM + so * 128:
                                      sk * DM + (so + 1) * 128],
                                _ap(otile, ob + sk * 128,
                                    [[2 * 4 * DM, 128], [DM, 4], [1, 128]]),
                                start=(sk == 0), stop=(sk == NS - 1))
                        return ins
                    t_mm = st("tensor", fn, deps)

                    def fn(e, sem, so=so, bank=bank):
                        return e.activation(atn[:, so * P3T:(so + 1) * P3T],
                                            bank[:], AF.Identity)
                    at_ev.append(st("scalar", fn, [t_mm]))
                atev_last[it] = at_ev[-1]

                for stt in range(4):
                    g = it * 4 + stt

                    def cs(j, it=it, stt=stt):
                        return self.lnstats[:, it * 32 + stt * 8 + j:
                                            it * 32 + stt * 8 + j + 1]

                    def fn(e, sem, stt=stt):
                        for s in range(NS):
                            ins = e.transpose(
                                ps_t[:, s * 128:(s + 1) * 128],
                                atn[:, s * P3T + stt * 128:
                                    s * P3T + stt * 128 + 128],
                                self.ident[:])
                        return ins
                    t_tr = st("tensor", fn, [at_ev[-1], pst_free])

                    def fn(e, sem):
                        for s in range(NS):
                            ins = e.activation(atT[:, s * 128:(s + 1) * 128],
                                               ps_t[:, s * 128:(s + 1) * 128],
                                               AF.Identity)
                        return ins
                    t_atT = st("scalar", fn, [t_tr])
                    pst_free = t_atT

                    def fn(e, sem, cs=cs):
                        e.tensor_scalar(cols[:, 0:1], cs(3), 1.0, None,
                                        ALU.add)
                        return e.tensor_tensor(cols[:, 1:2], cs(1), cs(3),
                                               ALU.mult)
                    t_ab = st("vector", fn, [t_xld, t_init])

                    def fn(e, sem):
                        return e.activation(cols[:, 2:4], cols[:, 0:2],
                                            AF.Identity)
                    t_abh = st("scalar", fn, [t_ab])

                    def fn(e, sem, stt=stt):
                        xa = xrow[:, stt * DM:(stt + 1) * DM]
                        x1a = x1r[:, stt * DM:(stt + 1) * DM]
                        e.scalar_tensor_tensor(x1a, xa, cols[:, 2:3], atT[:],
                                               ALU.mult, ALU.add)
                        return e.tensor_single_scalar(x1a, x1a, cols[:, 3:4],
                                                      ALU.subtract)
                    t_x1 = st("vector", fn, [t_abh, t_atT, sp_prev[stt]])
                    if stt == 3:
                        x1_last[it] = t_x1

                    def fn(e, sem, stt=stt):
                        return e.reduce_sum(cols[:, 4:5],
                                            x1r[:, stt * DM:(stt + 1) * DM],
                                            axis=AX.X)
                    t_s1 = st("vector", fn, [t_x1])

                    def fn(e, sem):
                        return e.activation(cols[:, 5:6], cols[:, 4:5],
                                            AF.Identity, scale=1.0 / DM)
                    t_s2 = st("scalar", fn, [t_s1])

                    def fn(e, sem, stt=stt):
                        x1a = x1r[:, stt * DM:(stt + 1) * DM]
                        e.tensor_single_scalar(scr[:], x1a, cols[:, 5:6],
                                               ALU.subtract)
                        e.scalar_tensor_tensor(scr2[:], scr[:], 1.0, scr[:],
                                               ALU.mult, ALU.mult,
                                               accum_out=cols[:, 6:7])
                        return e.tensor_reduce(cols[:, 7:8], scr[:], AX.X,
                                               ALU.max,
                                               apply_absolute_value=True)
                    t_s3 = st("vector", fn, [t_s2] + lad_prev)

                    def fn(e, sem):
                        e.activation(cols[:, 8:9], cols[:, 6:7], AF.Ln,
                                     bias=EPS, scale=1.0 / DM)
                        e.activation(cols[:, 13:14], cols[:, 7:8],
                                     AF.Identity)
                        return e.activation(cols[:, 8:9], cols[:, 8:9],
                                            AF.Exp, scale=-0.5)
                    t_rs2 = st("scalar", fn, [t_s3])

                    def fn(e, sem):
                        return e.tensor_tensor(cols[:, 9:10], cols[:, 13:14],
                                               cols[:, 8:9], ALU.mult)
                    t_mx = st("vector", fn, [t_rs2])

                    def fn(e, sem):
                        return e.activation(cols[:, 10:11], cols[:, 9:10],
                                            AF.Identity)
                    t_hop = st("scalar", fn, [t_mx])

                    def fn(e, sem):
                        return e.tensor_scalar(cols[:, 11:12], cols[:, 10:11],
                                               1e-5, None, ALU.max)
                    t_clp = st("vector", fn, [t_hop])

                    def fn(e, sem):
                        return e.activation(cols[:, 12:13], cols[:, 11:12],
                                            AF.Identity)
                    t_m2h = st("scalar", fn, [t_clp])

                    def fn(e, sem):
                        e.tensor_copy(m2c[:], cols[:, 12:13])
                        return e.reciprocal(cols[:, 14:15], cols[:, 12:13])
                    t_rc = st("vector", fn, [t_m2h, hdma_prev])

                    def fn(e, sem, g=g):
                        e.dma_start(out=self.hsrow[0:1, g * 128:(g + 1) * 128],
                                    in_=m2c[:]).then_inc(sem, 16)
                    hdma_prev = st("gpsimd", fn, [t_rc], dma=1)

                    def fn(e, sem):
                        return e.activation(cols[:, 15:16], cols[:, 14:15],
                                            AF.Identity)
                    t_rch = st("scalar", fn, [t_rc])

                    def fn(e, sem):
                        return e.scalar_tensor_tensor(
                            cols[:, 4:5], cols[:, 15:16], 127.0,
                            cols[:, 8:9], ALU.mult, ALU.mult)
                    t_sx = st("vector", fn, [t_rch])

                    def fn(e, sem):
                        return e.activation(cols[:, 5:6], cols[:, 4:5],
                                            AF.Identity)
                    t_sxh = st("scalar", fn, [t_sx] + lad_prev)

                    def fn(e, sem):
                        e.tensor_scalar(scr2[:, 0:512], scr[:, 0:512],
                                        cols[:, 5:6], 128.5, ALU.mult,
                                        ALU.add)
                        return e.tensor_scalar(scr2[:, 512:DM],
                                               scr[:, 512:DM],
                                               cols[:, 5:6], 128.5,
                                               ALU.mult, ALU.add)
                    t_pre = st("vector", fn, [t_sxh])

                    def fn(e, sem):
                        e.tensor_copy(ps_e[:], scr2[:, 0:512])
                        for kb in [7, 6, 5, 4, 3, 2, 1, 0]:
                            e.tensor_scalar(bitV[:], ps_e[:], float(2 ** kb),
                                            -float(2 ** kb), ALU.is_ge,
                                            ALU.mult)
                            e.tensor_tensor(ps_e[:], ps_e[:], bitV[:],
                                            ALU.add)
                        e.tensor_tensor(xqr[:, 0:512], scr2[:, 0:512],
                                        ps_e[:], ALU.subtract)
                        return e.tensor_scalar(xqr[:, 0:512], xqr[:, 0:512],
                                               -128.0, None, ALU.add)
                    ladV = st("vector", fn, [t_pre, sp_last])

                    def fn(e, sem):
                        e.tensor_copy(rG[:], scr2[:, 512:DM])
                        for kb in [7, 6, 5, 4, 3, 2, 1, 0]:
                            e.tensor_scalar(bitG[:], rG[:], float(2 ** kb),
                                            -float(2 ** kb), ALU.is_ge,
                                            ALU.mult)
                            e.tensor_tensor(rG[:], rG[:], bitG[:], ALU.add)
                        e.tensor_tensor(xqr[:, 512:DM], scr2[:, 512:DM],
                                        rG[:], ALU.subtract)
                        return e.tensor_scalar(xqr[:, 512:DM],
                                               xqr[:, 512:DM],
                                               -128.0, None, ALU.add)
                    ladG = st("vector", fn, [t_pre, sp_last])
                    lad_prev = [ladV, ladG]
                    if getattr(self, "dbg_p3", False) and it == 0 and stt == 0:
                        def fn(e, sem):
                            e.dma_start(out=_ap(d["DS"], 0,
                                                [[NS * T, 128], [1, 16]]),
                                        in_=cols[:]).then_inc(sem, 16)
                            e.dma_start(out=_ap(d["DS"], 16,
                                                [[NS * T, 128], [1, DM]]),
                                        in_=scr[:]).then_inc(sem, 16)
                            e.dma_start(out=_ap(d["DS"], 1040,
                                                [[NS * T, 128], [1, DM]]),
                                        in_=scr2[:]).then_inc(sem, 16)
                            e.dma_start(out=_ap(d["DS"], 2064,
                                                [[NS * T, 128], [1, DM]]),
                                        in_=xqr[:]).then_inc(sem, 16)
                        dtok = st("gpsimd", fn, [ladV, ladG], dma=4)
                        lad_prev = [ladV, ladG, dtok]
                        sp_last = dtok

                    def fn(e, sem):
                        for s in range(NS):
                            ins = e.transpose(ps_t[:, s * 128:(s + 1) * 128],
                                              xqr[:, s * 128:(s + 1) * 128],
                                              self.ident[:])
                        return ins
                    t_qtr = st("tensor", fn, [ladV, ladG, pst_free])

                    def fn(e, sem):
                        for s in range(NS):
                            ins = e.activation(xqr[:, s * 128:(s + 1) * 128],
                                               ps_t[:, s * 128:(s + 1) * 128],
                                               AF.Identity)
                        return ins
                    t_qev = st("scalar", fn, [t_qtr])
                    pst_free = t_qev

                    def fn(e, sem, g=g, stt=stt):
                        e.dma_start(out=_ap(d["xqS"], g * DM,
                                            [[NS * T, 128], [1, DM]]),
                                    in_=xqr[:]).then_inc(sem, 16)
                        e.dma_start(out=_ap(d["x1S"], g * DM,
                                            [[NS * T, 128], [1, DM]]),
                                    in_=x1r[:, stt * DM:(stt + 1) * DM]
                                    ).then_inc(sem, 16)
                    t_sp = st("gpsimd", fn, [t_qev], dma=2)
                    sp_prev[stt] = t_sp
                    sp_last = t_sp

            while nextload < NCHW:
                passA_load(nextload)
                nextload += 1
            while nextred < NCHW:
                passA_red(nextred)
                nextred += 1
            wq_hop = {}
            prev_tot = None
            for wi in range(2):
                ac = 8 * wi

                def fn(e, sem, ac=ac):
                    return e.activation(self.wq_s[:, ac + 1:ac + 2],
                                        self.wq_s[:, ac:ac + 1], AF.Identity)
                t_ah = st("scalar", fn, [pa_acc[8 * wi + 7], prev_tot])

                def fn(e, sem, ac=ac):
                    return e.activation(pab[:, 0:1],
                                        self.wq_s[:, ac + 1:ac + 2],
                                        AF.Identity)
                t_hi = st("scalar", fn, [t_ah])

                def fn(e, sem, ac=ac):
                    return e.tensor_tensor(pab[:, 1:2],
                                           self.wq_s[:, ac + 1:ac + 2],
                                           pab[:, 0:1], ALU.subtract)
                t_lo = st("vector", fn, [t_hi])

                def fn(e, sem):
                    return e.matmul(ps_c[0:1, 0:2], onesc[:, 0:1],
                                    pab[:, 0:2], start=True, stop=True)
                t_tot = st("tensor", fn, [t_lo, t_init])

                def fn(e, sem):
                    return e.tensor_copy(cols[0:1, 12:14], ps_c[0:1, 0:2])
                t_tc = st("vector", fn, [t_tot])

                def fn(e, sem):
                    return e.activation(cols[0:1, 14:16], cols[0:1, 12:14],
                                        AF.Identity)
                t_tch = st("scalar", fn, [t_tc])

                def fn(e, sem, ac=ac):
                    return e.tensor_tensor(self.wq_s[0:1, ac + 2:ac + 3],
                                           cols[0:1, 14:15],
                                           cols[0:1, 15:16], ALU.add)
                t_sum = st("vector", fn, [t_tch])

                def fn(e, sem, ac=ac):
                    return e.activation(cols[0:1, 13:14],
                                        self.wq_s[0:1, ac + 2:ac + 3],
                                        AF.Identity)
                t_sh = st("scalar", fn, [t_sum])

                def fn(e, sem, ac=ac):
                    return e.tensor_scalar(self.wq_s[0:1, ac + 3:ac + 4],
                                           cols[0:1, 13:14],
                                           1.0 / (DM * DHID), 1e-5,
                                           ALU.mult, ALU.max)
                t_mc = st("vector", fn, [t_sh])

                def fn(e, sem, ac=ac, wi=wi):
                    e.activation(pabr[0:1, 4 * wi:4 * wi + 1],
                                 self.wq_s[0:1, ac + 3:ac + 4], AF.Identity)
                    return e.activation(cols[0:1, 14:15],
                                        self.wq_s[0:1, ac + 3:ac + 4],
                                        AF.Identity)
                t_mch = st("scalar", fn, [t_mc])

                def fn(e, sem, wi=wi):
                    e.tensor_tensor(pabr[0:1, 4 * wi + 1:4 * wi + 2],
                                    cols[0:1, 14:15],
                                    pabr[0:1, 4 * wi:4 * wi + 1],
                                    ALU.subtract)
                    return e.reciprocal(cols[0:1, 11:12], cols[0:1, 14:15])
                t_rc2 = st("vector", fn, [t_mch])

                def fn(e, sem, wi=wi):
                    e.activation(cols[0:1, 10:11], cols[0:1, 11:12],
                                 AF.Identity)
                    return e.activation(pabr[0:1, 4 * wi + 2:4 * wi + 3],
                                        cols[0:1, 11:12], AF.Identity)
                t_swh = st("scalar", fn, [t_rc2])

                def fn(e, sem, wi=wi):
                    return e.tensor_tensor(pabr[0:1, 4 * wi + 3:4 * wi + 4],
                                           cols[0:1, 10:11],
                                           pabr[0:1, 4 * wi + 2:4 * wi + 3],
                                           ALU.subtract)
                t_swlo = st("vector", fn, [t_swh])

                def fn(e, sem, wi=wi):
                    return e.matmul(ps_c[0:128, 4 + 4 * wi:8 + 4 * wi],
                                    ones1[0:1, 0:128],
                                    pabr[0:1, 4 * wi:4 * wi + 4],
                                    start=True, stop=True)
                t_bc = st("tensor", fn, [t_swlo])

                def fn(e, sem, wi=wi):
                    b = 4 + 4 * wi
                    return e.tensor_copy(cols[:, 8:12], ps_c[0:128, b:b + 4])
                t_bcv = st("vector", fn, [t_bc])

                def fn(e, sem):
                    return e.activation(cols[:, 12:16], cols[:, 8:12],
                                        AF.Identity)
                t_bch = st("scalar", fn, [t_bcv])

                def fn(e, sem, ac=ac):
                    e.tensor_tensor(self.wq_s[:, ac + 4:ac + 5],
                                    cols[:, 12:13], cols[:, 13:14], ALU.add)
                    return e.tensor_tensor(self.wq_s[:, ac + 5:ac + 6],
                                           cols[:, 14:15], cols[:, 15:16],
                                           ALU.add)
                prev_tot = st("vector", fn, [t_bch])

                def fn(e, sem, ac=ac):
                    return e.activation(self.wq_s[:, ac + 6:ac + 8],
                                        self.wq_s[:, ac + 4:ac + 6],
                                        AF.Identity)
                wq_hop[wi] = st("scalar", fn, [prev_tot])

            def fn(e, sem):
                return e.tensor_scalar(self.hsrow[0:1, :],
                                       self.hsrow[0:1, :],
                                       self.wq_s[0:1, 6:7], 1.0 / 127.0,
                                       ALU.mult, ALU.mult)
            t_hs = st("vector", fn, [wq_hop[0], hdma_prev])

            def fn(e, sem):
                return e.memset(pab[:], 0.0)
            st("vector", fn, [sp_last, hdma_prev, t_hs, wq_hop[1]])

            with nc.Block() as blk:
                sch.emit(blk)

    def phase4(self):
        nc, d = self.nc, self.d
        P4T = 128
        NCB = DHID // 128
        n4 = getattr(self, "n4", T // P4T)
        with ExitStack() as ctx:
            E = ctx.enter_context
            w1q = E(nc.sbuf_tensor("p4_w1q", [128, NS * DHID], BF16))
            w2q = E(nc.sbuf_tensor("p4_w2q", [128, NCB * DM], BF16))
            xqt = E(nc.sbuf_tensor("p4_xqt", [128, DM], BF16))
            ht = [E(nc.sbuf_tensor(f"p4_h{i}", [128, NCB * P4T], BF16))
                  for i in range(2)]
            mxt = E(nc.sbuf_tensor("p4_mxt", [128, 128], BF16))
            hLR = E(nc.sbuf_tensor("p4_hlr", [128, 2 * NCB], BF16))
            cwcol = E(nc.sbuf_tensor("p4_cw", [128, NCB * 4], F32))
            ctile = E(nc.sbuf_tensor("p4_c", [128, NCB * P4T], F32))
            rmx = E(nc.sbuf_tensor("p4_rmx", [128, 128], F32))
            rmx2 = E(nc.sbuf_tensor("p4_rmx2", [128, 128], F32))
            rmxb = E(nc.sbuf_tensor("p4_rmxb", [128, 256], BF16))
            s256 = E(nc.sbuf_tensor("p4_s256", [128, 256], F32))
            pcol = E(nc.sbuf_tensor("p4_pcol", [128, 8], F32))
            pcb = E(nc.sbuf_tensor("p4_pcb", [128, 3], BF16))
            prow = E(nc.sbuf_tensor("p4_prow", [1, 384], BF16))
            sxb = E(nc.sbuf_tensor("p4_sxb", [128, 128], F32))
            fsb = E(nc.sbuf_tensor("p4_fsb", [128, 128], BF16))
            bitV = E(nc.sbuf_tensor("p4_btv", [128, 512], BF16))
            bitG = E(nc.sbuf_tensor("p4_btg", [128, 512], BF16))
            rG = E(nc.sbuf_tensor("p4_rg", [128, 512], F32))
            hq = E(nc.sbuf_tensor("p4_hq", [128, NCB * P4T], BF16))
            fch = E(nc.sbuf_tensor("p4_fch", [128, NS * P4T], BF16))
            fT = E(nc.sbuf_tensor("p4_fT", [128, DM], BF16))
            x1t = E(nc.sbuf_tensor("p4_x1t", [128, DM], F32))
            ones1 = E(nc.sbuf_tensor("p4_ones1", [1, 128], BF16))
            ps_a = E(nc.psum_tensor("p4_pa", [128, 512], F32))
            ps_b = E(nc.psum_tensor("p4_pb", [128, 512], F32))
            ps_c = E(nc.psum_tensor("p4_pc", [128, 512], F32))
            ps_d = E(nc.psum_tensor("p4_pd", [128, 512], F32))
            ps_e = E(nc.psum_tensor("p4_pe", [128, 512], F32))
            ps_f = E(nc.psum_tensor("p4_pf", [128, 512], F32))
            ps_g = E(nc.psum_tensor("p4_pg", [128, 512], F32))
            ps_t = E(nc.psum_tensor("p4_pt", [128, 1024], BF16))
            sch = self._sched(ctx, "p4")
            self.p4_sched = sch
            import os as _os
            _lim = int(_os.environ.get("P4LIM", "-1"))
            _skip = set(int(t) for t in
                        _os.environ.get("P4SKIP", "").split(",") if t)
            _nstep = [0]

            def st(eng, fn, deps=(), dma=0):
                i = _nstep[0]
                _nstep[0] += 1
                if (_lim >= 0 and i >= _lim) or i in _skip:
                    if dma:
                        def fn2(e, sem, n=dma):
                            for _ in range(n):
                                e.dma_start(out=pcol[0:1, 0:2],
                                            in_=_ap(d["x"], 0,
                                                    [[2, 1], [1, 2]])
                                            ).then_inc(sem, 16)
                        return sch.step(eng, fn2, deps, dma=dma)

                    def fn3(e, sem):
                        return e.memset(pcol[0:1, 0:1], 0.0)
                    return sch.step("vector", fn3, deps)
                return sch.step(eng, fn, deps, dma=dma)

            def fn(e, sem):
                e.memset(hLR[:], 0.0)
                e.memset(prow[:], 0.0)
                e.memset(pcb[:], 0.0)
                e.memset(hq[:], 0.0)
                e.memset(ctile[:], 0.0)
                e.memset(fch[:], 0.0)
                e.memset(fT[:], 0.0)
                e.memset(rG[:], 0.0)
                e.memset(bitG[:], 0.0)
                return e.memset(ones1[:], 1.0)
            t_init = st("vector", fn)

            def fn(e, sem):
                for j in range(3):
                    e.dma_start(out=_ap(cwcol, j, [[NCB * 4, 128], [4, NCB]]),
                                in_=_ap(d["convw"], j,
                                        [[3, 128], [128 * 3, NCB]]),
                                allow_slow_non_contiguous=True
                                ).then_inc(sem, 16)
                e.dma_start(out=_ap(cwcol, 3, [[NCB * 4, 128], [4, NCB]]),
                            in_=_ap(d["convb"], 0, [[1, 128], [128, NCB]]),
                            allow_slow_non_contiguous=True
                            ).then_inc(sem, 16)
            t_cw = st("sync", fn, dma=4)

            qz = {}
            mlt = {}
            for ci in range(32):
                wi = ci // 16

                def fn(e, sem, ci=ci):
                    if ci < 16:
                        sk, hf = ci // 2, ci % 2
                        src = _ap(d["w1T"], sk * 128 * DHID + hf * 2048,
                                  [[DHID, 128], [1, 2048]])
                    else:
                        cb0 = (ci - 16) * 2
                        src = _ap(d["w2T"], cb0 * 128 * DM,
                                  [[DM, 128], [DM * 128, 2], [1, DM]])
                    e.dma_start(out=ctile[:, (ci % 2) * 2048:
                                          (ci % 2 + 1) * 2048],
                                in_=src).then_inc(sem, 16)
                t_ld = st(["sync", "gpsimd"][ci % 2], fn,
                          [qz.get(ci - 2), t_init], dma=1)

                def fn(e, sem, ci=ci, wi=wi):
                    sl = ctile[:, (ci % 2) * 2048:(ci % 2 + 1) * 2048]
                    return e.tensor_single_scalar(
                        sl, sl, self.wq_s[:, 8 * wi + 7:8 * wi + 8],
                        ALU.mult)
                mlt[ci] = st("vector", fn, [t_ld])

                qeng = "vector"
                tmp0 = 0

                def fn(e, sem, ci=ci, tmp0=tmp0):
                    sl = ctile[:, (ci % 2) * 2048:(ci % 2 + 1) * 2048]
                    if ci < 16:
                        sk, hf = ci // 2, ci % 2
                        dst = w1q[:, sk * DHID + hf * 2048:
                                  sk * DHID + hf * 2048 + 2048]
                    else:
                        cb0 = (ci - 16) * 2
                        dst = w2q[:, cb0 * DM:cb0 * DM + 2048]
                    tmp = hq[:, tmp0:tmp0 + 2048]
                    e.tensor_single_scalar(dst, sl, 0.5, ALU.is_ge)
                    e.tensor_single_scalar(tmp, sl, -0.5, ALU.is_lt)
                    return e.tensor_tensor(dst, dst, tmp, ALU.subtract)
                qz[ci] = st(qeng, fn, [mlt[ci]])
            wdone = [qz[15], qz[14], qz[31], qz[30], mlt[31]]

            h_tok = {}
            evac_last = {}
            mm1_last = {}
            mm2_last = {}
            conv_done = {}
            add_done = {}
            out_done = {}
            hbe_prev = None
            pev_prev = None
            fev_prev = None
            pbc_prev = None
            pst_free = None
            ftr_prev = None
            LV = 8
            for it in range(n4 + 1):
                if it < n4:
                    def fn(e, sem, it=it):
                        e.dma_start(out=xqt[:],
                                    in_=_ap(d["xqS"], it * DM,
                                            [[NS * T, 128], [1, DM]])
                                    ).then_inc(sem, 16)
                    t_xq = st("sync", fn, wdone + [mm1_last.get(it - 1)],
                              dma=1)

                    def fn(e, sem, it=it):
                        return e.matmul(
                            ps_g[0:128, 0:128], ones1[0:1, 0:128],
                            self.hsrow[0:1, it * P4T:(it + 1) * P4T],
                            start=True, stop=True)
                    t_hb = st("tensor", fn, [t_init, hbe_prev, t_xq])

                    def fn(e, sem):
                        return e.activation(mxt[:], ps_g[0:128, 0:128],
                                            AF.Identity)
                    t_hbe = st("scalar", fn, [t_hb, evac_last.get(it - 1)])
                    hbe_prev = t_hbe

                    hev = []
                    for gj in range(NCB // 4):
                        bank = [ps_a, ps_b][gj % 2]
                        deps = [t_xq, t_hbe]
                        if len(hev) >= 2:
                            deps.append(hev[-2])

                        def fn(e, sem, gj=gj, bank=bank):
                            for r4 in range(4):
                                cb = gj * 4 + r4
                                for sk in range(NS):
                                    ins = e.matmul(
                                        bank[:, r4 * 128:(r4 + 1) * 128],
                                        w1q[:, sk * DHID + cb * 128:
                                            sk * DHID + (cb + 1) * 128],
                                        xqt[:, sk * 128:(sk + 1) * 128],
                                        start=(sk == 0), stop=(sk == NS - 1))
                            return ins
                        t_m1 = st("tensor", fn, deps)
                        if gj == NCB // 4 - 1:
                            mm1_last[it] = t_m1

                        def fn(e, sem, it=it, gj=gj, bank=bank):
                            for r4 in range(4):
                                cb = gj * 4 + r4
                                ins = e.tensor_tensor(
                                    ht[it % 2][:, cb * P4T:(cb + 1) * P4T],
                                    bank[:, r4 * 128:(r4 + 1) * 128],
                                    mxt[:], ALU.mult)
                            return ins
                        hev.append(st("vector", fn, [t_m1]))
                    h_tok[it] = hev[-1]
                    evac_last[it] = hev[-1]

                ot = it - 1
                if ot < 0:
                    continue
                hi, hni = ot % 2, (ot + 1) % 2
                have_next = ot + 1 < n4

                deps = [h_tok[ot], conv_done.get(ot - 1), t_init, t_cw]
                if have_next:
                    deps.append(h_tok[ot + 1])
                if ot >= 1:
                    deps.append(mm2_last[ot - 1])

                def fn(e, sem, ot=ot, hi=hi, hni=hni, have_next=have_next):
                    for cb in range(NCB):
                        sl = slice(cb * P4T, (cb + 1) * P4T)
                        w0 = cwcol[:, cb * 4:cb * 4 + 1]
                        w1c = cwcol[:, cb * 4 + 1:cb * 4 + 2]
                        w2c = cwcol[:, cb * 4 + 2:cb * 4 + 3]
                        e.tensor_single_scalar(ctile[:, sl], ht[hi][:, sl],
                                               w1c, ALU.mult)
                        e.scalar_tensor_tensor(
                            _ap(ctile, cb * P4T + 1,
                                [[NCB * P4T, 128], [1, P4T - 1]]),
                            _ap(ht[hi], cb * P4T,
                                [[NCB * P4T, 128], [1, P4T - 1]]),
                            w0,
                            _ap(ctile, cb * P4T + 1,
                                [[NCB * P4T, 128], [1, P4T - 1]]),
                            ALU.mult, ALU.add)
                        e.scalar_tensor_tensor(
                            ctile[:, cb * P4T:cb * P4T + 1],
                            hLR[:, cb:cb + 1], w0,
                            ctile[:, cb * P4T:cb * P4T + 1],
                            ALU.mult, ALU.add)
                        e.scalar_tensor_tensor(
                            _ap(ctile, cb * P4T,
                                [[NCB * P4T, 128], [1, P4T - 1]]),
                            _ap(ht[hi], cb * P4T + 1,
                                [[NCB * P4T, 128], [1, P4T - 1]]),
                            w2c,
                            _ap(ctile, cb * P4T,
                                [[NCB * P4T, 128], [1, P4T - 1]]),
                            ALU.mult, ALU.add)
                        src_r = (ht[hni][:, cb * P4T:cb * P4T + 1]
                                 if have_next
                                 else hLR[:, NCB + cb:NCB + cb + 1])
                        e.scalar_tensor_tensor(
                            ctile[:, (cb + 1) * P4T - 1:(cb + 1) * P4T],
                            src_r, w2c,
                            ctile[:, (cb + 1) * P4T - 1:(cb + 1) * P4T],
                            ALU.mult, ALU.add)
                    for cb in range(NCB):
                        ins = e.tensor_copy(hLR[:, cb:cb + 1],
                                            ht[hi][:, (cb + 1) * P4T - 1:
                                                    (cb + 1) * P4T])
                    return ins
                t_conv = st("vector", fn, deps)
                conv_done[ot] = t_conv

                sil = []
                for half in range(2):
                    def fn(e, sem, half=half):
                        for cb in range(half * 16, half * 16 + 16):
                            sl = slice(cb * P4T, (cb + 1) * P4T)
                            ins = e.activation(
                                ctile[:, sl], ctile[:, sl], AF.Silu,
                                bias=cwcol[:, cb * 4 + 3:cb * 4 + 4],
                                scale=1.0)
                        return ins
                    sil.append(st("scalar", fn, [t_conv]))

                def fn(e, sem):
                    return e.tensor_reduce(
                        rmx[:], _ap(ctile, 0, [[NCB * P4T, 128], [1, P4T],
                                               [P4T, NCB]]),
                        AX.X, ALU.max, apply_absolute_value=True)
                t_rmxp = st("vector", fn, sil)

                def fn(e, sem):
                    return e.activation(rmx2[:], rmx[:], AF.Identity)
                t_rmxh = st("scalar", fn, [t_rmxp])

                def fn(e, sem):
                    e.tensor_copy(rmxb[:, 0:128], rmx2[:])
                    e.scalar_tensor_tensor(s256[:, 0:128], rmxb[:, 0:128],
                                           -1.0, rmx2[:], ALU.mult, ALU.add)
                    return e.tensor_copy(rmxb[:, 128:256], s256[:, 0:128])
                t_rhl = st("vector", fn, [t_rmxh])

                def fn(e, sem):
                    e.transpose(ps_t[:, 768:896], rmxb[:, 0:128],
                                self.ident[:])
                    return e.transpose(ps_t[:, 896:1024], rmxb[:, 128:256],
                                       self.ident[:])
                t_rtr = st("tensor", fn, [t_rhl, pst_free])

                def fn(e, sem):
                    return e.activation(s256[:], ps_t[:, 768:1024],
                                        AF.Identity)
                t_rev = st("scalar", fn, [t_rtr])
                pst_free = t_rev

                def fn(e, sem):
                    e.tensor_tensor(s256[:, 0:128], s256[:, 0:128],
                                    s256[:, 128:256], ALU.add)
                    return e.tensor_reduce(pcol[:, 0:1], s256[:, 0:128],
                                           AX.X, ALU.max)
                t_m2r = st("vector", fn, [t_rev])

                def fn(e, sem):
                    return e.activation(pcol[:, 5:6], pcol[:, 0:1],
                                        AF.Identity)
                t_m2rh = st("scalar", fn, [t_m2r])

                def fn(e, sem):
                    return e.tensor_scalar(pcol[:, 1:2], pcol[:, 5:6], 1e-5,
                                           None, ALU.max)
                t_c1 = st("vector", fn, [t_m2rh])

                def fn(e, sem):
                    return e.activation(pcol[:, 6:7], pcol[:, 1:2],
                                        AF.Identity)
                t_c2 = st("scalar", fn, [t_c1])

                def fn(e, sem):
                    e.reciprocal(pcol[:, 2:3], pcol[:, 6:7])
                    return e.tensor_scalar(pcol[:, 4:5], pcol[:, 6:7],
                                           self.wq_s[:, 14:15], 1.0 / 127.0,
                                           ALU.mult, ALU.mult)
                t_c3 = st("vector", fn, [t_c2])

                def fn(e, sem):
                    e.activation(pcb[:, 2:3], pcol[:, 4:5], AF.Identity)
                    return e.activation(pcol[:, 7:8], pcol[:, 2:3],
                                        AF.Identity)
                t_c4 = st("scalar", fn, [t_c3])

                def fn(e, sem):
                    return e.tensor_scalar(pcol[:, 3:4], pcol[:, 7:8], 127.0,
                                           None, ALU.mult)
                t_c5 = st("vector", fn, [t_c4])

                def fn(e, sem):
                    e.activation(pcol[:, 0:1], pcol[:, 3:4], AF.Identity)
                    return e.activation(pcb[:, 0:1], pcol[:, 3:4],
                                        AF.Identity)
                t_c6 = st("scalar", fn, [t_c5])

                def fn(e, sem):
                    return e.tensor_tensor(pcb[:, 1:2], pcol[:, 0:1],
                                           pcb[:, 0:1], ALU.subtract)
                t_m2h = st("vector", fn, [t_c6])

                def fn(e, sem, ot=ot):
                    for j in range(3):
                        e.dma_start(out=prow[0:1, j * 128:(j + 1) * 128],
                                    in_=pcb[:, j:j + 1]).then_inc(sem, 16)
                t_pd = st("gpsimd", fn, [t_m2h, t_c4, pbc_prev], dma=3)

                def fn(e, sem):
                    e.matmul(ps_c[0:128, 256:512], ones1[0:1, 0:128],
                             prow[0:1, 0:256], start=True, stop=True)
                    return e.matmul(ps_c[0:128, 0:128], ones1[0:1, 0:128],
                                    prow[0:1, 256:384], start=True,
                                    stop=True)
                t_pbc = st("tensor", fn, [t_pd, pev_prev, fev_prev])
                pbc_prev = t_pbc

                def fn(e, sem):
                    e.tensor_copy(s256[:], ps_c[0:128, 256:512])
                    return e.tensor_tensor(sxb[:], s256[:, 0:128],
                                           s256[:, 128:256], ALU.add)
                t_pev = st("vector", fn, [t_pbc])
                pev_prev = t_pev

                def fn(e, sem):
                    return e.tensor_copy(fsb[:], ps_c[0:128, 0:128])
                t_fev = st("vector", fn, [t_pbc])
                fev_prev = t_fev

                def fn(e, sem):
                    for cb in range(LV):
                        sl = slice(cb * P4T, (cb + 1) * P4T)
                        e.tensor_tensor(ctile[:, sl], ctile[:, sl], sxb[:],
                                        ALU.mult)
                    return e.tensor_scalar(ctile[:, 0:LV * P4T],
                                          ctile[:, 0:LV * P4T], 128.5, None,
                                          ALU.add)
                t_preV = st("vector", fn, [t_pev] + sil)

                def fn(e, sem):
                    for cb in range(LV, NCB):
                        sl = slice(cb * P4T, (cb + 1) * P4T)
                        e.tensor_tensor(ctile[:, sl], ctile[:, sl], sxb[:],
                                        ALU.mult)
                    return e.tensor_scalar(ctile[:, LV * P4T:],
                                          ctile[:, LV * P4T:], 128.5, None,
                                          ALU.add)
                t_preG = st("vector", fn, [t_pev] + sil)

                lad_deps_extra = [mm2_last.get(ot - 1)]

                def fn(e, sem):
                    for q in range(LV * P4T // 512):
                        sl = slice(q * 512, (q + 1) * 512)
                        e.tensor_copy(ps_e[:], ctile[:, sl])
                        for kb in [7, 6, 5, 4, 3, 2, 1, 0]:
                            e.tensor_scalar(bitV[:], ps_e[:], float(2 ** kb),
                                            -float(2 ** kb), ALU.is_ge,
                                            ALU.mult)
                            e.tensor_tensor(ps_e[:], ps_e[:], bitV[:],
                                            ALU.add)
                        e.tensor_tensor(hq[:, sl], ctile[:, sl], ps_e[:],
                                        ALU.subtract)
                        ins = e.tensor_scalar(hq[:, sl], hq[:, sl], -128.0,
                                              None, ALU.add)
                    return ins
                ladV = st("vector", fn, [t_preV] + lad_deps_extra)

                def fn(e, sem):
                    for q in range(LV * P4T // 512, NCB * P4T // 512):
                        sl = slice(q * 512, (q + 1) * 512)
                        e.tensor_copy(rG[:], ctile[:, sl])
                        for kb in [7, 6, 5, 4, 3, 2, 1, 0]:
                            e.tensor_scalar(bitG[:], rG[:], float(2 ** kb),
                                            -float(2 ** kb), ALU.is_ge,
                                            ALU.mult)
                            e.tensor_tensor(rG[:], rG[:], bitG[:], ALU.add)
                        e.tensor_tensor(hq[:, sl], ctile[:, sl], rG[:],
                                        ALU.subtract)
                        ins = e.tensor_scalar(hq[:, sl], hq[:, sl], -128.0,
                                              None, ALU.add)
                    return ins
                ladG = st("vector", fn, [t_preG] + lad_deps_extra)

                m2ev = []
                for so in range(NS):
                    bank2 = [ps_d, ps_f][so % 2]
                    reg = (so // 2) % 2
                    deps = [ladV, ladG] + wdone
                    if len(m2ev) >= 2:
                        deps.append(m2ev[-2])
                    if so == 0:
                        deps.append(ftr_prev)

                    def fn(e, sem, so=so, reg=reg, bank2=bank2):
                        for cb in range(NCB):
                            ins = e.matmul(
                                bank2[:, reg * 128:(reg + 1) * 128],
                                w2q[:, cb * DM + so * 128:
                                    cb * DM + (so + 1) * 128],
                                hq[:, cb * P4T:(cb + 1) * P4T],
                                start=(cb == 0), stop=(cb == NCB - 1))
                        return ins
                    t_m2 = st("tensor", fn, deps)
                    if so == NS - 1:
                        mm2_last[ot] = t_m2

                    def fn(e, sem, so=so, reg=reg, bank2=bank2):
                        return e.tensor_tensor(
                            fch[:, so * P4T:(so + 1) * P4T],
                            bank2[:, reg * 128:(reg + 1) * 128], fsb[:],
                            ALU.mult)
                    m2ev.append(st("vector", fn, [t_m2, t_fev]))

                def fn(e, sem, ot=ot):
                    e.dma_start(out=x1t[:],
                                in_=_ap(d["x1S"], ot * DM,
                                        [[NS * T, 128], [1, DM]])
                                ).then_inc(sem, 16)
                t_x1l = st("sync", fn, [add_done.get(ot - 1)], dma=1)

                def fn(e, sem):
                    for s in range(NS):
                        ins = e.transpose(ps_t[:, s * 128:(s + 1) * 128],
                                          fch[:, s * P4T:(s + 1) * P4T],
                                          self.ident[:])
                    return ins
                t_ftr = st("tensor", fn, [m2ev[-1], pst_free])
                ftr_prev = t_ftr

                def fn(e, sem):
                    for s in range(NS):
                        ins = e.activation(fT[:, s * 128:(s + 1) * 128],
                                           ps_t[:, s * 128:(s + 1) * 128],
                                           AF.Identity)
                    return ins
                t_fte = st("scalar", fn, [t_ftr, add_done.get(ot - 1)])
                pst_free = t_fte

                def fn(e, sem):
                    return e.tensor_tensor(x1t[:], x1t[:], fT[:], ALU.add)
                t_add = st("vector", fn, [t_fte, t_x1l])
                add_done[ot] = t_add

                def fn(e, sem, ot=ot):
                    e.dma_start(out=_ap(d["out"], ot * P4T * DM,
                                        [[DM, 128], [1, DM]]),
                                in_=x1t[:]).then_inc(sem, 16)
                out_done[ot] = st("sync", fn, [t_add], dma=1)

            def fn(e, sem):
                return e.memset(pcol[:], 0.0)
            st("vector", fn, [out_done[n4 - 1]])

            with nc.Block() as blk:
                sch.emit(blk)


def _host_constants():
    ident = np.eye(128, dtype=np.float32)
    mask_nsr = np.zeros((128, 128), np.float32)
    mask_pq = np.zeros((128, 320), np.float32)
    for s in range(64):
        for t in range(64):
            st_ = 1.0 if s < t else 0.0
            inc = 1.0 if s <= t else 0.0
            mask_pq[s, 0 + t] = -st_
            mask_pq[s, 64 + t] = inc
            mask_pq[s, 128 + t] = st_
            mask_pq[s, 192 + t] = -st_
            mask_pq[s, 256 + t] = inc
    mask_pq[64:128] = mask_pq[0:64]
    hm_sq = np.zeros((128, NS * 16), np.float32)
    hm_bc = np.zeros((16, NS * 128), np.float32)
    hm_bc2 = np.zeros((48, NS * 128), np.float32)
    for s in range(NS):
        for r in range(128):
            h = (s * 128 + r) // DH
            hm_sq[r, s * 16 + h] = 1.0
            hm_bc[h, s * 128 + r] = 1.0
            hm_bc2[32 + h, s * 128 + r] = 1.0
    return ident, mask_nsr, mask_pq, hm_sq, hm_bc, hm_bc2


def _pack_wbg(wb, wg):
    out = np.zeros((1024, 48), np.float32)
    out[:, 0:16] = wb.T
    out[:, 32:48] = wg.T
    return np.ascontiguousarray(out)


def _pack_bbg(bb, bg):
    out = np.zeros(48, np.float32)
    out[0:16] = bb
    out[32:48] = bg
    return out


def make_inputs(full, core):
    f32 = np.float32
    ident, mask_nsr, mask_pq, hm_sq, hm_bc, hm_bc2 = _host_constants()
    import ml_dtypes
    bf = ml_dtypes.bfloat16
    g = lambda n: np.ascontiguousarray(np.asarray(full[n], f32))
    return {
        "x": np.ascontiguousarray(np.asarray(full["L0_attn_x"][core], f32)),
        "wqT": np.ascontiguousarray(g("L0_attn_Wq").T),
        "wkT": np.ascontiguousarray(g("L0_attn_Wk").T),
        "wvT": np.ascontiguousarray(g("L0_attn_Wv").T),
        "waT": np.ascontiguousarray(g("L0_attn_Wa").T),
        "waRT": np.ascontiguousarray(g("L0_attn_WaR").T),
        "woT": np.ascontiguousarray(g("L0_attn_Wo").T),
        "wbgT": _pack_wbg(g("L0_attn_Wb"), g("L0_attn_Wg")),
        "w1T": np.ascontiguousarray(g("L0_ffn_w1").T),
        "w2T": np.ascontiguousarray(g("L0_ffn_w2").T),
        "ba": g("L0_attn_ba"), "baR": g("L0_attn_baR"),
        "bbg": _pack_bbg(g("L0_attn_bb"), g("L0_attn_bg")),
        "lnw": g("L0_attn_ln_w"), "lnb": g("L0_attn_ln_b"),
        "convw": np.ascontiguousarray(g("L0_ffn_conv_w")[:, 0, :]),
        "convb": g("L0_ffn_conv_b"),
        "ident": ident.astype(bf), "identf": ident,
        "mask_nsr": mask_nsr.astype(bf), "mask_pq": mask_pq.astype(bf),
        "hmap_sq": hm_sq.astype(bf), "hmap_bc": hm_bc.astype(bf),
        "hmap_bc2": hm_bc2.astype(bf),
    }


def kernel(**inputs):
    mk = MK()
    nc = mk.build()
    in_maps = [make_inputs(inputs, core) for core in range(8)]
    res = run_bass_kernel_spmd(nc, in_maps, core_ids=list(range(8)))
    out = np.stack([r["out"] for r in res.results], axis=0)
    return out.astype(np.float32)


# version 04 (final)
# speedup vs baseline: 1.4724x; 1.4724x over previous
import numpy as np
from contextlib import ExitStack
import concourse.bass as bass
import concourse.mybir as mybir
from concourse.bass_utils import run_bass_kernel_spmd

F32 = mybir.dt.float32
BF16 = mybir.dt.bfloat16
AF = mybir.ActivationFunctionType
ALU = mybir.AluOpType
AX = mybir.AxisListType

T, DM, H, DH, NS = 4096, 1024, 16, 64, 8
C, NCH = 64, 64
GRP, GT, NG = 2, 128, 32
TILE, NT = 512, 8
FLOOR = 5.0
DHID = 4096
EPS = 1e-5

CONSTS = [EPS, 1.0 / DM, -0.5, 0.5, 2.0, -2.0, 127.0, 128.5, -128.0,
          1e12, -1e12, 1.0 / 127.0]

SKIP_SAME_D = True
ENGINES = ("sync", "gpsimd", "vector", "scalar", "tensor")


def _ap(t, off, dims):
    return bass.AP(t, off, dims)


class Sched:

    def __init__(self, nc, sems, dsems):
        self.nc = nc
        self.sems = sems
        self.dsems = dsems
        self.count = {e: 0 for e in sems}
        self.dcount = {e: 0 for e in sems}
        self.steps = {e: [] for e in sems}

    def step(self, eng, fn, deps=(), dma=0):
        flat = []
        for t in deps:
            if t is None:
                continue
            if isinstance(t, list):
                flat.extend(x for x in t if x is not None)
            else:
                flat.append(t)
        deps = tuple(flat)
        if dma:
            self.dcount[eng] += 16 * dma
            tok = (eng, "d", self.dcount[eng])
        else:
            self.count[eng] += 1
            tok = (eng, "c", self.count[eng])
        self.steps[eng].append((fn, deps, dma))
        return tok

    def emit(self, block):
        for eng_name in ENGINES:
            steps = self.steps[eng_name]
            if not steps:
                continue

            def body(e, steps=steps, eng_name=eng_name):
                for fn, deps, dma in steps:
                    for (deng, kind, dval) in deps:
                        if deng == eng_name and kind == "c":
                            continue
                        if deng == eng_name and kind == "d" and SKIP_SAME_D:
                            continue
                        sem = (self.dsems if kind == "d" else self.sems)[deng]
                        e.wait_ge(sem, dval)
                    last = fn(e, (self.dsems if dma else self.sems)[eng_name])
                    if not dma:
                        assert last is not None, "step fn must return instr"
                        last.then_inc(self.sems[eng_name], 1)
            getattr(block, eng_name)(body)


def lnwb_col_ap(mk, j):
    return mk.lnwb_col[:, j:j + 1]


class MK:
    def __init__(self, debug=(), p1_tiles=NT, phases=(1, 2, 3, 4)):
        self.p1_tiles = p1_tiles
        self.phases = set(phases)
        self.debug = set(debug)
        nc = self.nc = bass.Bass("TRN2", target_bir_lowering=False)
        d = self.d = {}

        def din(name, shape, dt=F32):
            d[name] = nc.dram_tensor(name, shape, dt, kind="ExternalInput")

        def dsc(name, shape, dt):
            kind = "ExternalOutput" if name in self.debug else "Internal"
            d[name] = nc.dram_tensor(name, shape, dt, kind=kind)

        din("x", [T, DM])
        for w in ("wqT", "wkT", "wvT", "waT", "waRT", "woT"):
            din(w, [DM, DM])
        din("wbgT", [DM, 48])
        din("w1T", [DM, DHID])
        din("w2T", [DHID, DM])
        din("ba", [DM]); din("baR", [DM]); din("bbg", [48])
        din("lnw", [DM]); din("lnb", [DM])
        din("convw", [DHID, 3]); din("convb", [DHID])
        din("ident", [128, 128], BF16)
        din("identf", [128, 128], F32)
        din("mask_nsr", [128, 128], BF16)
        din("mask_pq", [128, 320], BF16)
        din("hmap_sq", [128, NS * 16], BF16)
        din("hmap_bc", [16, NS * 128], BF16)
        din("hmap_bc2", [48, NS * 128], BF16)

        dsc("knS", [128, NS * T], BF16)
        dsc("knbS", [128, NS * T], BF16)
        dsc("kngS", [128, NS * T], BF16)
        dsc("qS", [128, NS * T], BF16)
        dsc("vS", [128, NS * T], BF16)
        dsc("DS", [128, NS * T], F32)
        dsc("DRS", [128, NS * T], F32)
        dsc("OS", [128, NS * T], BF16)
        dsc("x1S", [128, NS * T], F32)
        dsc("xqS", [128, NS * T], BF16)
        dsc("bgS", [128, 2 * T // 4], F32)
        d["out"] = nc.dram_tensor("out", [T, DM], F32, kind="ExternalOutput")

    def _sched(self, ctx, tag):
        E = ctx.enter_context
        sems = {e: E(self.nc.semaphore(f"{tag}_c_{e}")) for e in ENGINES}
        dsems = {e: E(self.nc.semaphore(f"{tag}_d_{e}")) for e in ENGINES}
        return Sched(self.nc, sems, dsems)

    def build(self):
        nc, d = self.nc, self.d
        with ExitStack() as octx:
            Eo = octx.enter_context
            ident_sb = Eo(nc.sbuf_tensor("ident_sb", [128, 128], BF16))
            lnstats = Eo(nc.sbuf_tensor("lnstats", [128, NT * 4 * 8], F32))
            cpool = Eo(nc.sbuf_tensor("cpool", [128, len(CONSTS)], F32))
            self.ident, self.lnstats = ident_sb, lnstats
            for ci, cv in enumerate(CONSTS):
                nc.const_aps.register(F32, cv, cpool[:, ci:ci + 1])
            with ExitStack() as mctx:
                Em = mctx.enter_context
                identf_sb = Em(nc.sbuf_tensor("identf_sb", [128, 128], F32))
                bgrow = Em(nc.sbuf_tensor("bgrow", [128, 32 * 48], F32))
                nbgrow = Em(nc.sbuf_tensor("nbgrow", [128, 32 * 48], F32))
                ba_col = Em(nc.sbuf_tensor("ba_col", [128, 16], F32))
                lnwb_col = Em(nc.sbuf_tensor("lnwb_col", [128, 16], F32))
                bbg_col = Em(nc.sbuf_tensor("bbg_col", [48, 1], F32))
                hmsq_sb = Em(nc.sbuf_tensor("hmsq_sb", [128, NS * 16], BF16))
                hmbc_sb = Em(nc.sbuf_tensor("hmbc_sb", [16, NS * 128], BF16))
                hmbc2_sb = Em(nc.sbuf_tensor("hmbc2_sb", [48, NS * 128],
                                             BF16))
                mnsr_sb = Em(nc.sbuf_tensor("mnsr_sb", [128, 128], BF16))
                mpq_sb = Em(nc.sbuf_tensor("mpq_sb", [128, 320], BF16))
                bgrow_c = Em(nc.sbuf_tensor("bgrow_c", [128, NCH * 64], F32))
                c0 = Em(nc.semaphore("c0"))
                self.identf = identf_sb
                self.bgrow, self.nbgrow = bgrow, nbgrow
                self.ba_col, self.lnwb_col, self.bbg_col = \
                    ba_col, lnwb_col, bbg_col
                self.hmsq, self.hmbc, self.mnsr, self.mpq = \
                    hmsq_sb, hmbc_sb, mnsr_sb, mpq_sb
                self.hmbc2 = hmbc2_sb
                self.bgrow_c = bgrow_c

                with nc.Block() as blk:
                    @blk.gpsimd
                    def _(e):
                        for ci, cv in enumerate(CONSTS):
                            e.memset(cpool[:, ci:ci + 1], float(cv))
                        e.sem_inc(c0, 1)

                    @blk.sync
                    def _(e):
                        loads = [
                            (ident_sb[:], d["ident"].ap()),
                            (identf_sb[:], d["identf"].ap()),
                            (hmsq_sb[:], d["hmap_sq"].ap()),
                            (hmbc_sb[:], d["hmap_bc"].ap()),
                            (hmbc2_sb[:], d["hmap_bc2"].ap()),
                            (mnsr_sb[:], d["mask_nsr"].ap()),
                            (mpq_sb[:], d["mask_pq"].ap()),
                            (ba_col[:, 0:8],
                             _ap(d["ba"], 0, [[1, 128], [128, 8]])),
                            (ba_col[:, 8:16],
                             _ap(d["baR"], 0, [[1, 128], [128, 8]])),
                            (lnwb_col[:, 0:8],
                             _ap(d["lnw"], 0, [[1, 128], [128, 8]])),
                            (lnwb_col[:, 8:16],
                             _ap(d["lnb"], 0, [[1, 128], [128, 8]])),
                            (bbg_col[:], _ap(d["bbg"], 0, [[1, 48], [1, 1]])),
                        ]
                        for o, i in loads:
                            e.dma_start(out=o, in_=i,
                                        allow_slow_non_contiguous=True
                                        ).then_inc(c0, 16)
                        e.wait_ge(c0, 16 * len(loads) + 1)
                nc.all_engine_barrier()
                if 1 in self.phases:
                    with nc.named_scope("phase1"):
                        self.phase1()
                nc.all_engine_barrier()
                if 2 in self.phases:
                    with nc.named_scope("phase2"):
                        self.phase2()
                nc.all_engine_barrier()
            with ExitStack() as pctx:
                Ep = pctx.enter_context
                self.hsrow = Ep(nc.sbuf_tensor("hsrow", [1, T], BF16))
                self.wq_s = Ep(nc.sbuf_tensor("wq_s", [128, 16], F32))
                if 3 in self.phases:
                    with nc.named_scope("phase3"):
                        self.phase3()
                nc.all_engine_barrier()
                if 4 in self.phases:
                    with nc.named_scope("phase4"):
                        self.phase4()
        return nc

    def phase1(self):
        nc, d = self.nc, self.d
        with ExitStack() as ctx:
            E = ctx.enter_context
            xrow = E(nc.sbuf_tensor("p1_xrow", [128, 4 * DM], F32))
            scr = E(nc.sbuf_tensor("p1_scr", [128, DM], F32))
            xnp = E(nc.sbuf_tensor("p1_xnp", [128, 2 * DM], BF16))
            xnT = E(nc.sbuf_tensor("p1_xnT", [128, NS * TILE], BF16))
            wq_sb = E(nc.sbuf_tensor("p1_wq", [128, NS * DM], BF16))
            wk_sb = E(nc.sbuf_tensor("p1_wk", [128, NS * DM], BF16))
            wv_sb = E(nc.sbuf_tensor("p1_wv", [128, NS * DM], BF16))
            wa_sb = E(nc.sbuf_tensor("p1_wa", [128, NS * DM], BF16))
            waR_sb = E(nc.sbuf_tensor("p1_waR", [128, NS * DM], BF16))
            wbg_sb = E(nc.sbuf_tensor("p1_wbg", [128, NS * 48], BF16))
            ktile = E(nc.sbuf_tensor("p1_kt", [128, NS * TILE], BF16))
            qtile = E(nc.sbuf_tensor("p1_qt", [128, NS * TILE], BF16))
            vtile = E(nc.sbuf_tensor("p1_vt", [128, NS * TILE], BF16))
            ksqt = E(nc.sbuf_tensor("p1_ksq", [128, TILE], BF16))
            za_sb = E(nc.sbuf_tensor("p1_za", [128, TILE], F32))

            dtile = E(nc.sbuf_tensor("p1_dt", [128, NS * TILE], F32))

            e1 = E(nc.sbuf_tensor("p1_e1", [128, TILE], F32))
            e2 = E(nc.sbuf_tensor("p1_e2", [128, TILE], F32))
            rin = E(nc.sbuf_tensor("p1_rin", [16, TILE], BF16))
            bgt = E(nc.sbuf_tensor("p1_bg", [48, TILE], F32))
            bgb = E(nc.sbuf_tensor("p1_bgb", [48, TILE], BF16))
            ps_t = E(nc.psum_tensor("p1_pt", [128, 1024], BF16))
            ps_a = E(nc.psum_tensor("p1_pa", [128, 512], F32))
            ps_b = E(nc.psum_tensor("p1_pb", [128, 512], F32))
            ps_c = E(nc.psum_tensor("p1_pc", [128, 512], F32))
            ps_d = E(nc.psum_tensor("p1_pd", [128, 512], F32))
            ps_e = E(nc.psum_tensor("p1_pe", [128, 512], F32))
            sch = self._sched(ctx, "p1")
            st = sch.step

            wload = {}
            for wname, wsb, ncol in [("wqT", wq_sb, DM), ("wkT", wk_sb, DM),
                                     ("wvT", wv_sb, DM), ("waT", wa_sb, DM),
                                     ("waRT", waR_sb, DM), ("wbgT", wbg_sb, 48)]:
                def fn(e, sem, wname=wname, wsb=wsb, ncol=ncol):
                    for s in range(NS):
                        e.dma_start(
                            out=wsb[:, s * ncol:(s + 1) * ncol],
                            in_=_ap(d[wname], s * 128 * ncol,
                                    [[ncol, 128], [1, ncol]])).then_inc(sem, 16)
                wload[wname] = st("gpsimd", fn, dma=NS)

            xload = {}
            xnT_ready_prev = None
            spill = {}
            tile_done = {}
            prev = {"tbg_evac": None, "arin": None, "d_last": None,
                    "bg_evac1": None}
            for it in range(self.p1_tiles):
                xsl = lambda stt: xrow[:, stt * DM:(stt + 1) * DM]

                def cs(stt, j, it=it):
                    b = it * 32 + stt * 8 + j
                    return self.lnstats[:, b:b + 1]

                deps = [xnT_ready_prev] if it >= 1 else []

                def fn(e, sem, it=it):
                    for stt in range(4):
                        e.dma_start(
                            out=xrow[:, stt * DM:(stt + 1) * DM],
                            in_=_ap(d["x"], (it * TILE + stt * 128) * DM,
                                    [[DM, 128], [1, DM]]),
                        ).then_inc(sem, 16)
                xload[it] = st("sync", fn, deps, dma=4)

                def fn(e, sem, xsl=xsl, cs=cs):
                    for stt in range(4):
                        ins = e.reduce_sum(cs(stt, 0), xsl(stt), axis=AX.X)
                    return ins
                v1a = st("vector", fn, [xload[it]])

                def fn(e, sem, cs=cs):
                    for stt in range(4):
                        ins = e.activation(cs(stt, 1), cs(stt, 0), AF.Identity,
                                           scale=1.0 / DM)
                    return ins
                a1a = st("scalar", fn, [v1a])

                def fn(e, sem, xsl=xsl, cs=cs):
                    for stt in range(4):
                        xa = xsl(stt)
                        e.tensor_single_scalar(xa, xa, cs(stt, 1), ALU.subtract)
                        ins = e.scalar_tensor_tensor(scr[:], xa, 1.0, xa,
                                                     ALU.mult, ALU.mult,
                                                     accum_out=cs(stt, 2))
                    return ins
                v1 = st("vector", fn, [a1a])

                def fn(e, sem, cs=cs):
                    for stt in range(4):
                        e.activation(cs(stt, 3), cs(stt, 2), AF.Ln, bias=EPS,
                                     scale=1.0 / DM)
                        ins = e.activation(cs(stt, 3), cs(stt, 3), AF.Exp,
                                           scale=-0.5)
                    return ins
                a1 = st("scalar", fn, [v1])

                def fn(e, sem, xsl=xsl, cs=cs):
                    for stt in range(4):
                        xa = xsl(stt)
                        e.tensor_single_scalar(xa, xa, cs(stt, 3), ALU.mult)
                        ins = e.reduce_sum(cs(stt, 4), xa, axis=AX.X)
                    return ins
                v2a = st("vector", fn, [a1])

                def fn(e, sem, cs=cs):
                    for stt in range(4):
                        ins = e.activation(cs(stt, 7), cs(stt, 4), AF.Identity,
                                           scale=1.0 / DM)
                    return ins
                a2a = st("scalar", fn, [v2a])

                def fn(e, sem, xsl=xsl, cs=cs):
                    for stt in range(4):
                        xa = xsl(stt)
                        e.tensor_single_scalar(xa, xa, cs(stt, 7), ALU.subtract)
                        ins = e.scalar_tensor_tensor(scr[:], xa, 1.0, xa,
                                                     ALU.mult, ALU.mult,
                                                     accum_out=cs(stt, 5))
                    return ins
                v2 = st("vector", fn, [a2a])

                def fn(e, sem, cs=cs):
                    for stt in range(4):
                        e.activation(cs(stt, 6), cs(stt, 5), AF.Ln, bias=EPS,
                                     scale=1.0 / DM)
                        ins = e.activation(cs(stt, 6), cs(stt, 6), AF.Exp,
                                           scale=-0.5)
                    return ins
                a2 = st("scalar", fn, [v2])

                tr_done = []
                for stt in range(4):
                    vx_deps = [a2]
                    if stt >= 2:
                        vx_deps.append(tr_done[2 * (stt - 2) + 1])

                    def fn(e, sem, stt=stt, xsl=xsl, cs=cs):
                        return e.tensor_single_scalar(
                            xnp[:, (stt % 2) * DM:(stt % 2 + 1) * DM],
                            xsl(stt), cs(stt, 6), ALU.mult)
                    vx = st("vector", fn, vx_deps)

                    t_deps = [vx]
                    if stt == 0 and prev["tbg_evac"] is not None:
                        t_deps.append(prev["tbg_evac"])
                    if stt >= 1:
                        t_deps.append(tr_done[2 * (stt - 1) + 1])

                    def fn(e, sem, stt=stt):
                        for s in range(4):
                            ins = e.transpose(ps_t[:, s * 128:(s + 1) * 128],
                                              xnp[:, (stt % 2) * DM + s * 128:
                                                  (stt % 2) * DM + (s + 1) * 128],
                                              self.ident[:])
                        return ins
                    t1 = st("tensor", fn, t_deps)

                    def fn(e, sem, stt=stt):
                        for s in range(4):
                            ins = e.activation(
                                xnT[:, s * TILE + stt * 128:
                                    s * TILE + stt * 128 + 128],
                                ps_t[:, s * 128:(s + 1) * 128],
                                AF.Identity, bias=lnwb_col_ap(self, 8 + s),
                                scale=lnwb_col_ap(self, s))
                        return ins
                    ev1 = st("scalar", fn, [t1])
                    tr_done.append(ev1)

                    def fn(e, sem, stt=stt):
                        for s in range(4, NS):
                            ins = e.transpose(ps_t[:, (s - 4) * 128:(s - 3) * 128],
                                              xnp[:, (stt % 2) * DM + s * 128:
                                                  (stt % 2) * DM + (s + 1) * 128],
                                              self.ident[:])
                        return ins
                    t2 = st("tensor", fn, [ev1])

                    def fn(e, sem, stt=stt):
                        for s in range(4, NS):
                            ins = e.activation(
                                xnT[:, s * TILE + stt * 128:
                                    s * TILE + stt * 128 + 128],
                                ps_t[:, (s - 4) * 128:(s - 3) * 128],
                                AF.Identity, bias=lnwb_col_ap(self, 8 + s),
                                scale=lnwb_col_ap(self, s))
                        return ins
                    ev2 = st("scalar", fn, [t2])
                    tr_done.append(ev2)
                xnT_ready = tr_done[-1]
                xnT_ready_prev = xnT_ready

                bg_deps = [xnT_ready, wload["wbgT"]]
                if prev["arin"] is not None:
                    bg_deps.append(prev["arin"])

                def fn(e, sem):
                    for sk in range(NS):
                        ins = e.matmul(ps_d[0:48, 0:TILE],
                                       wbg_sb[:, sk * 48:(sk + 1) * 48],
                                       xnT[:, sk * TILE:(sk + 1) * TILE],
                                       start=(sk == 0), stop=(sk == NS - 1))
                    return ins
                tbg = st("tensor", fn, bg_deps)

                def fn(e, sem):
                    return e.tensor_single_scalar(bgt[:], ps_d[0:48, 0:TILE],
                                                  self.bbg_col[:], ALU.add)
                vbg1 = st("vector", fn, [tbg])

                def fn(e, sem):
                    return e.activation(bgt[:], bgt[:], AF.Exp, scale=-1.0)
                abg = st("scalar", fn, [vbg1])

                def fn(e, sem, it=it):
                    e.tensor_scalar(bgt[:], bgt[:], 1.0, None, ALU.add)
                    e.reciprocal(bgt[:], bgt[:])
                    return e.tensor_copy(bgb[:], bgt[:])
                vbg2 = st("vector", fn, [abg])

                def fn(e, sem):
                    for gi in range(4):
                        ins = e.transpose(
                            ps_t[:, 384 + gi * 48:384 + (gi + 1) * 48],
                            bgb[:, gi * 128:(gi + 1) * 128],
                            self.ident[0:48, 0:48])
                    return ins
                tbt = st("tensor", fn, [vbg2, xnT_ready])

                def fn(e, sem, it=it):
                    for gi in range(4):
                        grp = it * 4 + gi
                        e.tensor_copy(self.bgrow[:, grp * 48:(grp + 1) * 48],
                                      ps_t[:, 384 + gi * 48:384 + (gi + 1) * 48])
                    return e.tensor_scalar(
                        self.nbgrow[:, it * 192:(it + 1) * 192],
                        self.bgrow[:, it * 192:(it + 1) * 192],
                        -1.0, None, ALU.mult)
                bg_evac = st("vector", fn, [tbt])
                prev["tbg_evac"] = bg_evac

                def fn(e, sem, it=it):
                    for half, neg in ((0, 0), (1, 0), (0, 1), (1, 1)):
                        buf = self.bgrow if neg == 0 else self.nbgrow
                        for bg2 in range(2):
                            e.dma_start(
                                out=_ap(self.bgrow_c,
                                        (8 * it + half) * 64 + neg * 32
                                        + bg2 * 16,
                                        [[NCH * 64, 64], [128, 4], [1, 16]]),
                                in_=_ap(buf,
                                        half * 64 * (32 * 48) + it * 192
                                        + bg2 * 32,
                                        [[32 * 48, 64], [48, 4], [1, 16]]),
                                allow_slow_non_contiguous=True
                                ).then_inc(sem, 16)
                bgc_tok = st("gpsimd", fn, [bg_evac], dma=8)
                bgc_last = bgc_tok
                tile_done[it] = bgc_tok

                psl = [ps_a, ps_b, ps_c]
                pv_done = {}
                projs = [("q", wq_sb, qtile, "qS"), ("v", wv_sb, vtile, "vS"),
                         ("k", wk_sb, ktile, None),
                         ("za", wa_sb, None, None),
                         ("zaR", waR_sb, None, None)]
                wmap = {"q": "wqT", "v": "wvT", "k": "wkT", "za": "waT",
                        "zaR": "waRT"}
                d_prev = prev["d_last"]
                for ip, (pn, wsb, dst, spname) in enumerate(projs):
                    for so in range(NS):
                        idx = ip * NS + so
                        bank = psl[idx % 3]
                        deps = [xnT_ready, wload[wmap[pn]]]
                        if idx >= 3:
                            deps.append(pv_done[idx - 3])
                        if idx < 3 and it >= 1:
                            deps.append(tile_done[it - 1])
                        if so == 0 and it >= 1 and pn in ("q", "v", "k"):
                            deps.append(spill[(it - 1, {"q": "knbS",
                                                        "v": "kngS",
                                                        "k": "kngS"}[pn])])

                        def fn(e, sem, wsb=wsb, so=so, bank=bank):
                            for sk in range(NS):
                                ins = e.matmul(bank[:],
                                               wsb[:, sk * DM + so * 128:
                                                   sk * DM + (so + 1) * 128],
                                               xnT[:, sk * TILE:(sk + 1) * TILE],
                                               start=(sk == 0),
                                               stop=(sk == NS - 1))
                            return ins
                        tp = st("tensor", fn, deps)

                        if pn in ("q", "v", "k"):
                            def fn(e, sem, so=so, bank=bank, dst=dst):
                                return e.tensor_copy(
                                    _ap(dst, so * 128,
                                        [[NS * TILE, 128], [1024, 4],
                                         [1, 128]]), bank[:])
                            pv_done[idx] = st("vector", fn, [tp])
                        else:
                            dep2 = [tp]
                            if d_prev is not None:
                                dep2.append(d_prev)
                            if pn == "zaR" and so == 0 and \
                               (it, "DS") in spill:
                                dep2.append(spill[(it, "DS")])

                            def fn(e, sem, pn=pn, so=so, bank=bank):
                                col = self.ba_col[:, (0 if pn == "za" else 8) + so:
                                                  (1 if pn == "za" else 9) + so]
                                e.tensor_single_scalar(za_sb[:], bank[:], col,
                                                       ALU.add)
                                return e.tensor_scalar(e2[:], za_sb[:], -1.0,
                                                       0.0, ALU.mult, ALU.max)
                            vd1 = st("vector", fn, dep2)
                            pv_done[idx] = vd1

                            def fn(e, sem):
                                e.activation(e1[:], za_sb[:], AF.Abs)
                                e.activation(e1[:], e1[:], AF.Exp, scale=-1.0)
                                return e.activation(e1[:], e1[:], AF.Ln,
                                                    bias=1.0)
                            ad = st("scalar", fn, [vd1])

                            def fn(e, sem, so=so):
                                return e.tensor_tensor(
                                    _ap(dtile, so * 128,
                                        [[NS * TILE, 128], [1024, 4],
                                         [1, 128]]), e1[:], e2[:], ALU.add)
                            d_prev = st("vector", fn, [ad])

                    if spname:
                        def fn(e, sem, it=it, spname=spname, dst=dst):
                            e.dma_start(
                                out=_ap(d[spname], it * NS * TILE,
                                        [[NS * T, 128], [1, NS * TILE]]),
                                in_=dst[:]).then_inc(sem, 16)
                        spill[(it, spname)] = st("gpsimd", fn,
                                                 [pv_done[ip * NS + NS - 1]],
                                                 dma=1)
                    if pn in ("za", "zaR"):
                        spname2 = "DS" if pn == "za" else "DRS"

                        def fn(e, sem, it=it, spname2=spname2):
                            e.dma_start(
                                out=_ap(d[spname2], it * NS * TILE,
                                        [[NS * T, 128], [1, NS * TILE]]),
                                in_=dtile[:]).then_inc(sem, 16)
                        spill[(it, spname2)] = st("gpsimd", fn, [d_prev],
                                                  dma=1)
                prev["d_last"] = d_prev

                k_last = pv_done[2 * NS + NS - 1]
                prev_sq = None
                for sk in range(NS):
                    def fn(e, sem, sk=sk):
                        sl_ap = _ap(ktile, sk * 128,
                                    [[NS * TILE, 128], [1024, 4], [1, 128]])
                        return e.tensor_tensor(ksqt[:], sl_ap, sl_ap, ALU.mult)
                    vs = st("vector", fn, [k_last] + ([prev_sq] if prev_sq else []))

                    nd = [vs]
                    if sk == 0:
                        nd.append(vbg1)

                    def fn(e, sem, sk=sk):
                        return e.matmul(ps_d[0:16, 0:TILE],
                                        self.hmsq[:, sk * 16:(sk + 1) * 16],
                                        ksqt[:],
                                        start=(sk == 0), stop=(sk == NS - 1))
                    prev_sq = st("tensor", fn, nd)
                tnsq = prev_sq

                def fn(e, sem):
                    e.activation(rin[:], ps_d[0:16, 0:TILE], AF.Ln)
                    return e.activation(rin[:], rin[:], AF.Exp, scale=-0.5)
                arin = st("scalar", fn, [tnsq])
                prev["arin"] = arin

                def fn(e, sem):
                    return e.tensor_scalar(rin[:], rin[:], 1e12, None, ALU.min)
                vrin = st("vector", fn, [arin])

                prev_kn = None
                for so in range(NS):
                    def fn(e, sem, so=so):
                        return e.matmul(ps_e[:],
                                        self.hmbc[:, so * 128:(so + 1) * 128],
                                        rin[:], start=True, stop=True)
                    tb = st("tensor", fn,
                            [vrin] + ([prev_kn] if prev_kn else []))

                    def fn(e, sem, so=so):
                        sl_ap = _ap(ktile, so * 128,
                                    [[NS * TILE, 128], [1024, 4], [1, 128]])
                        return e.tensor_tensor(sl_ap, sl_ap,
                                               ps_e[:, 0:TILE], ALU.mult)
                    prev_kn = st("vector", fn, [tb])

                def fn(e, sem, it=it):
                    e.dma_start(
                        out=_ap(d["knS"], it * NS * TILE,
                                [[NS * T, 128], [1, NS * TILE]]),
                        in_=ktile[:]).then_inc(sem, 16)
                spill[(it, "knS")] = st("gpsimd", fn, [prev_kn], dma=1)

                prev_kb = None
                for dec, (dstb, scn) in enumerate(
                        [(qtile, "knbS"), (vtile, "kngS")]):
                    for so in range(NS):
                        deps = [prev_kn, vbg2]
                        if prev_kb is not None:
                            deps.append(prev_kb)
                        if so == 0:
                            deps.append(spill[(it, "qS" if dec == 0 else "vS")])

                        def fn(e, sem, so=so, dec=dec):
                            src = bgb[0:16, :] if dec == 0 else bgb[32:48, :]
                            lh = (self.hmbc[:, so * 128:(so + 1) * 128]
                                  if dec == 0 else
                                  self.hmbc2[32:48, so * 128:(so + 1) * 128])
                            return e.matmul(ps_e[:], lh, src,
                                            start=True, stop=True)
                        tb2 = st("tensor", fn, deps)

                        def fn(e, sem, so=so, dstb=dstb):
                            kap = _ap(ktile, so * 128,
                                      [[NS * TILE, 128], [1024, 4], [1, 128]])
                            oap = _ap(dstb, so * 128,
                                      [[NS * TILE, 128], [1024, 4], [1, 128]])
                            return e.tensor_tensor(oap, kap, ps_e[:, 0:TILE],
                                                   ALU.mult)
                        prev_kb = st("vector", fn, [tb2])

                    def fn(e, sem, it=it, scn=scn, dstb=dstb):
                        e.dma_start(
                            out=_ap(d[scn], it * NS * TILE,
                                    [[NS * T, 128], [1, NS * TILE]]),
                            in_=dstb[:]).then_inc(sem, 16)
                    spill[(it, scn)] = st("gpsimd", fn, [prev_kb], dma=1)


            def fn(e, sem):
                e.dma_start(out=self.bgrow_c[64:128, :],
                            in_=self.bgrow_c[0:64, :]).then_inc(sem, 16)
            st("gpsimd", fn, [prev["tbg_evac"], bgc_last], dma=1)

            with nc.Block() as blk:
                sch.emit(blk)

    def phase2(self):
        nc, d = self.nc, self.d
        NCHD = getattr(self, "nch_dev", NCH)
        NGD = (NCHD + GRP - 1) // GRP
        with ExitStack() as ctx:
            E = ctx.enter_context
            gb = {}
            for nm in ("kn", "knb", "kng", "q", "v"):
                gb[nm] = [E(nc.sbuf_tensor(f"p2_{nm}{i}", [128, NS * GT], BF16))
                          for i in range(2)]
            for nm in ("Dx", "DRx"):
                gb[nm] = [E(nc.sbuf_tensor(f"p2_{nm}{i}", [128, NS * GT], F32))
                          for i in range(2)]
            VAR_S = ["mvar", "qvar", "pvar", "gvar", "mcvar", "qcvar", "pcvar",
                     "scvar", "kbar", "qbar", "ksh", "kend"]
            VAR_R = ["mvarR", "qvarR", "gvarR", "mcvarR", "qcvarR", "scvarR",
                     "kbarR", "qbarR", "kendR"]
            for nm in VAR_S + VAR_R:
                gb[nm] = [E(nc.sbuf_tensor(f"p2_{nm}{i}", [128, NS * GT], BF16))
                          for i in range(2)]
            gb["O"] = [E(nc.sbuf_tensor(f"p2_O{i}", [128, NS * GT], BF16))
                       for i in range(2)]
            cs = E(nc.sbuf_tensor("p2_cs", [128, NS * GT], F32))
            csf = E(nc.sbuf_tensor("p2_csf", [128, NS * GT], F32))
            dfb = E(nc.sbuf_tensor("p2_dfb", [128, NS * GT], F32))
            arg1 = E(nc.sbuf_tensor("p2_arg1", [128, NS * GT], F32))
            arg2 = E(nc.sbuf_tensor("p2_arg2", [128, NS * GT], F32))
            eb1 = E(nc.sbuf_tensor("p2_eb1", [128, NS * GT], BF16))
            eb2 = E(nc.sbuf_tensor("p2_eb2", [128, NS * GT], BF16))
            refs = E(nc.sbuf_tensor("p2_refs", [128, 128], F32))
            lamf = E(nc.sbuf_tensor("p2_lam", [128, 64], F32))
            S_sb = [E(nc.sbuf_tensor(f"p2_S{i}", [128, 8 * 64], F32))
                    for i in range(2)]
            R_sb = [E(nc.sbuf_tensor(f"p2_R{i}", [128, 8 * 64], F32))
                    for i in range(2)]
            S_bf = [E(nc.sbuf_tensor(f"p2_Sb{i}", [128, 8 * 64], BF16))
                    for i in range(2)]
            R_bf = [E(nc.sbuf_tensor(f"p2_Rb{i}", [128, 8 * 64], BF16))
                    for i in range(2)]
            nall = E(nc.sbuf_tensor("p2_nall", [128, 320], BF16))
            nrm = E(nc.sbuf_tensor("p2_nrm", [128, 128], BF16))
            npw = E(nc.sbuf_tensor("p2_npw", [128, 5 * 128], BF16))
            npwR = E(nc.sbuf_tensor("p2_npwR", [128, 5 * 128], BF16))
            xb1 = E(nc.sbuf_tensor("p2_xb1", [128, 7 * 64], BF16))
            xb2 = E(nc.sbuf_tensor("p2_xb2", [128, 7 * 64], BF16))
            vbb = E(nc.sbuf_tensor("p2_vb", [128, 64], BF16))
            rgb = E(nc.sbuf_tensor("p2_rg", [128, 64], BF16))
            scrf = E(nc.sbuf_tensor("p2_scf", [128, 64], F32))
            scrb = E(nc.sbuf_tensor("p2_scb", [128, 64], BF16))
            ktb = E(nc.sbuf_tensor("p2_ktb", [128, 128], BF16))
            bgc = self.bgrow_c
            ps_g = [E(nc.psum_tensor(f"p2_psg{i}", [128, 512], F32))
                    for i in range(2)]
            ps_s = [E(nc.psum_tensor(f"p2_pss{i}", [128, 512], F32))
                    for i in range(2)]
            ps_m = E(nc.psum_tensor("p2_psm", [128, 512], F32))
            ps_o = [E(nc.psum_tensor(f"p2_pso{i}", [128, 512], F32))
                    for i in range(2)]
            ps_t = E(nc.psum_tensor("p2_pst", [128, 1024], BF16))
            sch = self._sched(ctx, "p2")
            self.p2_sched = sch
            import os as _os
            _lim = int(_os.environ.get("P2LIM", "-1"))
            dumb = refs
            _nstep = [0]

            def st(eng, fn, deps=(), dma=0):
                i = _nstep[0]
                _nstep[0] += 1
                if _lim >= 0 and i >= _lim:
                    if dma:
                        def fn2(e, sem, n=dma):
                            for _ in range(n):
                                e.dma_start(out=dumb[0:1, 0:2],
                                            in_=_ap(d["x"], 0, [[2, 1], [1, 2]])
                                            ).then_inc(sem, 16)
                        return sch.step(eng, fn2, deps, dma=dma)

                    def fn3(e, sem):
                        return e.memset(dumb[0:1, 0:1], 0.0)
                    return sch.step("vector", fn3, deps)
                return sch.step(eng, fn, deps, dma=dma)

            def fn(e, sem):
                for b in (S_sb[0], S_sb[1], R_sb[0], R_sb[1]):
                    e.memset(b[:], 0.0)
                for b in (S_bf[0], S_bf[1], R_bf[0], R_bf[1]):
                    e.memset(b[:], 0.0)
                e.memset(lamf[:], 0.0)
                e.memset(gb["O"][0][:], 0.0)
                e.memset(gb["O"][1][:], 0.0)
                e.memset(scrf[:], 0.0)
                e.memset(scrb[:], 0.0)
                ins = e.memset(refs[:], 0.0)
                return ins
            init_tok = st("vector", fn)

            prep_tok = {}
            load_tok = {}
            ospill_tok = {}
            slot_state = {h: {"state": init_tok} for h in range(H)}
            last_slot_of_group = {}
            gram_free = [None, None]
            sol_free = [None, None]
            o_free = [None, None]
            misc_free = None
            pst_free = None
            slotbuf_free = [None, None]
            last = init_tok

            for g in range(NGD):
                gi = g % 2
                ldeps = [last_slot_of_group[g - 2]] if g >= 2 else []

                def fn(e, sem, g=g, gi=gi):
                    for nm, sc in [("kn", "knS"), ("knb", "knbS"),
                                   ("kng", "kngS"), ("q", "qS"), ("v", "vS"),
                                   ("Dx", "DS"), ("DRx", "DRS")]:
                        e.dma_start(out=gb[nm][gi][:],
                                    in_=_ap(d[sc], g * NS * GT,
                                            [[NS * T, 128], [1, NS * GT]])
                                    ).then_inc(sem, 16)
                load_tok[g] = st("sync", fn, ldeps, dma=7)

                ptoks = []
                for dec in range(2):
                    dnm = "Dx" if dec == 0 else "DRx"
                    roff = dec * 64
                    pfx = "" if dec == 0 else "R"
                    kvb = "knb" if dec == 0 else "kng"

                    def fn(e, sem, gi=gi, dnm=dnm):
                        return e.tensor_scalar(dfb[:], gb[dnm][gi][:], FLOOR,
                                               None, ALU.min)
                    dep0 = [load_tok[g]] + ([ptoks[-1]] if ptoks else [])
                    if g >= 1 and dec == 0:
                        dep0.append(last_slot_of_group[g - 1])
                    t_min = st("vector", fn, dep0)

                    def fn(e, sem, gi=gi, dnm=dnm):
                        for s in range(NS):
                            for j in range(GRP):
                                sl = slice(s * GT + j * C, s * GT + (j + 1) * C)
                                e.tensor_tensor_scan(cs[:, sl],
                                                     gb[dnm][gi][:, sl],
                                                     gb[dnm][gi][:, sl], 0.0,
                                                     ALU.add, ALU.bypass)
                        for s in range(NS):
                            for j in range(GRP):
                                sl = slice(s * GT + j * C, s * GT + (j + 1) * C)
                                ins = e.tensor_tensor_scan(
                                    csf[:, sl], dfb[:, sl], dfb[:, sl], 0.0,
                                    ALU.add, ALU.bypass)
                        return ins
                    t_scan = st("vector", fn, [t_min])

                    def fn(e, sem, roff=roff):
                        for k2, (buf, col) in enumerate([(cs, C - 1), (cs, 31),
                                                         (csf, 15), (csf, 47)]):
                            ins = e.activation(
                                refs[:, roff + k2 * 16:roff + (k2 + 1) * 16],
                                _ap(buf, col, [[NS * GT, 128], [C, 16]]),
                                AF.Identity)
                        ins = e.activation(lamf[:, dec * 32:dec * 32 + 16],
                                           refs[:, roff:roff + 16], AF.Exp,
                                           scale=-1.0)
                        return ins
                    t_ref = st("scalar", fn, [t_scan])

                    def fn(e, sem):
                        e.tensor_copy(_ap(arg1, 1, [[NS * GT, 128],
                                                    [1, NS * GT - 1]]),
                                      _ap(cs, 0, [[NS * GT, 128],
                                                  [1, NS * GT - 1]]))
                        return e.memset(_ap(arg1, 0, [[NS * GT, 128], [C, 16],
                                                      [1, 1]]), 0.0)
                    t_arg1 = st("vector", fn, [t_ref])

                    def fn(e, sem):
                        e.activation(eb1[:], cs[:], AF.Exp, scale=-1.0)
                        return e.activation(eb2[:], arg1[:], AF.Exp,
                                            scale=-1.0)
                    t_e1 = st("scalar", fn, [t_arg1])

                    def fn(e, sem, gi=gi, pfx=pfx, dec=dec):
                        e.tensor_tensor(gb["kbar" + pfx][gi][:],
                                        gb["kn"][gi][:], eb1[:], ALU.mult)
                        q_ins = e.tensor_tensor(gb["qbar" + pfx][gi][:],
                                                gb["q"][gi][:], eb1[:],
                                                ALU.mult)
                        if dec == 0:
                            q_ins = e.tensor_tensor(gb["ksh"][gi][:],
                                                    gb["kn"][gi][:], eb2[:],
                                                    ALU.mult)
                        return q_ins
                    t_v1 = st("vector", fn, [t_e1])

                    def fn(e, sem, roff=roff):
                        for s in range(NS):
                            for j in range(GRP):
                                sl = slice(s * GT + j * C, s * GT + (j + 1) * C)
                                rc = refs[:, roff + s * 2 + j:
                                          roff + s * 2 + j + 1]
                                ins = e.tensor_single_scalar(
                                    arg2[:, sl], cs[:, sl], rc, ALU.subtract)
                        return ins
                    t_arg2 = st("vector", fn, [t_v1])

                    def fn(e, sem):
                        return e.activation(eb1[:], arg2[:], AF.Exp)
                    t_e2 = st("scalar", fn, [t_arg2])

                    def fn(e, sem, gi=gi, pfx=pfx):
                        return e.tensor_tensor(gb["kend" + pfx][gi][:],
                                               gb["kn"][gi][:], eb1[:],
                                               ALU.mult)
                    t_v2 = st("vector", fn, [t_e2])

                    def fn(e, sem, roff=roff):
                        for s in range(NS):
                            for j in range(GRP):
                                for hh in range(2):
                                    sl = slice(s * GT + j * C + hh * 32,
                                               s * GT + j * C + (hh + 1) * 32)
                                    rc = refs[:, roff + (2 + hh) * 16 + s * 2 + j:
                                              roff + (2 + hh) * 16 + s * 2 + j + 1]
                                    ins = e.tensor_single_scalar(
                                        arg1[:, sl], csf[:, sl], rc,
                                        ALU.subtract)
                        return ins
                    t_arg3 = st("vector", fn, [t_v2])

                    def fn(e, sem):
                        e.activation(eb1[:], arg1[:], AF.Exp, scale=-1.0)
                        return e.activation(eb2[:], arg1[:], AF.Exp)
                    t_e3 = st("scalar", fn, [t_arg3])

                    def fn(e, sem, gi=gi, pfx=pfx, kvb=kvb):
                        e.tensor_tensor(gb["mvar" + pfx][gi][:],
                                        gb[kvb][gi][:], eb1[:], ALU.mult)
                        e.tensor_tensor(gb["qvar" + pfx][gi][:],
                                        gb["q"][gi][:], eb1[:], ALU.mult)
                        return e.tensor_tensor(gb["gvar" + pfx][gi][:],
                                               gb["kn"][gi][:], eb2[:],
                                               ALU.mult)
                    t_v3 = st("vector", fn, [t_e3])

                    if dec == 0:
                        def fn(e, sem, roff=roff):
                            for s in range(NS):
                                for j in range(GRP):
                                    for hh in range(2):
                                        c0 = s * GT + j * C + hh * 32
                                        rc = refs[:, roff + (2 + hh) * 16 + s * 2 + j:
                                                  roff + (2 + hh) * 16 + s * 2 + j + 1]
                                        if c0 == 0:
                                            ins = e.tensor_single_scalar(
                                                arg2[:, 1:32],
                                                _ap(csf, 0, [[NS * GT, 128],
                                                             [1, 31]]),
                                                rc, ALU.subtract)
                                        else:
                                            ins = e.tensor_single_scalar(
                                                arg2[:, c0:c0 + 32],
                                                _ap(csf, c0 - 1,
                                                    [[NS * GT, 128], [1, 32]]),
                                                rc, ALU.subtract)
                            return ins
                        t_arg4 = st("vector", fn, [t_v3])

                        def fn(e, sem):
                            return e.activation(eb1[:], arg2[:], AF.Exp,
                                                scale=-1.0)
                        t_e4 = st("scalar", fn, [t_arg4])

                        def fn(e, sem, gi=gi):
                            return e.tensor_tensor(gb["pvar"][gi][:],
                                                   gb["kn"][gi][:], eb1[:],
                                                   ALU.mult)
                        t_v3b = st("vector", fn, [t_e4])
                    else:
                        t_v3b = t_v3

                    def fn(e, sem, roff=roff):
                        for s in range(NS):
                            for j in range(GRP):
                                sl = slice(s * GT + j * C, s * GT + (j + 1) * C)
                                rc = refs[:, roff + 16 + s * 2 + j:
                                          roff + 16 + s * 2 + j + 1]
                                ins = e.tensor_single_scalar(
                                    arg1[:, sl], cs[:, sl], rc, ALU.subtract)
                        return ins
                    t_arg5 = st("vector", fn, [t_v3b])

                    def fn(e, sem):
                        e.activation(arg2[:], arg1[:], AF.Abs)
                        return e.activation(eb1[:], arg2[:], AF.Exp,
                                            scale=-1.0)
                    t_e5 = st("scalar", fn, [t_arg5])

                    def fn(e, sem, gi=gi, pfx=pfx, kvb=kvb):
                        e.tensor_tensor(gb["mcvar" + pfx][gi][:],
                                        gb[kvb][gi][:], eb1[:], ALU.mult)
                        e.tensor_tensor(gb["qcvar" + pfx][gi][:],
                                        gb["q"][gi][:], eb1[:], ALU.mult)
                        return e.tensor_tensor(gb["scvar" + pfx][gi][:],
                                               gb["kn"][gi][:], eb1[:],
                                               ALU.mult)
                    t_v5 = st("vector", fn, [t_e5])

                    if dec == 0:
                        def fn(e, sem):
                            return e.activation(
                                _ap(eb2, 1, [[NS * GT, 128], [1, NS * GT - 1]]),
                                _ap(arg1, 0, [[NS * GT, 128], [1, NS * GT - 1]]),
                                AF.Exp, scale=-1.0)
                        t_e6 = st("scalar", fn, [t_v5])

                        def fn(e, sem, gi=gi):
                            return e.tensor_tensor(gb["pcvar"][gi][:],
                                                   gb["kn"][gi][:], eb2[:],
                                                   ALU.mult)
                        ptoks.append(st("vector", fn, [t_e6]))
                    else:
                        ptoks.append(t_v5)
                prep_tok[g] = ptoks[-1]

                for j in range(GRP):
                    c = g * GRP + j
                    if c >= NCHD:
                        break
                    def _slot(c, h, g, gi, j):
                        nonlocal misc_free, pst_free
                        k = c * H + h
                        ki = k % 2
                        rows = (h % 2) * 64
                        cb = (h // 2) * GT + j * C
                        hb = (h // 2) * 64
                        sidx = (h // 2) * 2 + j
                        ss = slot_state[h]
                        first_slot_of_group = (j == 0 and h == 0)

                        deps = [prep_tok[g]]
                        if gram_free[ki] is not None:
                            deps.append(gram_free[ki])
                        if first_slot_of_group and g >= 2 and \
                                (g - 2) in ospill_tok:
                            deps.append(ospill_tok[g - 2])

                        def fn(e, sem, gi=gi, rows=rows, cb=cb, ki=ki):
                            ps = ps_g[ki]

                            def gs(nm, c0, n):
                                return gb[nm][gi][rows:rows + 64,
                                                  cb + c0:cb + c0 + n]
                            first = True
                            ins = None
                            for vb, tnm in enumerate(["mvar", "qvar", "pvar"]):
                                ins = e.matmul(ps[rows:rows + 32, vb * 64:vb * 64 + 32],
                                               gs("gvar", 0, 32),
                                               gs(tnm, 0, 32),
                                               start=True, stop=True,
                                               tile_position=(rows, rows))
                                first = False
                                ins = e.matmul(
                                    ps[rows + 32:rows + 64, vb * 64:vb * 64 + 64],
                                    gs("gvar", 32, 32), gs(tnm, 0, 64),
                                    start=True, stop=True,
                                    tile_position=(rows, rows + 32))
                                tc = {"mvar": "mcvar", "qvar": "qcvar",
                                      "pvar": "pcvar"}[tnm]
                                ins = e.matmul(
                                    ps[rows:rows + 32, vb * 64 + 32:vb * 64 + 64],
                                    gs("scvar", 0, 32), gs(tc, 32, 32),
                                    start=True, stop=True,
                                    tile_position=(rows, rows))
                            for vb, tnm in enumerate(["mvarR", "qvarR"]):
                                vb2 = vb + 3
                                e.matmul(ps[rows:rows + 32, vb2 * 64:vb2 * 64 + 32],
                                         gs("gvarR", 0, 32), gs(tnm, 0, 32),
                                         start=True, stop=True,
                                         tile_position=(rows, rows))
                                e.matmul(
                                    ps[rows + 32:rows + 64, vb2 * 64:vb2 * 64 + 64],
                                    gs("gvarR", 32, 32), gs(tnm, 0, 64),
                                    start=True, stop=True,
                                    tile_position=(rows, rows + 32))
                                tc = {"mvarR": "mcvarR", "qvarR": "qcvarR"}[tnm]
                                ins = e.matmul(
                                    ps[rows:rows + 32, vb2 * 64 + 32:vb2 * 64 + 64],
                                    gs("scvarR", 0, 32), gs(tc, 32, 32),
                                    start=True, stop=True,
                                    tile_position=(rows, rows))
                            return ins
                        t_gram = st("tensor", fn, deps)
                        yield

                        deps = [t_gram]
                        if slotbuf_free[ki] is not None:
                            deps.append(slotbuf_free[ki])

                        def fn(e, sem, rows=rows, ki=ki):
                            return e.tensor_tensor(nall[rows:rows + 64, :],
                                                   ps_g[ki][rows:rows + 64, 0:320],
                                                   self.mpq[rows:rows + 64, :], ALU.mult)
                        t_nall = st("vector", fn, deps)
                        yield
                        gram_free[ki] = t_nall

                        deps = [t_nall]
                        if pst_free is not None:
                            deps.append(pst_free)

                        def fn(e, sem, rows=rows, ki=ki):
                            e.transpose(ps_t[rows:rows + 64, 0:64], nall[rows:rows + 64, 0:64],
                                        self.ident[rows:rows + 64, rows:rows + 64])
                            return e.transpose(ps_t[rows:rows + 64, 64:128],
                                               nall[rows:rows + 64, 192:256],
                                               self.ident[rows:rows + 64, rows:rows + 64])
                        t_seed = st("tensor", fn, deps)
                        yield

                        def fn(e, sem, rows=rows, ki=ki):
                            return e.tensor_copy(nrm[rows:rows + 64, :], ps_t[rows:rows + 64, 0:128])
                        t_nrm = st("vector", fn, [t_seed])
                        yield
                        pst_free = t_nrm

                        sq_toks = []
                        for path in range(2):
                            npb = npw if path == 0 else npwR
                            nts0 = 0 if path == 0 else 192
                            nrm0 = 0 if path == 0 else 64
                            prevt = t_nrm
                            for lvl in range(5):
                                si = (k + lvl + path) % 2
                                deps = [prevt]
                                if sol_free[si] is not None:
                                    deps.append(sol_free[si])

                                def fn(e, sem, rows=rows, ki=ki, lvl=lvl, si=si, npb=npb,
                                       nts0=nts0, nrm0=nrm0):
                                    if lvl == 0:
                                        ts_ap = nall[rows:rows + 64, nts0:nts0 + 64]
                                        rm_ap = nrm[rows:rows + 64, nrm0:nrm0 + 64]
                                    else:
                                        ts_ap = npb[rows:rows + 64, (lvl - 1) * 128:
                                                        (lvl - 1) * 128 + 64]
                                        rm_ap = npb[rows:rows + 64,
                                                    (lvl - 1) * 128 + 64:
                                                    lvl * 128]
                                    e.matmul(ps_s[si][rows:rows + 64, 0:64],
                                             rm_ap, ts_ap, start=True,
                                             stop=False,
                                             tile_position=(rows, rows))
                                    return e.matmul(
                                        ps_s[si][rows:rows + 64, 64:128],
                                        ts_ap, rm_ap, start=False, stop=True,
                                        tile_position=(rows, rows))
                                t_sq = st("tensor", fn, deps)
                                yield

                                def fn(e, sem, rows=rows, ki=ki, lvl=lvl, si=si, npb=npb):
                                    return e.tensor_copy(
                                        npb[rows:rows + 64,
                                            lvl * 128:(lvl + 1) * 128],
                                        ps_s[si][rows:rows + 64, 0:128])
                                prevt = st("vector", fn, [t_sq])
                                yield
                                sol_free[si] = prevt
                            sq_toks.append(prevt)

                        deps = [prep_tok[g], pst_free]

                        def fn(e, sem, gi=gi, rows=rows, cb=cb):
                            return e.transpose(
                                ps_t[rows:rows + 64, 128:192],
                                gb["v"][gi][rows:rows + 64, cb:cb + 64],
                                self.ident[rows:rows + 64, rows:rows + 64])
                        t_vt = st("tensor", fn, deps)
                        yield

                        def fn(e, sem, rows=rows, ki=ki, c=c, h=h):
                            bcol = bgc[rows:rows + 64, c * 64 + h:c * 64 + h + 1]
                            return e.activation(vbb[rows:rows + 64, :], ps_t[rows:rows + 64, 128:192],
                                                AF.Identity, scale=bcol)
                        t_vb = st("scalar", fn, [t_vt])
                        yield
                        pst_free = t_vb

                        deps = [ss["state"], t_nall]
                        if misc_free is not None:
                            deps.append(misc_free)

                        def fn(e, sem, gi=gi, rows=rows, cb=cb, c=c, hb=hb):
                            return e.matmul(
                                ps_m[rows:rows + 64, 0:64],
                                gb["kbar"][gi][rows:rows + 64, cb:cb + 64],
                                S_bf[c % 2][rows:rows + 64, hb:hb + 64],
                                start=True, stop=True,
                                tile_position=(rows, rows))
                        t_u = st("tensor", fn, deps)
                        yield

                        def fn(e, sem, rows=rows, ki=ki, c=c, h=h):
                            nbcol = bgc[rows:rows + 64, c * 64 + 32 + h:c * 64 + 33 + h]
                            return e.scalar_tensor_tensor(
                                xb1[rows:rows + 64, 0:64], ps_m[rows:rows + 64, 0:64], nbcol,
                                vbb[rows:rows + 64, :], ALU.mult, ALU.add)
                        t_b = st("vector", fn, [t_vb, t_u])
                        yield
                        misc_free = t_b

                        prevt = t_b
                        for i in range(6):
                            si = (k + i) % 2
                            deps = [prevt, sq_toks[0]]
                            if sol_free[si] is not None:
                                deps.append(sol_free[si])

                            def fn(e, sem, rows=rows, ki=ki, i=i, si=si):
                                if i == 0:
                                    ts_ap = nall[rows:rows + 64, 0:64]
                                else:
                                    ts_ap = npw[rows:rows + 64, (i - 1) * 128:
                                                    (i - 1) * 128 + 64]
                                return e.matmul(
                                    ps_s[si][rows:rows + 64, 128:192], ts_ap,
                                    xb1[rows:rows + 64, i * 64:(i + 1) * 64],
                                    start=True, stop=True,
                                    tile_position=(rows, rows))
                            t_ap2 = st("tensor", fn, deps)
                            yield

                            def fn(e, sem, rows=rows, ki=ki, i=i, si=si):
                                return e.tensor_tensor(
                                    xb1[rows:rows + 64, (i + 1) * 64:(i + 2) * 64],
                                    ps_s[si][rows:rows + 64, 128:192],
                                    xb1[rows:rows + 64, i * 64:(i + 1) * 64],
                                    ALU.add)
                            prevt = st("vector", fn, [t_ap2])
                            yield
                            sol_free[si] = prevt
                        t_w = prevt

                        deps = [t_w, ss["state"], misc_free]

                        def fn(e, sem, gi=gi, rows=rows, cb=cb, ki=ki, c=c,
                               hb=hb):
                            e.matmul(ps_m[rows:rows + 64, 64:128],
                                     S_bf[c % 2][rows:rows + 64, hb:hb + 64],
                                     gb["ksh"][gi][rows:rows + 64, cb:cb + 64],
                                     start=True, stop=False,
                                     tile_position=(rows, rows))
                            return e.matmul(ps_m[rows:rows + 64, 64:128],
                                            xb1[rows:rows + 64, 384:448],
                                            nall[rows:rows + 64, 128:192],
                                            start=False, stop=True,
                                            tile_position=(rows, rows))
                        t_pred = st("tensor", fn, deps)
                        yield

                        def fn(e, sem, gi=gi, rows=rows, cb=cb, ki=ki):
                            e.scalar_tensor_tensor(
                                scrf[rows:rows + 64, :],
                                ps_m[rows:rows + 64, 64:128], -1.0,
                                gb["v"][gi][rows:rows + 64, cb:cb + 64],
                                ALU.mult, ALU.add)
                            return e.tensor_scalar(scrb[rows:rows + 64, :],
                                                   scrf[rows:rows + 64, :],
                                                   1.0, -1.0, ALU.min, ALU.max)
                        t_r = st("vector", fn, [t_pred])
                        yield
                        misc_free = t_r

                        deps = [t_r, pst_free]

                        def fn(e, sem, ki=ki, rows=rows):
                            return e.transpose(
                                ps_t[rows:rows + 64, 192:256], scrb[rows:rows + 64, :],
                                self.ident[rows:rows + 64, rows:rows + 64])
                        t_rt = st("tensor", fn, deps)
                        yield

                        def fn(e, sem, rows=rows, ki=ki, c=c, h=h):
                            gcol = bgc[rows:rows + 64, c * 64 + 16 + h:c * 64 + 17 + h]
                            return e.activation(rgb[rows:rows + 64, :], ps_t[rows:rows + 64, 192:256],
                                                AF.Identity, scale=gcol)
                        t_rg = st("scalar", fn, [t_rt])
                        yield
                        pst_free = t_rg

                        deps = [ss["state"], t_r]

                        def fn(e, sem, gi=gi, rows=rows, cb=cb, c=c, hb=hb):
                            return e.matmul(
                                ps_m[rows:rows + 64, 128:192],
                                gb["kbarR"][gi][rows:rows + 64, cb:cb + 64],
                                R_bf[c % 2][rows:rows + 64, hb:hb + 64],
                                start=True, stop=True,
                                tile_position=(rows, rows))
                        t_ur = st("tensor", fn, deps)
                        yield

                        def fn(e, sem, rows=rows, ki=ki, c=c, h=h):
                            ngcol = bgc[rows:rows + 64, c * 64 + 48 + h:c * 64 + 49 + h]
                            return e.scalar_tensor_tensor(
                                xb2[rows:rows + 64, 0:64], ps_m[rows:rows + 64, 128:192],
                                ngcol, rgb[rows:rows + 64, :], ALU.mult, ALU.add)
                        t_br = st("vector", fn, [t_rg, t_ur])
                        yield

                        prevt = t_br
                        for i in range(6):
                            si = (k + i + 1) % 2
                            deps = [prevt, sq_toks[1]]
                            if sol_free[si] is not None:
                                deps.append(sol_free[si])

                            def fn(e, sem, rows=rows, ki=ki, i=i, si=si):
                                if i == 0:
                                    ts_ap = nall[rows:rows + 64, 192:256]
                                else:
                                    ts_ap = npwR[rows:rows + 64, (i - 1) * 128:
                                                     (i - 1) * 128 + 64]
                                return e.matmul(
                                    ps_s[si][rows:rows + 64, 192:256], ts_ap,
                                    xb2[rows:rows + 64, i * 64:(i + 1) * 64],
                                    start=True, stop=True,
                                    tile_position=(rows, rows))
                            t_ap2 = st("tensor", fn, deps)
                            yield

                            def fn(e, sem, rows=rows, ki=ki, i=i, si=si):
                                return e.tensor_tensor(
                                    xb2[rows:rows + 64, (i + 1) * 64:(i + 2) * 64],
                                    ps_s[si][rows:rows + 64, 192:256],
                                    xb2[rows:rows + 64, i * 64:(i + 1) * 64],
                                    ALU.add)
                            prevt = st("vector", fn, [t_ap2])
                            yield
                            sol_free[si] = prevt
                        t_wr = prevt

                        deps = [t_w, t_wr, ss["state"]]
                        if o_free[ki] is not None:
                            deps.append(o_free[ki])

                        def fn(e, sem, gi=gi, rows=rows, cb=cb, ki=ki, c=c,
                               hb=hb):
                            ps = ps_o[ki]
                            e.matmul(ps[rows:rows + 64, 0:64],
                                     S_bf[c % 2][rows:rows + 64, hb:hb + 64],
                                     gb["qbar"][gi][rows:rows + 64,
                                                    cb:cb + 64],
                                     start=True, stop=False,
                                     tile_position=(rows, rows))
                            e.matmul(ps[rows:rows + 64, 0:64],
                                     xb1[rows:rows + 64, 384:448],
                                     nall[rows:rows + 64, 64:128],
                                     start=False, stop=False,
                                     tile_position=(rows, rows))
                            e.matmul(ps[rows:rows + 64, 0:64],
                                     R_bf[c % 2][rows:rows + 64, hb:hb + 64],
                                     gb["qbarR"][gi][rows:rows + 64,
                                                     cb:cb + 64],
                                     start=False, stop=False,
                                     tile_position=(rows, rows))
                            return e.matmul(ps[rows:rows + 64, 0:64],
                                            xb2[rows:rows + 64, 384:448],
                                            nall[rows:rows + 64, 256:320],
                                            start=False, stop=True,
                                            tile_position=(rows, rows))
                        t_omm = st("tensor", fn, deps)
                        yield

                        odeps = [t_omm]
                        if first_slot_of_group and g >= 2 and \
                                (g - 2) in ospill_tok:
                            odeps.append(ospill_tok[g - 2])

                        def fn(e, sem, gi=gi, rows=rows, cb=cb, ki=ki):
                            return e.tensor_copy(
                                gb["O"][gi][rows:rows + 64, cb:cb + 64],
                                ps_o[ki][rows:rows + 64, 0:64])
                        t_oev = st("vector", fn, odeps)
                        yield
                        o_free[ki] = t_oev

                        deps = [prep_tok[g], pst_free]

                        def fn(e, sem, gi=gi, rows=rows, cb=cb):
                            e.transpose(ps_t[rows:rows + 64, 256:320],
                                        gb["kend"][gi][rows:rows + 64,
                                                       cb:cb + 64],
                                        self.ident[rows:rows + 64,
                                                   rows:rows + 64])
                            return e.transpose(
                                ps_t[rows:rows + 64, 320:384],
                                gb["kendR"][gi][rows:rows + 64, cb:cb + 64],
                                self.ident[rows:rows + 64, rows:rows + 64])
                        t_kt = st("tensor", fn, deps)
                        yield

                        def fn(e, sem, rows=rows, ki=ki):
                            return e.tensor_copy(ktb[rows:rows + 64, :], ps_t[rows:rows + 64, 256:384])
                        t_ktev = st("vector", fn, [t_kt])
                        yield
                        pst_free = t_ktev

                        deps = [t_ktev, t_w, t_wr, t_omm, t_br]

                        def fn(e, sem, ki=ki, rows=rows):
                            e.matmul(ps_m[rows:rows + 64, 192:256],
                                     ktb[rows:rows + 64, 0:64],
                                     xb1[rows:rows + 64, 384:448],
                                     start=True, stop=True,
                                     tile_position=(rows, rows))
                            return e.matmul(ps_m[rows:rows + 64, 256:320],
                                            ktb[rows:rows + 64, 64:128],
                                            xb2[rows:rows + 64, 384:448],
                                            start=True, stop=True,
                                            tile_position=(rows, rows))
                        t_smm = st("tensor", fn, deps)
                        yield

                        def fn(e, sem, rows=rows, c=c, hb=hb, sidx=sidx):
                            lc_a = lamf[rows:rows + 64, sidx:sidx + 1]
                            lc_r = lamf[rows:rows + 64,
                                        32 + sidx:32 + sidx + 1]
                            e.scalar_tensor_tensor(
                                S_sb[(c + 1) % 2][rows:rows + 64, hb:hb + 64],
                                S_sb[c % 2][rows:rows + 64, hb:hb + 64], lc_a,
                                ps_m[rows:rows + 64, 192:256], ALU.mult,
                                ALU.add)
                            e.tensor_copy(
                                S_bf[(c + 1) % 2][rows:rows + 64, hb:hb + 64],
                                S_sb[(c + 1) % 2][rows:rows + 64, hb:hb + 64])
                            e.scalar_tensor_tensor(
                                R_sb[(c + 1) % 2][rows:rows + 64, hb:hb + 64],
                                R_sb[c % 2][rows:rows + 64, hb:hb + 64], lc_r,
                                ps_m[rows:rows + 64, 256:320], ALU.mult,
                                ALU.add)
                            return e.tensor_copy(
                                R_bf[(c + 1) % 2][rows:rows + 64, hb:hb + 64],
                                R_sb[(c + 1) % 2][rows:rows + 64, hb:hb + 64])
                        t_sev = st("vector", fn, [t_smm])
                        yield
                        if getattr(self, "dbg_slot", None) == (c, h):
                            def fn(e, sem, rows=rows, ki=ki):
                                e.dma_start(out=_ap(d["x1S"], 0,
                                                    [[NS * T, 64], [1, 320]]),
                                            in_=nall[rows:rows + 64, :]).then_inc(sem, 16)
                                e.dma_start(out=_ap(d["x1S"], 512,
                                                    [[NS * T, 64], [1, 448]]),
                                            in_=xb1[rows:rows + 64, :]).then_inc(sem, 16)
                                e.dma_start(out=_ap(d["x1S"], 1024,
                                                    [[NS * T, 64], [1, 448]]),
                                            in_=xb2[rows:rows + 64, :]).then_inc(sem, 16)
                                e.dma_start(out=_ap(d["x1S"], 1536,
                                                    [[NS * T, 128], [1, 64]]),
                                            in_=scrb[:]).then_inc(sem, 16)
                            st("gpsimd", fn, [t_sev], dma=4)
                            yield
                        misc_free = t_sev
                        ss["state"] = t_sev
                        slotbuf_free[ki] = t_sev
                        last = t_sev

                    INTERLEAVE = 2
                    from collections import deque as _dq
                    gens = _dq()
                    for h in range(H):
                        gens.append(_slot(c, h, g, gi, j))
                        while len(gens) >= INTERLEAVE:
                            n0 = len(gens)
                            for _ in range(n0):
                                gg = gens.popleft()
                                try:
                                    next(gg)
                                    gens.append(gg)
                                except StopIteration:
                                    pass
                            if len(gens) < n0:
                                break
                    while gens:
                        gg = gens.popleft()
                        try:
                            next(gg)
                            gens.append(gg)
                        except StopIteration:
                            pass
                    last_slot_of_group[g] = [slot_state[H - 2]["state"],
                                             slot_state[H - 1]["state"]]

                def fn(e, sem, g=g, gi=gi):
                    e.dma_start(out=_ap(d["OS"], g * NS * GT,
                                        [[NS * T, 128], [1, NS * GT]]),
                                in_=gb["O"][gi][:]).then_inc(sem, 16)
                ospill_tok[g] = st("gpsimd", fn, [last_slot_of_group[g]],
                                   dma=1)

            with nc.Block() as blk:
                sch.emit(blk)

    def phase3(self):
        nc, d = self.nc, self.d
        P3T = 512
        n3 = getattr(self, "n3", T // P3T)
        NCHW = 16
        with ExitStack() as ctx:
            E = ctx.enter_context
            wo_sb = E(nc.sbuf_tensor("p3_wo", [128, NS * DM], BF16))
            otile = E(nc.sbuf_tensor("p3_ot", [128, 2 * 4 * DM], BF16))
            xrow = E(nc.sbuf_tensor("p3_xr", [128, 4 * DM], F32))
            atn = E(nc.sbuf_tensor("p3_atn", [128, NS * P3T], BF16))
            atT = E(nc.sbuf_tensor("p3_atT", [128, DM], F32))
            x1r = E(nc.sbuf_tensor("p3_x1r", [128, 4 * DM], F32))
            scr = E(nc.sbuf_tensor("p3_scr", [128, DM], F32))
            scr2 = E(nc.sbuf_tensor("p3_scr2", [128, DM], F32))
            xqr = E(nc.sbuf_tensor("p3_xqr", [128, DM], BF16))
            bitV = E(nc.sbuf_tensor("p3_btv", [128, 512], BF16))
            bitG = E(nc.sbuf_tensor("p3_btg", [128, 512], BF16))
            rG = E(nc.sbuf_tensor("p3_rg", [128, 512], F32))
            cols = E(nc.sbuf_tensor("p3_cols", [128, 16], F32))
            m2c = E(nc.sbuf_tensor("p3_m2c", [128, 1], BF16))
            wch = [E(nc.sbuf_tensor(f"p3_wch{i}", [128, 4096], F32))
                   for i in range(2)]
            pab = E(nc.sbuf_tensor("p3_pab", [128, 4], BF16))
            pabr = E(nc.sbuf_tensor("p3_pabr", [1, 8], BF16))
            ones1 = E(nc.sbuf_tensor("p3_ones1", [1, 128], BF16))
            onesc = E(nc.sbuf_tensor("p3_onesc", [128, 1], BF16))
            ps_a = E(nc.psum_tensor("p3_pa", [128, 512], F32))
            ps_b = E(nc.psum_tensor("p3_pb", [128, 512], F32))
            ps_c = E(nc.psum_tensor("p3_pc", [128, 512], F32))
            ps_e = E(nc.psum_tensor("p3_pe", [128, 512], F32))
            ps_t = E(nc.psum_tensor("p3_pt", [128, 1024], BF16))
            sch = self._sched(ctx, "p3")
            st = sch.step

            def fn(e, sem):
                e.memset(self.hsrow[:], 0.0)
                e.memset(cols[:], 0.0)
                e.memset(self.wq_s[:], 0.0)
                e.memset(pab[:], 0.0)
                e.memset(pabr[:], 0.0)
                e.memset(ones1[:], 1.0)
                return e.memset(onesc[:], 1.0)
            t_init = st("vector", fn)

            def fn(e, sem):
                for s in range(NS):
                    e.dma_start(
                        out=wo_sb[:, s * DM:(s + 1) * DM],
                        in_=_ap(d["woT"], s * 128 * DM,
                                [[DM, 128], [1, DM]])).then_inc(sem, 16)
            t_wo = st("gpsimd", fn, dma=NS)

            pa_dma, pa_red, pa_acc = {}, {}, {}

            def passA_load(ci):
                wnm = "w1T" if ci < 8 else "w2T"
                ch = ci % 8

                def fn(e, sem, wnm=wnm, ch=ch, ci=ci):
                    e.dma_start(out=wch[ci % 2][:],
                                in_=_ap(d[wnm], ch * 128 * 4096,
                                        [[4096, 128], [1, 4096]])
                                ).then_inc(sem, 16)
                pa_dma[ci] = st("gpsimd", fn, [pa_red.get(ci - 2)], dma=1)

            def passA_red(ci):
                ac = 8 * (ci // 8)

                def fn(e, sem, ci=ci, ac=ac):
                    return e.activation(wch[ci % 2][:], wch[ci % 2][:],
                                        AF.Abs,
                                        accum_out=self.wq_s[:, ac + 1:ac + 2])
                pa_red[ci] = st("scalar", fn, [pa_dma[ci], pa_acc.get(ci - 1)])

                def fn(e, sem, ci=ci, ac=ac):
                    if ci % 8 == 0:
                        return e.tensor_copy(self.wq_s[:, ac:ac + 1],
                                             self.wq_s[:, ac + 1:ac + 2])
                    return e.tensor_tensor(self.wq_s[:, ac:ac + 1],
                                           self.wq_s[:, ac:ac + 1],
                                           self.wq_s[:, ac + 1:ac + 2],
                                           ALU.add)
                pa_acc[ci] = st("vector", fn, [pa_red[ci], t_init])

            pst_free = None
            atev_last = {}
            x1_last = {}
            sp_prev = [None] * 4
            sp_last = None
            hdma_prev = None
            lad_prev = []
            nextload = nextred = 0
            for it in range(n3):
                while nextload < min(NCHW, 2 * (it + 1)):
                    passA_load(nextload)
                    nextload += 1
                while nextred < min(NCHW, 2 * it):
                    passA_red(nextred)
                    nextred += 1

                def fn(e, sem, it=it):
                    e.dma_start(
                        out=otile[:, (it % 2) * 4 * DM:(it % 2 + 1) * 4 * DM],
                        in_=_ap(d["OS"], it * 4 * DM,
                                [[NS * T, 128], [1, 4 * DM]])
                        ).then_inc(sem, 16)
                t_old = st("sync", fn, [atev_last.get(it - 2)], dma=1)

                def fn(e, sem, it=it):
                    for stt in range(4):
                        e.dma_start(
                            out=xrow[:, stt * DM:(stt + 1) * DM],
                            in_=_ap(d["x"], (it * P3T + stt * 128) * DM,
                                    [[DM, 128], [1, DM]]),
                        ).then_inc(sem, 16)
                t_xld = st("sync", fn, [x1_last.get(it - 1)], dma=4)

                at_ev = []
                for so in range(NS):
                    bank = [ps_a, ps_b][so % 2]
                    deps = [t_old, t_wo]
                    if len(at_ev) >= 2:
                        deps.append(at_ev[-2])

                    def fn(e, sem, it=it, so=so, bank=bank):
                        ob = (it % 2) * 4 * DM
                        for sk in range(NS):
                            ins = e.matmul(
                                bank[:],
                                wo_sb[:, sk * DM + so * 128:
                                      sk * DM + (so + 1) * 128],
                                _ap(otile, ob + sk * 128,
                                    [[2 * 4 * DM, 128], [DM, 4], [1, 128]]),
                                start=(sk == 0), stop=(sk == NS - 1))
                        return ins
                    t_mm = st("tensor", fn, deps)

                    def fn(e, sem, so=so, bank=bank):
                        return e.activation(atn[:, so * P3T:(so + 1) * P3T],
                                            bank[:], AF.Identity)
                    at_ev.append(st("scalar", fn, [t_mm]))
                atev_last[it] = at_ev[-1]

                for stt in range(4):
                    g = it * 4 + stt

                    def cs(j, it=it, stt=stt):
                        return self.lnstats[:, it * 32 + stt * 8 + j:
                                            it * 32 + stt * 8 + j + 1]

                    def fn(e, sem, stt=stt):
                        for s in range(NS):
                            ins = e.transpose(
                                ps_t[:, s * 128:(s + 1) * 128],
                                atn[:, s * P3T + stt * 128:
                                    s * P3T + stt * 128 + 128],
                                self.ident[:])
                        return ins
                    t_tr = st("tensor", fn, [at_ev[-1], pst_free])

                    def fn(e, sem):
                        for s in range(NS):
                            ins = e.activation(atT[:, s * 128:(s + 1) * 128],
                                               ps_t[:, s * 128:(s + 1) * 128],
                                               AF.Identity)
                        return ins
                    t_atT = st("scalar", fn, [t_tr])
                    pst_free = t_atT

                    def fn(e, sem, cs=cs):
                        e.tensor_scalar(cols[:, 0:1], cs(3), 1.0, None,
                                        ALU.add)
                        return e.tensor_tensor(cols[:, 1:2], cs(1), cs(3),
                                               ALU.mult)
                    t_ab = st("vector", fn, [t_xld, t_init])

                    def fn(e, sem):
                        return e.activation(cols[:, 2:4], cols[:, 0:2],
                                            AF.Identity)
                    t_abh = st("scalar", fn, [t_ab])

                    def fn(e, sem, stt=stt):
                        xa = xrow[:, stt * DM:(stt + 1) * DM]
                        x1a = x1r[:, stt * DM:(stt + 1) * DM]
                        e.scalar_tensor_tensor(x1a, xa, cols[:, 2:3], atT[:],
                                               ALU.mult, ALU.add)
                        return e.tensor_single_scalar(x1a, x1a, cols[:, 3:4],
                                                      ALU.subtract)
                    t_x1 = st("vector", fn, [t_abh, t_atT, sp_prev[stt]])
                    if stt == 3:
                        x1_last[it] = t_x1

                    def fn(e, sem, stt=stt):
                        return e.reduce_sum(cols[:, 4:5],
                                            x1r[:, stt * DM:(stt + 1) * DM],
                                            axis=AX.X)
                    t_s1 = st("vector", fn, [t_x1])

                    def fn(e, sem):
                        return e.activation(cols[:, 5:6], cols[:, 4:5],
                                            AF.Identity, scale=1.0 / DM)
                    t_s2 = st("scalar", fn, [t_s1])

                    def fn(e, sem, stt=stt):
                        x1a = x1r[:, stt * DM:(stt + 1) * DM]
                        e.tensor_single_scalar(scr[:], x1a, cols[:, 5:6],
                                               ALU.subtract)
                        e.scalar_tensor_tensor(scr2[:], scr[:], 1.0, scr[:],
                                               ALU.mult, ALU.mult,
                                               accum_out=cols[:, 6:7])
                        return e.tensor_reduce(cols[:, 7:8], scr[:], AX.X,
                                               ALU.max,
                                               apply_absolute_value=True)
                    t_s3 = st("vector", fn, [t_s2] + lad_prev)

                    def fn(e, sem):
                        e.activation(cols[:, 8:9], cols[:, 6:7], AF.Ln,
                                     bias=EPS, scale=1.0 / DM)
                        e.activation(cols[:, 13:14], cols[:, 7:8],
                                     AF.Identity)
                        return e.activation(cols[:, 8:9], cols[:, 8:9],
                                            AF.Exp, scale=-0.5)
                    t_rs2 = st("scalar", fn, [t_s3])

                    def fn(e, sem):
                        return e.tensor_tensor(cols[:, 9:10], cols[:, 13:14],
                                               cols[:, 8:9], ALU.mult)
                    t_mx = st("vector", fn, [t_rs2])

                    def fn(e, sem):
                        return e.activation(cols[:, 10:11], cols[:, 9:10],
                                            AF.Identity)
                    t_hop = st("scalar", fn, [t_mx])

                    def fn(e, sem):
                        return e.tensor_scalar(cols[:, 11:12], cols[:, 10:11],
                                               1e-5, None, ALU.max)
                    t_clp = st("vector", fn, [t_hop])

                    def fn(e, sem):
                        return e.activation(cols[:, 12:13], cols[:, 11:12],
                                            AF.Identity)
                    t_m2h = st("scalar", fn, [t_clp])

                    def fn(e, sem):
                        e.tensor_copy(m2c[:], cols[:, 12:13])
                        return e.reciprocal(cols[:, 14:15], cols[:, 12:13])
                    t_rc = st("vector", fn, [t_m2h, hdma_prev])

                    def fn(e, sem, g=g):
                        e.dma_start(out=self.hsrow[0:1, g * 128:(g + 1) * 128],
                                    in_=m2c[:]).then_inc(sem, 16)
                    hdma_prev = st("gpsimd", fn, [t_rc], dma=1)

                    def fn(e, sem):
                        return e.activation(cols[:, 15:16], cols[:, 14:15],
                                            AF.Identity)
                    t_rch = st("scalar", fn, [t_rc])

                    def fn(e, sem):
                        return e.scalar_tensor_tensor(
                            cols[:, 4:5], cols[:, 15:16], 127.0,
                            cols[:, 8:9], ALU.mult, ALU.mult)
                    t_sx = st("vector", fn, [t_rch])

                    def fn(e, sem):
                        return e.activation(cols[:, 5:6], cols[:, 4:5],
                                            AF.Identity)
                    t_sxh = st("scalar", fn, [t_sx] + lad_prev)

                    def fn(e, sem):
                        e.tensor_scalar(scr2[:, 0:512], scr[:, 0:512],
                                        cols[:, 5:6], 128.5, ALU.mult,
                                        ALU.add)
                        return e.tensor_scalar(scr2[:, 512:DM],
                                               scr[:, 512:DM],
                                               cols[:, 5:6], 128.5,
                                               ALU.mult, ALU.add)
                    t_pre = st("vector", fn, [t_sxh])

                    def fn(e, sem):
                        e.tensor_copy(ps_e[:], scr2[:, 0:512])
                        for kb in [7, 6, 5, 4, 3, 2, 1, 0]:
                            e.tensor_scalar(bitV[:], ps_e[:], float(2 ** kb),
                                            -float(2 ** kb), ALU.is_ge,
                                            ALU.mult)
                            e.tensor_tensor(ps_e[:], ps_e[:], bitV[:],
                                            ALU.add)
                        e.tensor_tensor(xqr[:, 0:512], scr2[:, 0:512],
                                        ps_e[:], ALU.subtract)
                        return e.tensor_scalar(xqr[:, 0:512], xqr[:, 0:512],
                                               -128.0, None, ALU.add)
                    ladV = st("vector", fn, [t_pre, sp_last])

                    def fn(e, sem):
                        e.tensor_copy(rG[:], scr2[:, 512:DM])
                        for kb in [7, 6, 5, 4, 3, 2, 1, 0]:
                            e.tensor_scalar(bitG[:], rG[:], float(2 ** kb),
                                            -float(2 ** kb), ALU.is_ge,
                                            ALU.mult)
                            e.tensor_tensor(rG[:], rG[:], bitG[:], ALU.add)
                        e.tensor_tensor(xqr[:, 512:DM], scr2[:, 512:DM],
                                        rG[:], ALU.subtract)
                        return e.tensor_scalar(xqr[:, 512:DM],
                                               xqr[:, 512:DM],
                                               -128.0, None, ALU.add)
                    ladG = st("vector", fn, [t_pre, sp_last])
                    lad_prev = [ladV, ladG]
                    if getattr(self, "dbg_p3", False) and it == 0 and stt == 0:
                        def fn(e, sem):
                            e.dma_start(out=_ap(d["DS"], 0,
                                                [[NS * T, 128], [1, 16]]),
                                        in_=cols[:]).then_inc(sem, 16)
                            e.dma_start(out=_ap(d["DS"], 16,
                                                [[NS * T, 128], [1, DM]]),
                                        in_=scr[:]).then_inc(sem, 16)
                            e.dma_start(out=_ap(d["DS"], 1040,
                                                [[NS * T, 128], [1, DM]]),
                                        in_=scr2[:]).then_inc(sem, 16)
                            e.dma_start(out=_ap(d["DS"], 2064,
                                                [[NS * T, 128], [1, DM]]),
                                        in_=xqr[:]).then_inc(sem, 16)
                        dtok = st("gpsimd", fn, [ladV, ladG], dma=4)
                        lad_prev = [ladV, ladG, dtok]
                        sp_last = dtok

                    def fn(e, sem):
                        for s in range(NS):
                            ins = e.transpose(ps_t[:, s * 128:(s + 1) * 128],
                                              xqr[:, s * 128:(s + 1) * 128],
                                              self.ident[:])
                        return ins
                    t_qtr = st("tensor", fn, [ladV, ladG, pst_free])

                    def fn(e, sem):
                        for s in range(NS):
                            ins = e.activation(xqr[:, s * 128:(s + 1) * 128],
                                               ps_t[:, s * 128:(s + 1) * 128],
                                               AF.Identity)
                        return ins
                    t_qev = st("scalar", fn, [t_qtr])
                    pst_free = t_qev

                    def fn(e, sem, g=g, stt=stt):
                        e.dma_start(out=_ap(d["xqS"], g * DM,
                                            [[NS * T, 128], [1, DM]]),
                                    in_=xqr[:]).then_inc(sem, 16)
                        e.dma_start(out=_ap(d["x1S"], g * DM,
                                            [[NS * T, 128], [1, DM]]),
                                    in_=x1r[:, stt * DM:(stt + 1) * DM]
                                    ).then_inc(sem, 16)
                    t_sp = st("gpsimd", fn, [t_qev], dma=2)
                    sp_prev[stt] = t_sp
                    sp_last = t_sp

            while nextload < NCHW:
                passA_load(nextload)
                nextload += 1
            while nextred < NCHW:
                passA_red(nextred)
                nextred += 1
            wq_hop = {}
            prev_tot = None
            for wi in range(2):
                ac = 8 * wi

                def fn(e, sem, ac=ac):
                    return e.activation(self.wq_s[:, ac + 1:ac + 2],
                                        self.wq_s[:, ac:ac + 1], AF.Identity)
                t_ah = st("scalar", fn, [pa_acc[8 * wi + 7], prev_tot])

                def fn(e, sem, ac=ac):
                    return e.activation(pab[:, 0:1],
                                        self.wq_s[:, ac + 1:ac + 2],
                                        AF.Identity)
                t_hi = st("scalar", fn, [t_ah])

                def fn(e, sem, ac=ac):
                    return e.tensor_tensor(pab[:, 1:2],
                                           self.wq_s[:, ac + 1:ac + 2],
                                           pab[:, 0:1], ALU.subtract)
                t_lo = st("vector", fn, [t_hi])

                def fn(e, sem):
                    return e.matmul(ps_c[0:1, 0:2], onesc[:, 0:1],
                                    pab[:, 0:2], start=True, stop=True)
                t_tot = st("tensor", fn, [t_lo, t_init])

                def fn(e, sem):
                    return e.tensor_copy(cols[0:1, 12:14], ps_c[0:1, 0:2])
                t_tc = st("vector", fn, [t_tot])

                def fn(e, sem):
                    return e.activation(cols[0:1, 14:16], cols[0:1, 12:14],
                                        AF.Identity)
                t_tch = st("scalar", fn, [t_tc])

                def fn(e, sem, ac=ac):
                    return e.tensor_tensor(self.wq_s[0:1, ac + 2:ac + 3],
                                           cols[0:1, 14:15],
                                           cols[0:1, 15:16], ALU.add)
                t_sum = st("vector", fn, [t_tch])

                def fn(e, sem, ac=ac):
                    return e.activation(cols[0:1, 13:14],
                                        self.wq_s[0:1, ac + 2:ac + 3],
                                        AF.Identity)
                t_sh = st("scalar", fn, [t_sum])

                def fn(e, sem, ac=ac):
                    return e.tensor_scalar(self.wq_s[0:1, ac + 3:ac + 4],
                                           cols[0:1, 13:14],
                                           1.0 / (DM * DHID), 1e-5,
                                           ALU.mult, ALU.max)
                t_mc = st("vector", fn, [t_sh])

                def fn(e, sem, ac=ac, wi=wi):
                    e.activation(pabr[0:1, 4 * wi:4 * wi + 1],
                                 self.wq_s[0:1, ac + 3:ac + 4], AF.Identity)
                    return e.activation(cols[0:1, 14:15],
                                        self.wq_s[0:1, ac + 3:ac + 4],
                                        AF.Identity)
                t_mch = st("scalar", fn, [t_mc])

                def fn(e, sem, wi=wi):
                    e.tensor_tensor(pabr[0:1, 4 * wi + 1:4 * wi + 2],
                                    cols[0:1, 14:15],
                                    pabr[0:1, 4 * wi:4 * wi + 1],
                                    ALU.subtract)
                    return e.reciprocal(cols[0:1, 11:12], cols[0:1, 14:15])
                t_rc2 = st("vector", fn, [t_mch])

                def fn(e, sem, wi=wi):
                    e.activation(cols[0:1, 10:11], cols[0:1, 11:12],
                                 AF.Identity)
                    return e.activation(pabr[0:1, 4 * wi + 2:4 * wi + 3],
                                        cols[0:1, 11:12], AF.Identity)
                t_swh = st("scalar", fn, [t_rc2])

                def fn(e, sem, wi=wi):
                    return e.tensor_tensor(pabr[0:1, 4 * wi + 3:4 * wi + 4],
                                           cols[0:1, 10:11],
                                           pabr[0:1, 4 * wi + 2:4 * wi + 3],
                                           ALU.subtract)
                t_swlo = st("vector", fn, [t_swh])

                def fn(e, sem, wi=wi):
                    return e.matmul(ps_c[0:128, 4 + 4 * wi:8 + 4 * wi],
                                    ones1[0:1, 0:128],
                                    pabr[0:1, 4 * wi:4 * wi + 4],
                                    start=True, stop=True)
                t_bc = st("tensor", fn, [t_swlo])

                def fn(e, sem, wi=wi):
                    b = 4 + 4 * wi
                    return e.tensor_copy(cols[:, 8:12], ps_c[0:128, b:b + 4])
                t_bcv = st("vector", fn, [t_bc])

                def fn(e, sem):
                    return e.activation(cols[:, 12:16], cols[:, 8:12],
                                        AF.Identity)
                t_bch = st("scalar", fn, [t_bcv])

                def fn(e, sem, ac=ac):
                    e.tensor_tensor(self.wq_s[:, ac + 4:ac + 5],
                                    cols[:, 12:13], cols[:, 13:14], ALU.add)
                    return e.tensor_tensor(self.wq_s[:, ac + 5:ac + 6],
                                           cols[:, 14:15], cols[:, 15:16],
                                           ALU.add)
                prev_tot = st("vector", fn, [t_bch])

                def fn(e, sem, ac=ac):
                    return e.activation(self.wq_s[:, ac + 6:ac + 8],
                                        self.wq_s[:, ac + 4:ac + 6],
                                        AF.Identity)
                wq_hop[wi] = st("scalar", fn, [prev_tot])

            def fn(e, sem):
                return e.tensor_scalar(self.hsrow[0:1, :],
                                       self.hsrow[0:1, :],
                                       self.wq_s[0:1, 6:7], 1.0 / 127.0,
                                       ALU.mult, ALU.mult)
            t_hs = st("vector", fn, [wq_hop[0], hdma_prev])

            def fn(e, sem):
                return e.memset(pab[:], 0.0)
            st("vector", fn, [sp_last, hdma_prev, t_hs, wq_hop[1]])

            with nc.Block() as blk:
                sch.emit(blk)

    def phase4(self):
        nc, d = self.nc, self.d
        P4T = 128
        NCB = DHID // 128
        n4 = getattr(self, "n4", T // P4T)
        with ExitStack() as ctx:
            E = ctx.enter_context
            w1q = E(nc.sbuf_tensor("p4_w1q", [128, NS * DHID], BF16))
            w2q = E(nc.sbuf_tensor("p4_w2q", [128, NCB * DM], BF16))
            xqt = E(nc.sbuf_tensor("p4_xqt", [128, DM], BF16))
            ht = [E(nc.sbuf_tensor(f"p4_h{i}", [128, NCB * P4T], BF16))
                  for i in range(2)]
            mxt = E(nc.sbuf_tensor("p4_mxt", [128, 128], BF16))
            hLR = E(nc.sbuf_tensor("p4_hlr", [128, 2 * NCB], BF16))
            cwcol = E(nc.sbuf_tensor("p4_cw", [128, NCB * 4], F32))
            ctile = E(nc.sbuf_tensor("p4_c", [128, NCB * P4T], F32))
            rmx = E(nc.sbuf_tensor("p4_rmx", [128, 128], F32))
            rmx2 = E(nc.sbuf_tensor("p4_rmx2", [128, 128], F32))
            rmxb = E(nc.sbuf_tensor("p4_rmxb", [128, 256], BF16))
            s256 = E(nc.sbuf_tensor("p4_s256", [128, 256], F32))
            pcol = E(nc.sbuf_tensor("p4_pcol", [128, 8], F32))
            pcb = E(nc.sbuf_tensor("p4_pcb", [128, 3], BF16))
            prow = E(nc.sbuf_tensor("p4_prow", [1, 384], BF16))
            sxb = E(nc.sbuf_tensor("p4_sxb", [128, 128], F32))
            fsb = E(nc.sbuf_tensor("p4_fsb", [128, 128], BF16))
            bitV = E(nc.sbuf_tensor("p4_btv", [128, 512], BF16))
            bitG = E(nc.sbuf_tensor("p4_btg", [128, 512], BF16))
            rG = E(nc.sbuf_tensor("p4_rg", [128, 512], F32))
            hq = E(nc.sbuf_tensor("p4_hq", [128, NCB * P4T], BF16))
            fch = E(nc.sbuf_tensor("p4_fch", [128, NS * P4T], BF16))
            fT = E(nc.sbuf_tensor("p4_fT", [128, DM], BF16))
            x1t = E(nc.sbuf_tensor("p4_x1t", [128, DM], F32))
            ones1 = E(nc.sbuf_tensor("p4_ones1", [1, 128], BF16))
            ps_a = E(nc.psum_tensor("p4_pa", [128, 512], F32))
            ps_b = E(nc.psum_tensor("p4_pb", [128, 512], F32))
            ps_c = E(nc.psum_tensor("p4_pc", [128, 512], F32))
            ps_d = E(nc.psum_tensor("p4_pd", [128, 512], F32))
            ps_e = E(nc.psum_tensor("p4_pe", [128, 512], F32))
            ps_f = E(nc.psum_tensor("p4_pf", [128, 512], F32))
            ps_g = E(nc.psum_tensor("p4_pg", [128, 512], F32))
            ps_t = E(nc.psum_tensor("p4_pt", [128, 1024], BF16))
            sch = self._sched(ctx, "p4")
            self.p4_sched = sch
            import os as _os
            _lim = int(_os.environ.get("P4LIM", "-1"))
            _skip = set(int(t) for t in
                        _os.environ.get("P4SKIP", "").split(",") if t)
            _nstep = [0]

            def st(eng, fn, deps=(), dma=0):
                i = _nstep[0]
                _nstep[0] += 1
                if (_lim >= 0 and i >= _lim) or i in _skip:
                    if dma:
                        def fn2(e, sem, n=dma):
                            for _ in range(n):
                                e.dma_start(out=pcol[0:1, 0:2],
                                            in_=_ap(d["x"], 0,
                                                    [[2, 1], [1, 2]])
                                            ).then_inc(sem, 16)
                        return sch.step(eng, fn2, deps, dma=dma)

                    def fn3(e, sem):
                        return e.memset(pcol[0:1, 0:1], 0.0)
                    return sch.step("vector", fn3, deps)
                return sch.step(eng, fn, deps, dma=dma)

            def fn(e, sem):
                e.memset(hLR[:], 0.0)
                e.memset(prow[:], 0.0)
                e.memset(pcb[:], 0.0)
                e.memset(hq[:], 0.0)
                e.memset(ctile[:], 0.0)
                e.memset(fch[:], 0.0)
                e.memset(fT[:], 0.0)
                e.memset(rG[:], 0.0)
                e.memset(bitG[:], 0.0)
                return e.memset(ones1[:], 1.0)
            t_init = st("vector", fn)

            def fn(e, sem):
                for j in range(3):
                    e.dma_start(out=_ap(cwcol, j, [[NCB * 4, 128], [4, NCB]]),
                                in_=_ap(d["convw"], j,
                                        [[3, 128], [128 * 3, NCB]]),
                                allow_slow_non_contiguous=True
                                ).then_inc(sem, 16)
                e.dma_start(out=_ap(cwcol, 3, [[NCB * 4, 128], [4, NCB]]),
                            in_=_ap(d["convb"], 0, [[1, 128], [128, NCB]]),
                            allow_slow_non_contiguous=True
                            ).then_inc(sem, 16)
            t_cw = st("sync", fn, dma=4)

            qz = {}
            mlt = {}
            for ci in range(32):
                wi = ci // 16

                def fn(e, sem, ci=ci):
                    if ci < 16:
                        sk, hf = ci // 2, ci % 2
                        src = _ap(d["w1T"], sk * 128 * DHID + hf * 2048,
                                  [[DHID, 128], [1, 2048]])
                    else:
                        cb0 = (ci - 16) * 2
                        src = _ap(d["w2T"], cb0 * 128 * DM,
                                  [[DM, 128], [DM * 128, 2], [1, DM]])
                    e.dma_start(out=ctile[:, (ci % 2) * 2048:
                                          (ci % 2 + 1) * 2048],
                                in_=src).then_inc(sem, 16)
                t_ld = st(["sync", "gpsimd"][ci % 2], fn,
                          [qz.get(ci - 2), t_init], dma=1)

                def fn(e, sem, ci=ci, wi=wi):
                    sl = ctile[:, (ci % 2) * 2048:(ci % 2 + 1) * 2048]
                    return e.tensor_single_scalar(
                        sl, sl, self.wq_s[:, 8 * wi + 7:8 * wi + 8],
                        ALU.mult)
                mlt[ci] = st("vector", fn, [t_ld])

                qeng = "vector"
                tmp0 = 0

                def fn(e, sem, ci=ci, tmp0=tmp0):
                    sl = ctile[:, (ci % 2) * 2048:(ci % 2 + 1) * 2048]
                    if ci < 16:
                        sk, hf = ci // 2, ci % 2
                        dst = w1q[:, sk * DHID + hf * 2048:
                                  sk * DHID + hf * 2048 + 2048]
                    else:
                        cb0 = (ci - 16) * 2
                        dst = w2q[:, cb0 * DM:cb0 * DM + 2048]
                    tmp = hq[:, tmp0:tmp0 + 2048]
                    e.tensor_single_scalar(dst, sl, 0.5, ALU.is_ge)
                    e.tensor_single_scalar(tmp, sl, -0.5, ALU.is_lt)
                    return e.tensor_tensor(dst, dst, tmp, ALU.subtract)
                qz[ci] = st(qeng, fn, [mlt[ci]])
            wdone = [qz[15], qz[14], qz[31], qz[30], mlt[31]]

            h_tok = {}
            evac_last = {}
            mm1_last = {}
            mm2_last = {}
            conv_done = {}
            add_done = {}
            out_done = {}
            hbe_prev = None
            pev_prev = None
            fev_prev = None
            pbc_prev = None
            pst_free = None
            ftr_prev = None
            LV = 8
            for it in range(n4 + 1):
                if it < n4:
                    def fn(e, sem, it=it):
                        e.dma_start(out=xqt[:],
                                    in_=_ap(d["xqS"], it * DM,
                                            [[NS * T, 128], [1, DM]])
                                    ).then_inc(sem, 16)
                    t_xq = st("sync", fn, wdone + [mm1_last.get(it - 1)],
                              dma=1)

                    def fn(e, sem, it=it):
                        return e.matmul(
                            ps_g[0:128, 0:128], ones1[0:1, 0:128],
                            self.hsrow[0:1, it * P4T:(it + 1) * P4T],
                            start=True, stop=True)
                    t_hb = st("tensor", fn, [t_init, hbe_prev, t_xq])

                    def fn(e, sem):
                        return e.activation(mxt[:], ps_g[0:128, 0:128],
                                            AF.Identity)
                    t_hbe = st("scalar", fn, [t_hb, evac_last.get(it - 1)])
                    hbe_prev = t_hbe

                    hev = []
                    for gj in range(NCB // 4):
                        bank = [ps_a, ps_b][gj % 2]
                        deps = [t_xq, t_hbe]
                        if len(hev) >= 2:
                            deps.append(hev[-2])

                        def fn(e, sem, gj=gj, bank=bank):
                            for r4 in range(4):
                                cb = gj * 4 + r4
                                for sk in range(NS):
                                    ins = e.matmul(
                                        bank[:, r4 * 128:(r4 + 1) * 128],
                                        w1q[:, sk * DHID + cb * 128:
                                            sk * DHID + (cb + 1) * 128],
                                        xqt[:, sk * 128:(sk + 1) * 128],
                                        start=(sk == 0), stop=(sk == NS - 1))
                            return ins
                        t_m1 = st("tensor", fn, deps)
                        if gj == NCB // 4 - 1:
                            mm1_last[it] = t_m1

                        def fn(e, sem, it=it, gj=gj, bank=bank):
                            for r4 in range(4):
                                cb = gj * 4 + r4
                                ins = e.tensor_tensor(
                                    ht[it % 2][:, cb * P4T:(cb + 1) * P4T],
                                    bank[:, r4 * 128:(r4 + 1) * 128],
                                    mxt[:], ALU.mult)
                            return ins
                        hev.append(st("vector", fn, [t_m1]))
                    h_tok[it] = hev[-1]
                    evac_last[it] = hev[-1]

                ot = it - 1
                if ot < 0:
                    continue
                hi, hni = ot % 2, (ot + 1) % 2
                have_next = ot + 1 < n4

                deps = [h_tok[ot], conv_done.get(ot - 1), t_init, t_cw]
                if have_next:
                    deps.append(h_tok[ot + 1])
                if ot >= 1:
                    deps.append(mm2_last[ot - 1])

                def fn(e, sem, ot=ot, hi=hi, hni=hni, have_next=have_next):
                    for cb in range(NCB):
                        sl = slice(cb * P4T, (cb + 1) * P4T)
                        w0 = cwcol[:, cb * 4:cb * 4 + 1]
                        w1c = cwcol[:, cb * 4 + 1:cb * 4 + 2]
                        w2c = cwcol[:, cb * 4 + 2:cb * 4 + 3]
                        e.tensor_single_scalar(ctile[:, sl], ht[hi][:, sl],
                                               w1c, ALU.mult)
                        e.scalar_tensor_tensor(
                            _ap(ctile, cb * P4T + 1,
                                [[NCB * P4T, 128], [1, P4T - 1]]),
                            _ap(ht[hi], cb * P4T,
                                [[NCB * P4T, 128], [1, P4T - 1]]),
                            w0,
                            _ap(ctile, cb * P4T + 1,
                                [[NCB * P4T, 128], [1, P4T - 1]]),
                            ALU.mult, ALU.add)
                        e.scalar_tensor_tensor(
                            ctile[:, cb * P4T:cb * P4T + 1],
                            hLR[:, cb:cb + 1], w0,
                            ctile[:, cb * P4T:cb * P4T + 1],
                            ALU.mult, ALU.add)
                        e.scalar_tensor_tensor(
                            _ap(ctile, cb * P4T,
                                [[NCB * P4T, 128], [1, P4T - 1]]),
                            _ap(ht[hi], cb * P4T + 1,
                                [[NCB * P4T, 128], [1, P4T - 1]]),
                            w2c,
                            _ap(ctile, cb * P4T,
                                [[NCB * P4T, 128], [1, P4T - 1]]),
                            ALU.mult, ALU.add)
                        src_r = (ht[hni][:, cb * P4T:cb * P4T + 1]
                                 if have_next
                                 else hLR[:, NCB + cb:NCB + cb + 1])
                        e.scalar_tensor_tensor(
                            ctile[:, (cb + 1) * P4T - 1:(cb + 1) * P4T],
                            src_r, w2c,
                            ctile[:, (cb + 1) * P4T - 1:(cb + 1) * P4T],
                            ALU.mult, ALU.add)
                    for cb in range(NCB):
                        ins = e.tensor_copy(hLR[:, cb:cb + 1],
                                            ht[hi][:, (cb + 1) * P4T - 1:
                                                    (cb + 1) * P4T])
                    return ins
                t_conv = st("vector", fn, deps)
                conv_done[ot] = t_conv

                sil = []
                for half in range(2):
                    def fn(e, sem, half=half):
                        for cb in range(half * 16, half * 16 + 16):
                            sl = slice(cb * P4T, (cb + 1) * P4T)
                            ins = e.activation(
                                ctile[:, sl], ctile[:, sl], AF.Silu,
                                bias=cwcol[:, cb * 4 + 3:cb * 4 + 4],
                                scale=1.0)
                        return ins
                    sil.append(st("scalar", fn, [t_conv]))

                def fn(e, sem):
                    return e.tensor_reduce(
                        rmx[:], _ap(ctile, 0, [[NCB * P4T, 128], [1, P4T],
                                               [P4T, NCB]]),
                        AX.X, ALU.max, apply_absolute_value=True)
                t_rmxp = st("vector", fn, sil)

                def fn(e, sem):
                    return e.activation(rmx2[:], rmx[:], AF.Identity)
                t_rmxh = st("scalar", fn, [t_rmxp])

                def fn(e, sem):
                    e.tensor_copy(rmxb[:, 0:128], rmx2[:])
                    e.scalar_tensor_tensor(s256[:, 0:128], rmxb[:, 0:128],
                                           -1.0, rmx2[:], ALU.mult, ALU.add)
                    return e.tensor_copy(rmxb[:, 128:256], s256[:, 0:128])
                t_rhl = st("vector", fn, [t_rmxh])

                def fn(e, sem):
                    e.transpose(ps_t[:, 768:896], rmxb[:, 0:128],
                                self.ident[:])
                    return e.transpose(ps_t[:, 896:1024], rmxb[:, 128:256],
                                       self.ident[:])
                t_rtr = st("tensor", fn, [t_rhl, pst_free])

                def fn(e, sem):
                    return e.activation(s256[:], ps_t[:, 768:1024],
                                        AF.Identity)
                t_rev = st("scalar", fn, [t_rtr])
                pst_free = t_rev

                def fn(e, sem):
                    e.tensor_tensor(s256[:, 0:128], s256[:, 0:128],
                                    s256[:, 128:256], ALU.add)
                    return e.tensor_reduce(pcol[:, 0:1], s256[:, 0:128],
                                           AX.X, ALU.max)
                t_m2r = st("vector", fn, [t_rev])

                def fn(e, sem):
                    return e.activation(pcol[:, 5:6], pcol[:, 0:1],
                                        AF.Identity)
                t_m2rh = st("scalar", fn, [t_m2r])

                def fn(e, sem):
                    return e.tensor_scalar(pcol[:, 1:2], pcol[:, 5:6], 1e-5,
                                           None, ALU.max)
                t_c1 = st("vector", fn, [t_m2rh])

                def fn(e, sem):
                    return e.activation(pcol[:, 6:7], pcol[:, 1:2],
                                        AF.Identity)
                t_c2 = st("scalar", fn, [t_c1])

                def fn(e, sem):
                    e.reciprocal(pcol[:, 2:3], pcol[:, 6:7])
                    return e.tensor_scalar(pcol[:, 4:5], pcol[:, 6:7],
                                           self.wq_s[:, 14:15], 1.0 / 127.0,
                                           ALU.mult, ALU.mult)
                t_c3 = st("vector", fn, [t_c2])

                def fn(e, sem):
                    e.activation(pcb[:, 2:3], pcol[:, 4:5], AF.Identity)
                    return e.activation(pcol[:, 7:8], pcol[:, 2:3],
                                        AF.Identity)
                t_c4 = st("scalar", fn, [t_c3])

                def fn(e, sem):
                    return e.tensor_scalar(pcol[:, 3:4], pcol[:, 7:8], 127.0,
                                           None, ALU.mult)
                t_c5 = st("vector", fn, [t_c4])

                def fn(e, sem):
                    e.activation(pcol[:, 0:1], pcol[:, 3:4], AF.Identity)
                    return e.activation(pcb[:, 0:1], pcol[:, 3:4],
                                        AF.Identity)
                t_c6 = st("scalar", fn, [t_c5])

                def fn(e, sem):
                    return e.tensor_tensor(pcb[:, 1:2], pcol[:, 0:1],
                                           pcb[:, 0:1], ALU.subtract)
                t_m2h = st("vector", fn, [t_c6])

                def fn(e, sem, ot=ot):
                    for j in range(3):
                        e.dma_start(out=prow[0:1, j * 128:(j + 1) * 128],
                                    in_=pcb[:, j:j + 1]).then_inc(sem, 16)
                t_pd = st("gpsimd", fn, [t_m2h, t_c4, pbc_prev], dma=3)

                def fn(e, sem):
                    e.matmul(ps_c[0:128, 256:512], ones1[0:1, 0:128],
                             prow[0:1, 0:256], start=True, stop=True)
                    return e.matmul(ps_c[0:128, 0:128], ones1[0:1, 0:128],
                                    prow[0:1, 256:384], start=True,
                                    stop=True)
                t_pbc = st("tensor", fn, [t_pd, pev_prev, fev_prev])
                pbc_prev = t_pbc

                def fn(e, sem):
                    e.tensor_copy(s256[:], ps_c[0:128, 256:512])
                    return e.tensor_tensor(sxb[:], s256[:, 0:128],
                                           s256[:, 128:256], ALU.add)
                t_pev = st("vector", fn, [t_pbc])
                pev_prev = t_pev

                def fn(e, sem):
                    return e.tensor_copy(fsb[:], ps_c[0:128, 0:128])
                t_fev = st("vector", fn, [t_pbc])
                fev_prev = t_fev

                def fn(e, sem):
                    for cb in range(LV):
                        sl = slice(cb * P4T, (cb + 1) * P4T)
                        e.tensor_tensor(ctile[:, sl], ctile[:, sl], sxb[:],
                                        ALU.mult)
                    return e.tensor_scalar(ctile[:, 0:LV * P4T],
                                          ctile[:, 0:LV * P4T], 128.5, None,
                                          ALU.add)
                t_preV = st("vector", fn, [t_pev] + sil)

                def fn(e, sem):
                    for cb in range(LV, NCB):
                        sl = slice(cb * P4T, (cb + 1) * P4T)
                        e.tensor_tensor(ctile[:, sl], ctile[:, sl], sxb[:],
                                        ALU.mult)
                    return e.tensor_scalar(ctile[:, LV * P4T:],
                                          ctile[:, LV * P4T:], 128.5, None,
                                          ALU.add)
                t_preG = st("vector", fn, [t_pev] + sil)

                lad_deps_extra = [mm2_last.get(ot - 1)]

                def fn(e, sem):
                    for q in range(LV * P4T // 512):
                        sl = slice(q * 512, (q + 1) * 512)
                        e.tensor_copy(ps_e[:], ctile[:, sl])
                        for kb in [7, 6, 5, 4, 3, 2, 1, 0]:
                            e.tensor_scalar(bitV[:], ps_e[:], float(2 ** kb),
                                            -float(2 ** kb), ALU.is_ge,
                                            ALU.mult)
                            e.tensor_tensor(ps_e[:], ps_e[:], bitV[:],
                                            ALU.add)
                        e.tensor_tensor(hq[:, sl], ctile[:, sl], ps_e[:],
                                        ALU.subtract)
                        ins = e.tensor_scalar(hq[:, sl], hq[:, sl], -128.0,
                                              None, ALU.add)
                    return ins
                ladV = st("vector", fn, [t_preV] + lad_deps_extra)

                def fn(e, sem):
                    for q in range(LV * P4T // 512, NCB * P4T // 512):
                        sl = slice(q * 512, (q + 1) * 512)
                        e.tensor_copy(rG[:], ctile[:, sl])
                        for kb in [7, 6, 5, 4, 3, 2, 1, 0]:
                            e.tensor_scalar(bitG[:], rG[:], float(2 ** kb),
                                            -float(2 ** kb), ALU.is_ge,
                                            ALU.mult)
                            e.tensor_tensor(rG[:], rG[:], bitG[:], ALU.add)
                        e.tensor_tensor(hq[:, sl], ctile[:, sl], rG[:],
                                        ALU.subtract)
                        ins = e.tensor_scalar(hq[:, sl], hq[:, sl], -128.0,
                                              None, ALU.add)
                    return ins
                ladG = st("vector", fn, [t_preG] + lad_deps_extra)

                m2ev = []
                for so in range(NS):
                    bank2 = [ps_d, ps_f][so % 2]
                    reg = (so // 2) % 2
                    deps = [ladV, ladG] + wdone
                    if len(m2ev) >= 2:
                        deps.append(m2ev[-2])
                    if so == 0:
                        deps.append(ftr_prev)

                    def fn(e, sem, so=so, reg=reg, bank2=bank2):
                        for cb in range(NCB):
                            ins = e.matmul(
                                bank2[:, reg * 128:(reg + 1) * 128],
                                w2q[:, cb * DM + so * 128:
                                    cb * DM + (so + 1) * 128],
                                hq[:, cb * P4T:(cb + 1) * P4T],
                                start=(cb == 0), stop=(cb == NCB - 1))
                        return ins
                    t_m2 = st("tensor", fn, deps)
                    if so == NS - 1:
                        mm2_last[ot] = t_m2

                    def fn(e, sem, so=so, reg=reg, bank2=bank2):
                        return e.tensor_tensor(
                            fch[:, so * P4T:(so + 1) * P4T],
                            bank2[:, reg * 128:(reg + 1) * 128], fsb[:],
                            ALU.mult)
                    m2ev.append(st("vector", fn, [t_m2, t_fev]))

                def fn(e, sem, ot=ot):
                    e.dma_start(out=x1t[:],
                                in_=_ap(d["x1S"], ot * DM,
                                        [[NS * T, 128], [1, DM]])
                                ).then_inc(sem, 16)
                t_x1l = st("sync", fn, [add_done.get(ot - 1)], dma=1)

                def fn(e, sem):
                    for s in range(NS):
                        ins = e.transpose(ps_t[:, s * 128:(s + 1) * 128],
                                          fch[:, s * P4T:(s + 1) * P4T],
                                          self.ident[:])
                    return ins
                t_ftr = st("tensor", fn, [m2ev[-1], pst_free])
                ftr_prev = t_ftr

                def fn(e, sem):
                    for s in range(NS):
                        ins = e.activation(fT[:, s * 128:(s + 1) * 128],
                                           ps_t[:, s * 128:(s + 1) * 128],
                                           AF.Identity)
                    return ins
                t_fte = st("scalar", fn, [t_ftr, add_done.get(ot - 1)])
                pst_free = t_fte

                def fn(e, sem):
                    return e.tensor_tensor(x1t[:], x1t[:], fT[:], ALU.add)
                t_add = st("vector", fn, [t_fte, t_x1l])
                add_done[ot] = t_add

                def fn(e, sem, ot=ot):
                    e.dma_start(out=_ap(d["out"], ot * P4T * DM,
                                        [[DM, 128], [1, DM]]),
                                in_=x1t[:]).then_inc(sem, 16)
                out_done[ot] = st("sync", fn, [t_add], dma=1)

            def fn(e, sem):
                return e.memset(pcol[:], 0.0)
            st("vector", fn, [out_done[n4 - 1]])

            with nc.Block() as blk:
                sch.emit(blk)


def _host_constants():
    ident = np.eye(128, dtype=np.float32)
    mask_nsr = np.zeros((128, 128), np.float32)
    mask_pq = np.zeros((128, 320), np.float32)
    for s in range(64):
        for t in range(64):
            st_ = 1.0 if s < t else 0.0
            inc = 1.0 if s <= t else 0.0
            mask_pq[s, 0 + t] = -st_
            mask_pq[s, 64 + t] = inc
            mask_pq[s, 128 + t] = st_
            mask_pq[s, 192 + t] = -st_
            mask_pq[s, 256 + t] = inc
    mask_pq[64:128] = mask_pq[0:64]
    hm_sq = np.zeros((128, NS * 16), np.float32)
    hm_bc = np.zeros((16, NS * 128), np.float32)
    hm_bc2 = np.zeros((48, NS * 128), np.float32)
    for s in range(NS):
        for r in range(128):
            h = (s * 128 + r) // DH
            hm_sq[r, s * 16 + h] = 1.0
            hm_bc[h, s * 128 + r] = 1.0
            hm_bc2[32 + h, s * 128 + r] = 1.0
    return ident, mask_nsr, mask_pq, hm_sq, hm_bc, hm_bc2


def _pack_wbg(wb, wg):
    out = np.zeros((1024, 48), np.float32)
    out[:, 0:16] = wb.T
    out[:, 32:48] = wg.T
    return np.ascontiguousarray(out)


def _pack_bbg(bb, bg):
    out = np.zeros(48, np.float32)
    out[0:16] = bb
    out[32:48] = bg
    return out


def make_inputs(full, core):
    f32 = np.float32
    ident, mask_nsr, mask_pq, hm_sq, hm_bc, hm_bc2 = _host_constants()
    import ml_dtypes
    bf = ml_dtypes.bfloat16
    g = lambda n: np.ascontiguousarray(np.asarray(full[n], f32))
    return {
        "x": np.ascontiguousarray(np.asarray(full["L0_attn_x"][core], f32)),
        "wqT": np.ascontiguousarray(g("L0_attn_Wq").T),
        "wkT": np.ascontiguousarray(g("L0_attn_Wk").T),
        "wvT": np.ascontiguousarray(g("L0_attn_Wv").T),
        "waT": np.ascontiguousarray(g("L0_attn_Wa").T),
        "waRT": np.ascontiguousarray(g("L0_attn_WaR").T),
        "woT": np.ascontiguousarray(g("L0_attn_Wo").T),
        "wbgT": _pack_wbg(g("L0_attn_Wb"), g("L0_attn_Wg")),
        "w1T": np.ascontiguousarray(g("L0_ffn_w1").T),
        "w2T": np.ascontiguousarray(g("L0_ffn_w2").T),
        "ba": g("L0_attn_ba"), "baR": g("L0_attn_baR"),
        "bbg": _pack_bbg(g("L0_attn_bb"), g("L0_attn_bg")),
        "lnw": g("L0_attn_ln_w"), "lnb": g("L0_attn_ln_b"),
        "convw": np.ascontiguousarray(g("L0_ffn_conv_w")[:, 0, :]),
        "convb": g("L0_ffn_conv_b"),
        "ident": ident.astype(bf), "identf": ident,
        "mask_nsr": mask_nsr.astype(bf), "mask_pq": mask_pq.astype(bf),
        "hmap_sq": hm_sq.astype(bf), "hmap_bc": hm_bc.astype(bf),
        "hmap_bc2": hm_bc2.astype(bf),
    }


def kernel(**inputs):
    mk = MK()
    nc = mk.build()
    in_maps = [make_inputs(inputs, core) for core in range(8)]
    res = run_bass_kernel_spmd(nc, in_maps, core_ids=list(range(8)))
    out = np.stack([r["out"] for r in res.results], axis=0)
    return out.astype(np.float32)
